# version 04
# baseline (speedup 1.0000x reference)
import numpy as np
import ml_dtypes
from contextlib import ExitStack
import concourse.bass as bass
import concourse.mybir as mybir
from concourse.bass_utils import run_bass_kernel_spmd

F32 = mybir.dt.float32
BF16 = mybir.dt.bfloat16
AF = mybir.ActivationFunctionType
ALU = mybir.AluOpType
AX = mybir.AxisListType.X
RG8 = [[0, 1, 2, 3, 4, 5, 6, 7]]

NCORE = 8
L = 16384
D = 2048
TPC = L // NCORE
EPS_LN = 1e-5
EPS_L0 = 1e-6


class Prog:
    def __init__(self, nc, es):
        self.nc = nc
        self.es = es
        self._scopes = []

    def _cur(self):
        return self._scopes[-1] if self._scopes else self.es

    def push_scope(self):
        st = ExitStack()
        self.es.enter_context(st)
        self._scopes.append(st)
        return st

    def pop_scope(self):
        st = self._scopes.pop()
        st.close()

    def sbuf(self, name, shape, dtype=F32, **kw):
        return self._cur().enter_context(self.nc.sbuf_tensor(name, shape, dtype, **kw))

    def psum(self, name, shape, dtype=F32):
        return self._cur().enter_context(self.nc.psum_tensor(name, shape, dtype))

    def sem(self, name):
        return self.es.enter_context(self.nc.semaphore(name))


class Ch:

    def __init__(self, sem):
        self.s = sem
        self.n = 0

    def inc(self, instr):
        instr.then_inc(self.s, 1)
        self.n += 1
        return self.n


def hadamard_pm(n):
    H = np.array([[1.0]], np.float32)
    while H.shape[0] < n:
        H = np.block([[H, H], [H, -H]])
    return H



def emit_ph0(p, block, x_rows, ln0T_sh, ident_bf, ag=None, tag="p0", done=None):
    NT = TPC // 128
    ND = D // 128

    sx = [p.sbuf(f"{tag}_sx{i}", [128, D], F32) for i in range(2)]
    sq = p.sbuf(f"{tag}_sq", [128, D], F32)
    lnb = p.sbuf(f"{tag}_lnb", [128, D], BF16)
    sid = p.sbuf(f"{tag}_sid", [128, 128], BF16)
    stat = p.sbuf(f"{tag}_stat", [128, 8], F32)
    epsc = p.sbuf(f"{tag}_eps", [128, 1], F32)
    slabs = p.sbuf(f"{tag}_slabs", [128, ND * TPC], BF16)
    ptr = [p.psum(f"{tag}_ptr{i}", [128, 128], BF16) for i in range(2)]

    ldc = p.sem(tag + "_ldc")
    ldp = [p.sem(tag + "_ld0"), p.sem(tag + "_ld1")]
    A = Ch(p.sem(tag + "_A"))
    V = Ch(p.sem(tag + "_V"))
    T = Ch(p.sem(tag + "_T"))
    E = Ch(p.sem(tag + "_E"))
    st = p.sem(tag + "_st")

    VT = 7

    @block.sync
    def _(sync):
        sync.dma_start(out=sid[:], in_=ident_bf).then_inc(ldc, 16)
        for i in range(NT):
            if i >= 2:
                sync.wait_ge(V.s, VT * (i - 1))
            sync.dma_start(out=sx[i % 2][:],
                           in_=x_rows[i * 128:(i + 1) * 128, :]).then_inc(ldp[i % 2], 16)
        if done is not None:
            sync.wait_ge(E.s, NT * ND)
            sync.wait_ge(st, 16 * ND)
            sync.sem_inc(done, 1)

    @block.scalar
    def _(scalar):
        for i in range(NT):
            scalar.wait_ge(ldp[i % 2], 16 * (i // 2 + 1))
            A.inc(scalar.activation(sq[:], sx[i % 2][:], AF.Square,
                                    accum_out=stat[:, 1:2]))
            scalar.wait_ge(V.s, VT * i + 5)
            A.inc(scalar.activation(stat[:, 6:7], stat[:, 4:5], AF.Sqrt,
                                    bias=epsc[:]))
            for k in range(ND):
                scalar.wait_ge(T.s, i * ND + k + 1)
                E.inc(scalar.copy(
                    slabs[:, k * TPC + i * 128: k * TPC + (i + 1) * 128],
                    ptr[k % 2][:]))
        if done is not None:
            scalar.wait_ge(st, 16 * ND)
            scalar.sem_inc(done, 1)

    @block.vector
    def _(vector):
        vector.memset(epsc[:], EPS_LN)
        for i in range(NT):
            b = VT * i
            vector.wait_ge(ldp[i % 2], 16 * (i // 2 + 1))
            s = sx[i % 2]
            V.inc(vector.reduce_sum(stat[:, 0:1], s[:], axis=AX))
            vector.wait_ge(V.s, b + 1)
            V.inc(vector.tensor_scalar(stat[:, 2:3], stat[:, 0:1],
                                       1.0 / D, None, ALU.mult))
            vector.wait_ge(A.s, 2 * i + 1)
            V.inc(vector.tensor_scalar(stat[:, 3:4], stat[:, 1:2],
                                       1.0 / D, None, ALU.mult))
            vector.wait_ge(V.s, b + 2)
            V.inc(vector.tensor_mul(stat[:, 5:6], stat[:, 2:3], stat[:, 2:3]))
            vector.wait_ge(V.s, b + 4)
            V.inc(vector.tensor_sub(stat[:, 4:5], stat[:, 3:4], stat[:, 5:6]))
            vector.wait_ge(A.s, 2 * i + 2)
            V.inc(vector.reciprocal(stat[:, 7:8], stat[:, 6:7]))
            vector.wait_ge(V.s, b + 6)
            if i >= 1:
                vector.wait_ge(T.s, i * ND)
            V.inc(vector.tensor_scalar(lnb[:], s[:], stat[:, 2:3], stat[:, 7:8],
                                       ALU.subtract, ALU.mult))
        if done is not None:
            vector.wait_ge(E.s, NT * ND)
            vector.wait_ge(st, 16 * ND)
            vector.sem_inc(done, 1)

    @block.tensor
    def _(tensor):
        tensor.wait_ge(ldc, 16)
        for i in range(NT):
            tensor.wait_ge(V.s, VT * i + 7)
            for k in range(ND):
                g = i * ND + k
                if g >= 2:
                    tensor.wait_ge(E.s, g - 1)
                T.inc(tensor.transpose(ptr[k % 2][:],
                                       lnb[:, k * 128:(k + 1) * 128], sid[:]))
        if done is not None:
            tensor.wait_ge(E.s, NT * ND)
            tensor.wait_ge(st, 16 * ND)
            tensor.sem_inc(done, 1)

    @block.gpsimd
    def _(gpsimd):
        gpsimd.wait_ge(E.s, NT * ND)
        for k in range(ND):
            gpsimd.dma_start(out=ln0T_sh[k * 128:(k + 1) * 128, :],
                             in_=slabs[:, k * TPC:(k + 1) * TPC]).then_inc(st, 16)
        gpsimd.wait_ge(st, 16 * ND)
        if ag is not None:
            ag_out, cc = ag
            gpsimd.collective_compute(
                "AllGather", ALU.bypass, replica_groups=RG8,
                ins=[ln0T_sh[:, :].opt()], outs=[ag_out[:, :].opt()],
            ).then_inc(cc, 1)
        if done is not None:
            gpsimd.sem_inc(done, 1)



ENGS = ("sync", "scalar", "vector", "tensor", "gpsimd")


class Sched:
    def __init__(self, p, tag):
        self.p = p
        self.tag = tag
        self.lists = {e: [] for e in ENGS}
        self.chains = {}

    def ch(self, name):
        if name not in self.chains:
            self.chains[name] = Ch(self.p.sem(f"{self.tag}_{name}"))
        return self.chains[name]

    def wait(self, eng, ch, n):
        if n <= 0:
            return
        sem = ch.s if isinstance(ch, Ch) else ch
        self.lists[eng].append(("w", sem, n))

    def op(self, eng, fn, ch=None, inc=1):
        sem = ch.s if ch is not None else None
        self.lists[eng].append(("o", fn, sem, inc))
        if ch is not None:
            ch.n += inc
            return ch.n
        return None

    def barrier(self, name, pre_dma_waits=()):
        G = self.ch(name)
        for (eng, ch, n) in pre_dma_waits:
            self.wait(eng, ch, n)
        base = G.n
        for eng in ENGS:
            self.lists[eng].append(("o", lambda e: e.sem_inc(G.s, 1), None, 0))
        G.n += 5
        for eng in ENGS:
            self.wait(eng, G, base + 5)

    def emit(self, block):
        def mk(name):
            def body(eng):
                for it in self.lists[name]:
                    if it[0] == "w":
                        eng.wait_ge(it[1], it[2])
                    else:
                        instr = it[1](eng)
                        if it[2] is not None:
                            instr.then_inc(it[2], it[3])
            return body
        block.sync(mk("sync"))
        block.scalar(mk("scalar"))
        block.vector(mk("vector"))
        block.tensor(mk("tensor"))
        block.gpsimd(mk("gpsimd"))



HK_C, HV_C, DK, DV, KC = 2, 4, 128, 128, 4
CHUNK = 128
import os as _os
NBLK = int(_os.environ.get("PH1_NBLK", L // 512))
CPB = 4
RESET_CHUNKS = {0, 16, 32, 48}


def emit_ph1(p, sched, H):
    s = sched
    nc = p.nc

    wqkvb = p.sbuf("p1_wqkvb", [128, 16 * 1024], BF16)
    wzb = p.sbuf("p1_wzb", [128, 16 * 512], BF16)
    wabb = p.sbuf("p1_wabb", [128, 16 * 8], BF16)
    wld = p.sbuf("p1_wld", [128, 1024], F32)
    convc = p.sbuf("p1_convc", [128, 32], F32)
    normw = p.sbuf("p1_normw", [128, 1], F32)
    dtb_t = p.sbuf("p1_dtbt", [128, 16], F32)
    nega_t = p.sbuf("p1_negat", [128, 16], F32)
    row16 = p.sbuf("p1_row16", [1, 16], F32)
    onecol = p.sbuf("p1_onecol", [128, 1], F32)
    onescol_b = p.sbuf("p1_onescolb", [128, 1], BF16)
    ones1b = p.sbuf("p1_ones1b", [1, 128], BF16)

    sid_b = p.sbuf("p1_sidb", [128, 128], BF16)
    sid_f = p.sbuf("p1_sidf", [128, 128], F32)
    ut_f = p.sbuf("p1_utf", [128, 128], F32)
    isl_f = p.sbuf("p1_islf", [128, 128], F32)
    ilc_f = p.sbuf("p1_ilcf", [128, 128], F32)
    sel4f = p.sbuf("p1_sel4f", [4, 512], F32)
    sel12b = p.sbuf("p1_sel12b", [12, 1536], BF16)

    ldbuf = p.sbuf("p1_ldbuf", [128, 16 * 512], BF16)
    qkvh = [p.sbuf(f"p1_qkvh{j}", [128, 516], F32) for j in range(8)]
    convacc = p.sbuf("p1_convacc", [128, 512], F32)
    qksilu = p.sbuf("p1_qksilu", [128, 4 * 512], F32)
    sqb = p.sbuf("p1_sqb", [128, 512], BF16)
    nrm = p.sbuf("p1_nrm", [1, 2048], F32)
    nrmb = p.sbuf("p1_nrmb", [1, 2048], BF16)
    qkT = p.sbuf("p1_qkT", [128, 4 * 512], BF16)
    vT = p.sbuf("p1_vT", [128, 4 * 512], BF16)
    szT = p.sbuf("p1_szT", [128, 4 * 512], BF16)
    abf = p.sbuf("p1_abf", [8, 512], F32)
    ab_tm = p.sbuf("p1_abtm", [128, 32], F32)
    gsc = p.sbuf("p1_gsc", [128, 16], F32)
    gsc2 = p.sbuf("p1_gsc2", [128, 16], F32)
    pk = p.sbuf("p1_pk", [128, 80], F32)
    pkb = p.sbuf("p1_pkb", [128, 48], BF16)
    rows4 = p.sbuf("p1_rows4", [4, 128], F32)
    rows12 = p.sbuf("p1_rows12", [12, 128], BF16)
    W4 = p.sbuf("p1_W4", [128, 512], F32)
    Wn = p.sbuf("p1_Wn", [128, 512], F32)
    D4 = p.sbuf("p1_D4", [128, 512], BF16)
    DT4 = p.sbuf("p1_DT4", [128, 512], BF16)
    tmpM = p.sbuf("p1_tmpM", [128, 512], BF16)
    Ncur = [p.sbuf(f"p1_Ncur{i}", [128, 512], BF16) for i in range(2)]
    NTst = p.sbuf("p1_NTst", [128, 7 * 512], BF16)
    GT4 = p.sbuf("p1_GT4", [128, 512], BF16)
    Kt4 = p.sbuf("p1_Kt4", [128, 512], BF16)
    Qt4 = p.sbuf("p1_Qt4", [128, 512], BF16)
    Kh4 = p.sbuf("p1_Kh4", [128, 512], BF16)
    Vtm = p.sbuf("p1_Vtm", [128, 512], BF16)
    Vsc = p.sbuf("p1_Vsc", [128, 512], BF16)
    Ktm = p.sbuf("p1_Ktm", [128, 256], BF16)
    IL4 = p.sbuf("p1_IL4", [128, 512], BF16)
    Xb = [p.sbuf(f"p1_X{i}", [128, 512], BF16) for i in range(2)]
    Rb = p.sbuf("p1_Rb", [128, 512], BF16)
    S0f = p.sbuf("p1_S0f", [128, 512], F32)
    S0b = p.sbuf("p1_S0b", [128, 512], BF16)
    oT = p.sbuf("p1_oT", [128, 2048], F32)
    gt1 = p.sbuf("p1_gt1", [128, 512], F32)
    gsq = p.sbuf("p1_gsq", [128, 512], BF16)
    grow = p.sbuf("p1_grow", [1, 2048], F32)
    growb = p.sbuf("p1_growb", [1, 2048], BF16)
    yslab = p.sbuf("p1_yslab", [128, 2048], BF16)

    pb0 = p.psum("p1_pb0", [128, 512], F32)
    pbT4 = p.psum("p1_pbT4", [4, 128], F32)
    pbT12 = p.psum("p1_pbT12", [12, 128], BF16)
    pb3 = p.psum("p1_pb3", [128, 512], F32)
    pb4 = p.psum("p1_pb4", [128, 512], F32)
    pb5 = p.psum("p1_pb5", [128, 512], F32)
    pb6 = p.psum("p1_pb6", [128, 512], F32)
    pb7 = p.psum("p1_pb7", [128, 512], BF16)

    LD = s.ch("LD")
    LDC = s.ch("LDC")
    LDW = s.ch("LDW")
    T = s.ch("T")
    V = s.ch("V")
    A = s.ch("A")
    G = s.ch("G")
    ST = s.ch("ST")

    for (ch, n) in H.get("pre_waits1", []):
        for eng in ENGS:
            s.wait(eng, ch, n)

    def view_b(buf, cols, c, n):
        return buf[:, cols * c: cols * c + n]

    s.op("sync", lambda e: e.dma_start(out=sid_b[:], in_=H["identb"]), LDC, 16)
    s.op("sync", lambda e: e.dma_start(out=sid_f[:], in_=H["identf"]), LDC, 16)
    s.op("sync", lambda e: e.dma_start(out=ut_f[:], in_=H["utones_f"]), LDC, 16)
    s.op("sync", lambda e: e.dma_start(out=isl_f[:], in_=H["isl_f"]), LDC, 16)
    s.op("sync", lambda e: e.dma_start(out=ilc_f[:], in_=H["ilc_f"]), LDC, 16)
    s.op("sync", lambda e: e.dma_start(out=sel4f[:], in_=H["sel4f"]), LDC, 16)
    s.op("sync", lambda e: e.dma_start(out=sel12b[:], in_=H["sel12b"]), LDC, 16)
    s.op("sync", lambda e: e.dma_start(
        out=convc[:].rearrange("p (j l) -> p j l", j=8),
        in_=H["conv_s"].rearrange("(j p) l -> p j l", p=128)), LDC, 16)
    s.op("sync", lambda e: e.dma_start(out=normw[:], in_=H["normw_s"]), LDC, 16)
    s.op("sync", lambda e: e.dma_start(out=row16[0:1, 0:4], in_=H["alog_s"]), LDC, 16)
    s.op("sync", lambda e: e.dma_start(out=row16[0:1, 4:8], in_=H["dtb_s"]), LDC, 16)
    nconst = LDC.n

    for k in range(16):
        s.wait("sync", V, 0)
        s.op("sync", lambda e, k=k: e.dma_start(
            out=wld[:], in_=H["wqkv_s"][k * 128:(k + 1) * 128, :]), LDW, 16)
        s.wait("vector", LDW, LDW.n)
        s.op("vector", lambda e, k=k: e.tensor_copy(wqkvb[:, k * 1024:(k + 1) * 1024], wld[:]), V)
        s.wait("sync", V, V.n)
    for k in range(16):
        s.op("sync", lambda e, k=k: e.dma_start(
            out=wld[:, 0:512], in_=H["wz_s"][k * 128:(k + 1) * 128, :]), LDW, 16)
        s.wait("vector", LDW, LDW.n)
        s.op("vector", lambda e, k=k: e.tensor_copy(wzb[:, k * 512:(k + 1) * 512], wld[:, 0:512]), V)
        s.wait("sync", V, V.n)
    for k in range(16):
        s.op("sync", lambda e, k=k: e.dma_start(
            out=wld[:, 0:8], in_=H["wab_s"][k * 128:(k + 1) * 128, :]), LDW, 16)
        s.wait("vector", LDW, LDW.n)
        s.op("vector", lambda e, k=k: e.tensor_copy(wabb[:, k * 8:(k + 1) * 8], wld[:, 0:8]), V)
        s.wait("sync", V, V.n)

    onesrow_f = p.sbuf("p1_onesrowf", [1, 128], F32)
    s.wait("vector", LDC, nconst)
    s.op("vector", lambda e: e.memset(onecol[:], 1.0), V)
    s.op("vector", lambda e: e.memset(onescol_b[:], 1.0), V)
    s.op("vector", lambda e: e.memset(ones1b[:], 1.0), V)
    s.op("vector", lambda e: e.memset(onesrow_f[:], 1.0), V)
    s.wait("scalar", LDC, nconst)
    s.op("scalar", lambda e: e.activation(row16[0:1, 8:12], row16[0:1, 0:4], AF.Exp), A)
    s.wait("vector", A, A.n)
    s.op("vector", lambda e: e.tensor_scalar(row16[0:1, 8:12], row16[0:1, 8:12], -1.0, None, ALU.mult), V)
    nneg = V.n
    s.wait("vector", V, nneg)
    s.op("vector", lambda e: e.tensor_copy(
        gsc[0:1, 0:16].rearrange("p (c h) -> p c h", c=4),
        row16[0:1, 8:12].unsqueeze(1).broadcast_to([1, 4, 4])), V)
    s.op("vector", lambda e: e.tensor_copy(
        gsc2[0:1, 0:16].rearrange("p (c h) -> p c h", c=4),
        row16[0:1, 4:8].unsqueeze(1).broadcast_to([1, 4, 4])), V)
    s.wait("tensor", V, V.n)
    s.wait("tensor", LDC, nconst)
    s.op("tensor", lambda e: e.matmul(pb3[:, 0:16], onesrow_f[:], gsc[0:1, 0:16], start=True, stop=True), T)
    s.op("tensor", lambda e: e.matmul(pb4[:, 0:16], onesrow_f[:], gsc2[0:1, 0:16], start=True, stop=True), T)
    s.wait("vector", T, T.n)
    s.op("vector", lambda e: e.tensor_copy(nega_t[:], pb3[:, 0:16]), V)
    s.op("vector", lambda e: e.tensor_copy(dtb_t[:], pb4[:, 0:16]), V)
    wdone_V, wdone_T = V.n, T.n

    prevRV = V.n
    prevRowsVcopy = 0
    prevRowsUseT = 0
    prevPBfreeV = V.n
    prevVtmUseV = 0
    prevPb7FreeV = 0
    prevST = 0
    prevProjT = T.n
    prevHaloG = 0
    prevAEnd = A.n
    for b in range(NBLK):
        r, t0 = b // 4, (b % 4) * 512

        s.wait("sync", T, prevProjT)
        ld_base = LD.n
        for k in range(16):
            s.op("sync", lambda e, k=k, r=r, t0=t0: e.dma_start(
                out=ldbuf[:, k * 512:(k + 1) * 512],
                in_=H["ln0T_ag"][2048 * r + 128 * k: 2048 * r + 128 * (k + 1),
                                 t0:t0 + 512]), LD, 16)
        ld_done = LD.n

        if b == 0:
            s.wait("tensor", V, wdone_V)
        s.wait("tensor", LD, ld_done)
        s.wait("tensor", V, prevPBfreeV)
        s.wait("tensor", A, prevAEnd)
        s.wait("vector", G, prevHaloG)
        pbanks = [pb3, pb4, pb5, pb6]
        evV = []
        for g in range(13):
            bank = pbanks[g % 4]
            if g >= 4:
                ch4, n4 = evV[g - 4]
                s.wait("tensor", ch4, n4)
            for k in range(16):
                if g < 8:
                    lhs = lambda e, k=k, g=g: wqkvb[:, 1024 * k + 128 * g: 1024 * k + 128 * (g + 1)]
                    ncols = 128
                elif g < 12:
                    lhs = lambda e, k=k, g=g: wzb[:, 512 * k + 128 * (g - 8): 512 * k + 128 * (g - 7)]
                    ncols = 128
                else:
                    lhs = lambda e, k=k: wabb[:, 8 * k: 8 * (k + 1)]
                    ncols = 8
                s.op("tensor", lambda e, k=k, g=g, bank=bank, lhs=lhs, ncols=ncols: e.matmul(
                    bank[0:ncols, :], lhs(e), ldbuf[:, k * 512:(k + 1) * 512],
                    start=(k == 0), stop=(k == 15)), T)
            tdone = T.n
            if g < 8:
                s.wait("vector", T, tdone)
                s.op("vector", lambda e, g=g, bank=bank: e.tensor_copy(
                    qkvh[g][:, 3:515], bank[:]), V)
                evV.append((V, V.n))
            elif g < 12:
                s.wait("scalar", T, tdone)
                s.op("scalar", lambda e, g=g, bank=bank: e.activation(
                    szT[:, (g - 8) * 512:(g - 7) * 512], bank[:], AF.Silu), A)
                evV.append((A, A.n))
            else:
                s.wait("vector", T, tdone)
                s.op("vector", lambda e, bank=bank: e.tensor_copy(abf[:], bank[0:8, :]), V)
                evV.append((V, V.n))
        projA = A.n

        for j in range(8):
            s.wait("vector", evV[j][0], evV[j][1])
            if b in (0, 4, 8, 12):
                s.op("vector", lambda e, j=j: e.memset(qkvh[j][:, 0:3], 0.0), V)
            cc0 = V.n
            s.op("vector", lambda e, j=j: e.tensor_scalar(
                convacc[:], qkvh[j][:, 3:515], convc[:, 4 * j + 3: 4 * j + 4],
                None, ALU.mult), V)
            for d2 in (1, 2, 3):
                s.wait("vector", V, V.n)
                s.op("vector", lambda e, j=j, d2=d2: e.scalar_tensor_tensor(
                    convacc[:], qkvh[j][:, 3 - d2:515 - d2],
                    convc[:, 4 * j + 3 - d2: 4 * j + 4 - d2], convacc[:],
                    ALU.mult, ALU.add), V)
            cdone = V.n
            s.wait("scalar", V, cdone)
            if j < 4:
                s.op("scalar", lambda e, j=j: e.activation(
                    qksilu[:, j * 512:(j + 1) * 512], convacc[:], AF.Silu), A)
            else:
                s.op("scalar", lambda e, j=j: e.activation(
                    vT[:, (j - 4) * 512:(j - 3) * 512], convacc[:], AF.Silu), A)
            s.wait("gpsimd", V, cdone)
            s.op("gpsimd", lambda e, j=j: e.tensor_copy(
                qkvh[j][:, 0:3], qkvh[j][:, 512:515]), G)
            s.wait("vector", A, A.n)
        silA = A.n

        s.wait("tensor", V, prevRV)
        for j in range(4):
            if j > 0:
                s.wait("scalar", T, T.n)
                s.wait("tensor", V, V.n)
            s.op("scalar", lambda e, j=j: e.activation(
                sqb[:], qksilu[:, j * 512:(j + 1) * 512], AF.Square), A)
            s.wait("tensor", A, A.n)
            s.op("tensor", lambda e: e.matmul(pb0[0:1, :], onescol_b[:], sqb[:],
                                              start=True, stop=True), T)
            s.wait("vector", T, T.n)
            s.op("vector", lambda e, j=j: e.tensor_copy(nrm[0:1, j * 512:(j + 1) * 512],
                                                        pb0[0:1, :]), V)
        s.wait("vector", V, V.n)
        s.op("vector", lambda e: e.tensor_scalar(nrm[0:1, :], nrm[0:1, :], EPS_L0, None, ALU.add), V)
        s.wait("scalar", V, V.n)
        s.op("scalar", lambda e: e.activation(nrm[0:1, :], nrm[0:1, :], AF.Sqrt), A)
        s.wait("vector", A, A.n)
        s.op("vector", lambda e: e.reciprocal(nrm[0:1, :], nrm[0:1, :]), V)
        s.wait("vector", V, V.n)
        s.op("vector", lambda e: e.tensor_scalar(nrm[0:1, 0:1024], nrm[0:1, 0:1024],
                                                 DK ** -0.5, None, ALU.mult), V)
        s.wait("vector", V, V.n)
        s.op("vector", lambda e: e.tensor_copy(nrmb[0:1, :], nrm[0:1, :]), V)
        nrmV = V.n
        for j in range(4):
            s.wait("tensor", V, nrmV if j == 0 else V.n)
            s.op("tensor", lambda e, j=j: e.matmul(
                pb0[:], ones1b[:], nrmb[0:1, j * 512:(j + 1) * 512],
                start=True, stop=True), T)
            s.wait("vector", T, T.n)
            s.op("vector", lambda e, j=j: e.tensor_mul(
                qkT[:, j * 512:(j + 1) * 512], qksilu[:, j * 512:(j + 1) * 512],
                pb0[:]), V)
        qkV = V.n
        if b == 0 and H.get("dbg"):
            s.wait("gpsimd", V, qkV)
            s.wait("gpsimd", A, silA)
            s.op("gpsimd", lambda e: e.dma_start(out=H["dbg"]["qkT"], in_=qkT[:]), ST, 16)
            s.op("gpsimd", lambda e: e.dma_start(out=H["dbg"]["vT"], in_=vT[:]), ST, 16)
            s.op("gpsimd", lambda e: e.dma_start(out=H["dbg"]["szT"], in_=szT[:]), ST, 16)

        s.wait("tensor", evV[12][0], evV[12][1])
        s.wait("tensor", V, qkV)
        for c in range(4):
            s.op("tensor", lambda e, c=c: e.transpose(
                pb0[:, 8 * c: 8 * (c + 1)], abf[:, c * 128:(c + 1) * 128],
                sid_f[0:8, 0:8]), T)
        abT = T.n
        s.wait("vector", T, abT)
        s.op("vector", lambda e: e.tensor_copy(ab_tm[:], pb0[:, 0:32]), V)
        s.wait("vector", V, V.n)
        av = ab_tm[:].rearrange("p (c q) -> p c q", c=4)[:, :, 0:4]
        bv = ab_tm[:].rearrange("p (c q) -> p c q", c=4)[:, :, 4:8]
        pkv = pk[:].rearrange("p (c q) -> p c q", c=4)
        dtbv = dtb_t[:].rearrange("p (c h) -> p c h", c=4)
        s.op("vector", lambda e, av=av, dtbv=dtbv: e.tensor_tensor(
            gsc[:].rearrange("p (c h) -> p c h", c=4), av, dtbv, ALU.add), V)
        s.wait("scalar", V, V.n)
        s.op("scalar", lambda e: e.activation(gsc2[:], gsc[:], AF.Abs), A)
        s.wait("scalar", A, A.n)
        s.op("scalar", lambda e: e.activation(gsc2[:], gsc2[:], AF.Exp, scale=-1.0), A)
        s.wait("scalar", A, A.n)
        s.op("scalar", lambda e: e.activation(gsc2[:], gsc2[:], AF.Ln, bias=onecol[:]), A)
        s.wait("vector", A, A.n)
        s.op("vector", lambda e: e.tensor_relu(W4[:, 0:16], gsc[:]), V)
        s.wait("vector", V, V.n)
        s.op("vector", lambda e: e.tensor_add(gsc2[:], gsc2[:], W4[:, 0:16]), V)
        s.wait("vector", V, V.n)
        s.op("vector", lambda e: e.tensor_mul(gsc[:], gsc2[:], nega_t[:]), V)
        gV = V.n
        s.wait("tensor", V, gV)
        s.op("tensor", lambda e: e.matmul(pb0[:, 0:16], ut_f[:], gsc[:],
                                          start=True, stop=True), T)
        s.wait("vector", T, T.n)
        s.op("vector", lambda e, pkv=pkv: e.tensor_copy(
            pkv[:, :, 0:4], pb0[:, 0:16].rearrange("p (c h) -> p c h", c=4)), V)
        cumV = V.n
        s.wait("scalar", V, cumV)
        s.op("scalar", lambda e, pkv=pkv: e.activation(pkv[:, :, 12:16],
                                                       pkv[:, :, 0:4], AF.Exp), A)
        s.op("scalar", lambda e, pkv=pkv, bv=bv: e.activation(pkv[:, :, 16:20],
                                                              bv, AF.Sigmoid), A)
        betA = A.n
        s.wait("vector", A, betA)
        s.op("vector", lambda e, pkv=pkv: e.tensor_scalar(
            pkv[:, :, 4:8], pkv[:, :, 16:20], -1.0, None, ALU.mult), V)
        s.op("vector", lambda e, pkv=pkv: e.tensor_tensor(
            pkv[:, :, 8:12], pkv[:, :, 16:20], pkv[:, :, 12:16], ALU.mult), V)
        s.wait("vector", V, V.n)
        s.op("vector", lambda e, pkv=pkv: e.tensor_copy(
            pkb[:].rearrange("p (c q) -> p c q", c=4), pkv[:, :, 4:16]), V)
        pkV = V.n
        if b == 0 and H.get("dbg"):
            s.wait("gpsimd", V, pkV)
            s.op("gpsimd", lambda e: e.dma_start(out=H["dbg"]["pk"], in_=pk[:]), ST, 16)

        for c in range(CPB):
            gchunk = b * CPB + c
            qk_c = lambda kh, cc=c: qkT[:, 512 * kh + 128 * cc: 512 * kh + 128 * (cc + 1)]
            kk_c = lambda kh, cc=c: qkT[:, 512 * (2 + kh) + 128 * cc: 512 * (2 + kh) + 128 * (cc + 1)]
            vv_c = lambda h, cc=c: vT[:, 512 * h + 128 * cc: 512 * h + 128 * (cc + 1)]
            cumcol = pk[:, 20 * c + 0: 20 * c + 4]
            negbcol = pk[:, 20 * c + 4: 20 * c + 8]
            betacol = pk[:, 20 * c + 16: 20 * c + 20]

            if gchunk in RESET_CHUNKS:
                s.wait("vector", T, T.n)
                s.wait("vector", A, A.n)
                s.op("vector", lambda e: e.memset(S0f[:], 0.0), V)
                s.wait("vector", V, V.n)
                s.op("vector", lambda e: e.tensor_copy(S0b[:], S0f[:]), V)
                s.wait("tensor", V, V.n)

            s.wait("tensor", V, max(prevRV, pkV))
            s.wait("tensor", V, qkV)
            for kh in range(2):
                kap, qap = kk_c(kh), qk_c(kh)
                s.op("tensor", lambda e, kh=kh, kap=kap: e.matmul(
                    pb0[:, 128 * kh:128 * (kh + 1)], kap, kap,
                    start=True, stop=True), T)
                s.op("tensor", lambda e, kh=kh, kap=kap, qap=qap: e.matmul(
                    pb0[:, 256 + 128 * kh:256 + 128 * (kh + 1)], kap, qap,
                    start=True, stop=True), T)
            pT = T.n

            s.wait("tensor", V, pkV)
            s.wait("tensor", V, prevRowsVcopy)
            s.op("tensor", lambda e, c=c: e.transpose(
                pbT4[:], pk[:, 20 * c: 20 * c + 4], sid_f[:]), T)
            cumT_T = T.n
            s.op("tensor", lambda e, c=c: e.transpose(
                pbT12[:], pkb[:, 12 * c: 12 * c + 12], sid_b[:]), T)
            r12T = T.n
            s.wait("vector", T, cumT_T)
            s.wait("vector", T, prevRowsUseT)
            s.op("vector", lambda e: e.tensor_copy(rows4[:], pbT4[:]), V)
            s.wait("vector", T, r12T)
            s.op("vector", lambda e: e.tensor_copy(rows12[:], pbT12[:]), V)
            rowsV = V.n
            prevRowsVcopy = rowsV

            s.wait("tensor", V, rowsV)
            s.wait("tensor", V, prevPBfreeV)
            for h in range(4):
                s.op("tensor", lambda e, h=h: e.matmul(
                    pb3[:, 128 * h:128 * (h + 1)], sel4f[:, 128 * h:128 * (h + 1)],
                    rows4[:], start=True, stop=True), T)
            for h in range(4):
                s.op("tensor", lambda e, h=h: e.matmul(
                    pb5[:, 128 * h:128 * (h + 1)], sel12b[:, 512 + 128 * h: 640 + 128 * h],
                    rows12[:], start=True, stop=True), T)
            for h in range(4):
                s.op("tensor", lambda e, h=h: e.matmul(
                    pb6[:, 128 * h:128 * (h + 1)], sel12b[:, 1024 + 128 * h: 1152 + 128 * h],
                    rows12[:], start=True, stop=True), T)
            selT = T.n
            prevRowsUseT = selT

            if b == 0 and c == 0 and H.get("dbg"):
                s.wait("vector", T, selT)
                s.op("vector", lambda e: e.tensor_copy(gt1[:], pb5[:]), V)
                s.wait("gpsimd", V, V.n)
                s.op("gpsimd", lambda e: e.dma_start(out=H["dbg"]["pb5"], in_=gt1[:]), ST, 16)
                s.op("gpsimd", lambda e: e.dma_start(out=H["dbg"]["rows12"], in_=rows12[:]), ST, 16)
                s.wait("vector", ST, ST.n)
            cumv = cumcol.unsqueeze(2).broadcast_to([128, 4, 128])
            s.wait("vector", T, selT)
            s.op("vector", lambda e, cumv=cumv: e.tensor_tensor(
                W4[:].rearrange("p (h t) -> p h t", h=4),
                pb3[:].rearrange("p (h t) -> p h t", h=4), cumv, ALU.subtract), V)
            s.wait("vector", V, V.n)
            s.op("vector", lambda e: e.tensor_scalar(Wn[:], W4[:], -1.0, 0.0,
                                                     ALU.mult, ALU.min), V)
            s.wait("scalar", V, V.n)
            s.op("scalar", lambda e: e.activation(D4[:], Wn[:], AF.Exp), A)
            s.wait("vector", A, A.n)
            islv = isl_f[:].unsqueeze(1).broadcast_to([128, 4, 128])
            s.op("vector", lambda e, islv=islv: e.tensor_tensor(
                D4[:].rearrange("p (h t) -> p h t", h=4),
                D4[:].rearrange("p (h t) -> p h t", h=4), islv, ALU.mult), V)
            dV = V.n
            s.wait("vector", V, dV)
            s.op("vector", lambda e: e.tensor_scalar(Wn[:], W4[:], 0.0, None,
                                                     ALU.min), V)
            s.wait("scalar", V, V.n)
            s.op("scalar", lambda e: e.activation(DT4[:], Wn[:], AF.Exp), A)
            s.wait("vector", A, A.n)
            ilcv = ilc_f[:].unsqueeze(1).broadcast_to([128, 4, 128])
            s.op("vector", lambda e, ilcv=ilcv: e.tensor_tensor(
                DT4[:].rearrange("p (h t) -> p h t", h=4),
                DT4[:].rearrange("p (h t) -> p h t", h=4), ilcv, ALU.mult), V)
            dtV = V.n

            if b == 0 and c == 0 and H.get("dbg"):
                s.wait("gpsimd", V, dtV)
                s.op("gpsimd", lambda e: e.dma_start(out=H["dbg"]["W4"], in_=W4[:]), ST, 16)
                s.op("gpsimd", lambda e: e.dma_start(out=H["dbg"]["D4"], in_=D4[:]), ST, 16)
                s.op("gpsimd", lambda e: e.dma_start(out=H["dbg"]["DT4"], in_=DT4[:]), ST, 16)
                s.wait("vector", ST, ST.n)
            pview = pb0[:, 0:256].rearrange("p (kh t) -> p kh t", kh=2
                        ).unsqueeze(2).broadcast_to([128, 2, 2, 128])
            kqview = pb0[:, 256:512].rearrange("p (kh t) -> p kh t", kh=2
                        ).unsqueeze(2).broadcast_to([128, 2, 2, 128])
            s.wait("vector", T, pT)
            s.wait("vector", V, dV)
            s.op("vector", lambda e, pview=pview: e.tensor_tensor(
                tmpM[:].rearrange("p (a b t) -> p a b t", a=2, b=2),
                pview, D4[:].rearrange("p (a b t) -> p a b t", a=2, b=2), ALU.mult), V)
            s.wait("vector", V, V.n)
            negbv = negbcol.unsqueeze(2).broadcast_to([128, 4, 128])
            s.op("vector", lambda e, negbv=negbv: e.tensor_tensor(
                Ncur[0][:].rearrange("p (h t) -> p h t", h=4),
                tmpM[:].rearrange("p (h t) -> p h t", h=4), negbv, ALU.mult), V)
            nV = V.n
            s.wait("tensor", V, nV)
            s.wait("tensor", V, prevPb7FreeV)
            for h in range(4):
                s.op("tensor", lambda e, h=h: e.transpose(
                    pb7[:, 128 * h:128 * (h + 1)], Ncur[0][:, 128 * h:128 * (h + 1)],
                    sid_b[:]), T)
            s.wait("vector", T, T.n)
            s.op("vector", lambda e: e.tensor_copy(NTst[:, 0:512], pb7[:]), V)
            ntV = V.n
            s.op("vector", lambda e, kqview=kqview: e.tensor_tensor(
                GT4[:].rearrange("p (a b t) -> p a b t", a=2, b=2),
                kqview, DT4[:].rearrange("p (a b t) -> p a b t", a=2, b=2), ALU.mult), V)
            gtV = V.n

            for h in range(4):
                kh = h // 2
                kap, qap = kk_c(kh), qk_c(kh)
                s.op("vector", lambda e, h=h, kap=kap: e.tensor_tensor(
                    Kt4[:, 128 * h:128 * (h + 1)], kap,
                    pb5[:, 128 * h:128 * (h + 1)], ALU.mult), V)
                s.op("vector", lambda e, h=h, qap=qap: e.tensor_tensor(
                    Qt4[:, 128 * h:128 * (h + 1)], qap,
                    pb6[:, 128 * h:128 * (h + 1)], ALU.mult), V)
            ktqtV = V.n
            for h in range(4):
                s.op("vector", lambda e, h=h: e.tensor_scalar(
                    IL4[:, 128 * h:128 * (h + 1)], sid_b[:],
                    pb6[:, 128 * h + 127:128 * h + 128], None, ALU.mult), V)
            ilV = V.n
            prevPBfreeV = V.n

            if b == 0 and c == 0 and H.get("dbg"):
                s.wait("gpsimd", V, max(nV, ntV, gtV))
                s.op("gpsimd", lambda e: e.dma_start(out=H["dbg"]["N0"], in_=Ncur[0][:]), ST, 16)
                s.op("gpsimd", lambda e: e.dma_start(out=H["dbg"]["NT0"], in_=NTst[:, 0:512]), ST, 16)
                s.op("gpsimd", lambda e: e.dma_start(out=H["dbg"]["GT4"], in_=GT4[:]), ST, 16)
                s.wait("vector", ST, ST.n)
            cur = 0
            s.wait("tensor", V, max(nV, ntV))
            for lev in range(1, 7):
                for h in range(4):
                    s.op("tensor", lambda e, lev=lev, h=h, cur=cur: e.matmul(
                        pb3[:, 128 * h:128 * (h + 1)],
                        NTst[:, 512 * (lev - 1) + 128 * h: 512 * (lev - 1) + 128 * (h + 1)],
                        Ncur[cur][:, 128 * h:128 * (h + 1)],
                        start=True, stop=True), T)
                sqT = T.n
                s.wait("vector", T, sqT)
                s.op("vector", lambda e, cur=cur: e.tensor_copy(Ncur[1 - cur][:], pb3[:]), V)
                nlevV = V.n
                s.wait("tensor", V, nlevV)
                for h in range(4):
                    s.op("tensor", lambda e, h=h, cur=cur: e.transpose(
                        pb7[:, 128 * h:128 * (h + 1)],
                        Ncur[1 - cur][:, 128 * h:128 * (h + 1)], sid_b[:]), T)
                s.wait("vector", T, T.n)
                s.op("vector", lambda e, lev=lev: e.tensor_copy(
                    NTst[:, 512 * lev: 512 * (lev + 1)], pb7[:]), V)
                s.wait("tensor", V, V.n)
                cur = 1 - cur

            if b == 0 and c == 0 and H.get("dbg"):
                s.wait("gpsimd", V, V.n)
                s.op("gpsimd", lambda e: e.dma_start(out=H["dbg"]["NT1"], in_=NTst[:, 512:1024]), ST, 16)
                s.op("gpsimd", lambda e: e.dma_start(out=H["dbg"]["NT2"], in_=NTst[:, 1024:1536]), ST, 16)
                s.op("gpsimd", lambda e: e.dma_start(out=H["dbg"]["NT6"], in_=NTst[:, 3072:3584]), ST, 16)
                s.wait("vector", ST, ST.n)
                s.wait("tensor", ST, ST.n)
            s.wait("tensor", A, silA)
            s.wait("tensor", V, prevVtmUseV)
            for h in range(4):
                vap = vv_c(h)
                s.op("tensor", lambda e, h=h, vap=vap: e.transpose(
                    pb7[:, 128 * h:128 * (h + 1)], vap, sid_b[:]), T)
            vtT = T.n
            s.wait("vector", T, vtT)
            s.op("vector", lambda e: e.tensor_copy(Vtm[:], pb7[:]), V)
            vtV = V.n
            if b == 0 and c == 0 and H.get("dbg"):
                s.wait("gpsimd", V, vtV)
                s.op("gpsimd", lambda e: e.dma_start(out=H["dbg"]["Vtm"], in_=Vtm[:]), ST, 16)
                s.wait("vector", ST, ST.n)
            s.wait("tensor", V, vtV)
            for kh in range(2):
                kap = kk_c(kh)
                s.op("tensor", lambda e, kh=kh, kap=kap: e.transpose(
                    pb7[:, 128 * kh:128 * (kh + 1)], kap, sid_b[:]), T)
            ktT = T.n
            s.wait("vector", T, ktT)
            s.op("vector", lambda e: e.tensor_copy(Ktm[:], pb7[:, 0:256]), V)
            ktV = V.n
            betav = betacol.unsqueeze(2).broadcast_to([128, 4, 128])
            s.wait("vector", V, vtV)
            s.op("vector", lambda e, betav=betav: e.tensor_tensor(
                Vsc[:].rearrange("p (h d) -> p h d", h=4),
                Vtm[:].rearrange("p (h d) -> p h d", h=4), betav, ALU.mult), V)
            vscV = V.n
            ktmv = Ktm[:].rearrange("p (kh d) -> p kh d", kh=2
                       ).unsqueeze(2).broadcast_to([128, 2, 2, 128])
            dtl = DT4[:].rearrange("p (h t) -> p h t", h=4)[:, :, 127:128
                       ].broadcast_to([128, 4, 128])
            s.wait("vector", V, ktV)
            s.op("vector", lambda e, ktmv=ktmv, dtl=dtl: e.tensor_tensor(
                Kh4[:].rearrange("p (a b d) -> p a b d", a=2, b=2), ktmv,
                dtl.rearrange("p (a b) d -> p a b d", a=2), ALU.mult), V)
            khV = V.n
            prevVtmUseV = V.n
            prevPb7FreeV = V.n

            s.wait("tensor", V, ktqtV)
            for h in range(4):
                s.op("tensor", lambda e, h=h: e.matmul(
                    pb0[:, 128 * h:128 * (h + 1)], Kt4[:, 128 * h:128 * (h + 1)],
                    S0b[:, 128 * h:128 * (h + 1)], start=True, stop=True), T)
            ksT = T.n
            s.wait("vector", T, ksT)
            s.wait("vector", V, vscV)
            s.op("vector", lambda e: e.tensor_sub(Rb[:], Vsc[:], pb0[:]), V)
            prevRV = V.n

            xcur = 0
            s.wait("vector", V, prevRV)
            s.op("vector", lambda e: e.tensor_copy(Xb[0][:], Rb[:]), V)
            s.wait("vector", V, V.n)
            for i in range(6, -1, -1):
                s.wait("tensor", V, V.n)
                for h in range(4):
                    s.op("tensor", lambda e, i=i, h=h, xcur=xcur: e.matmul(
                        pb3[:, 128 * h:128 * (h + 1)],
                        NTst[:, 512 * i + 128 * h: 512 * i + 128 * (h + 1)],
                        Xb[xcur][:, 128 * h:128 * (h + 1)],
                        start=True, stop=True), T)
                s.wait("vector", T, T.n)
                s.op("vector", lambda e, xcur=xcur: e.tensor_add(
                    Xb[1 - xcur][:], pb3[:], Xb[xcur][:]), V)
                s.wait("vector", V, V.n)
                xcur = 1 - xcur
            deltaV = V.n
            delta = Xb[xcur]

            if b == 0 and c == 0 and H.get("dbg"):
                s.wait("gpsimd", V, deltaV)
                s.op("gpsimd", lambda e, delta=delta: e.dma_start(out=H["dbg"]["X"], in_=delta[:]), ST, 16)
                s.op("gpsimd", lambda e: e.dma_start(out=H["dbg"]["Rb"], in_=Rb[:]), ST, 16)
                s.op("gpsimd", lambda e: e.dma_start(out=H["dbg"]["Kt4"], in_=Kt4[:]), ST, 16)
                s.op("gpsimd", lambda e: e.dma_start(out=H["dbg"]["Qt4"], in_=Qt4[:]), ST, 16)
                s.wait("vector", ST, ST.n)
            s.wait("tensor", V, max(deltaV, gtV))
            for h in range(4):
                s.op("tensor", lambda e, h=h, delta=delta: e.matmul(
                    pb5[:, 128 * h:128 * (h + 1)], delta[:, 128 * h:128 * (h + 1)],
                    GT4[:, 128 * h:128 * (h + 1)], start=True, stop=False), T)
                s.op("tensor", lambda e, h=h: e.matmul(
                    pb5[:, 128 * h:128 * (h + 1)], S0b[:, 128 * h:128 * (h + 1)],
                    Qt4[:, 128 * h:128 * (h + 1)], start=False, stop=True), T)
            oT_T = T.n
            s.wait("vector", T, oT_T)
            s.op("vector", lambda e, c=c: e.tensor_copy(
                oT[:].rearrange("p (h ct) -> p h ct", h=4)[:, :, 128 * c:128 * (c + 1)],
                pb5[:].rearrange("p (h t) -> p h t", h=4)), V)
            oEvV = V.n

            s.wait("tensor", V, max(khV, ilV, deltaV))
            for h in range(4):
                s.op("tensor", lambda e, h=h, delta=delta: e.matmul(
                    pb6[:, 128 * h:128 * (h + 1)], Kh4[:, 128 * h:128 * (h + 1)],
                    delta[:, 128 * h:128 * (h + 1)], start=True, stop=False), T)
                s.op("tensor", lambda e, h=h: e.matmul(
                    pb6[:, 128 * h:128 * (h + 1)], IL4[:, 128 * h:128 * (h + 1)],
                    S0b[:, 128 * h:128 * (h + 1)], start=False, stop=True), T)
            sT = T.n
            s.wait("vector", T, sT)
            s.op("vector", lambda e: e.tensor_copy(S0f[:], pb6[:]), V)
            s.wait("scalar", T, sT)
            s.wait("scalar", V, V.n)
            s.op("scalar", lambda e: e.copy(S0b[:], pb6[:]), A)
            s.wait("tensor", A, A.n)
            prevPBfreeV = V.n

        if b == 0 and H.get("dbg"):
            s.wait("gpsimd", V, V.n)
            s.op("gpsimd", lambda e: e.dma_start(out=H["dbg"]["oT"], in_=oT[:]), ST, 16)
            s.wait("vector", ST, ST.n)
        s.wait("vector", ST, prevST)
        s.wait("vector", A, A.n)
        for h in range(4):
            s.wait("vector", A, A.n)
            s.op("vector", lambda e, h=h: e.tensor_mul(
                gt1[:], oT[:, 512 * h: 512 * (h + 1)], szT[:, 512 * h:512 * (h + 1)]), V)
            s.wait("scalar", V, V.n)
            s.op("scalar", lambda e: e.activation(gsq[:], gt1[:], AF.Square), A)
            s.wait("tensor", A, A.n)
            s.wait("tensor", V, V.n if h > 0 else prevRV)
            s.op("tensor", lambda e: e.matmul(pb0[0:1, :], onescol_b[:], gsq[:],
                                              start=True, stop=True), T)
            s.wait("vector", T, T.n)
            s.op("vector", lambda e, h=h: e.tensor_copy(
                grow[0:1, 512 * h:512 * (h + 1)], pb0[0:1, :]), V)
        s.wait("vector", V, V.n)
        s.op("vector", lambda e: e.tensor_scalar(grow[0:1, :], grow[0:1, :],
                                                 1.0 / DV, EPS_L0, ALU.mult, ALU.add), V)
        s.wait("scalar", V, V.n)
        s.op("scalar", lambda e: e.activation(grow[0:1, :], grow[0:1, :], AF.Sqrt), A)
        s.wait("vector", A, A.n)
        s.op("vector", lambda e: e.reciprocal(grow[0:1, :], grow[0:1, :]), V)
        s.wait("vector", V, V.n)
        s.op("vector", lambda e: e.tensor_copy(growb[0:1, :], grow[0:1, :]), V)
        growV = V.n
        for h in range(4):
            s.wait("tensor", V, V.n if h > 0 else growV)
            s.op("tensor", lambda e, h=h: e.matmul(
                pb0[:], ones1b[:], growb[0:1, 512 * h:512 * (h + 1)],
                start=True, stop=True), T)
            s.wait("vector", T, T.n)
            s.op("vector", lambda e, h=h: e.tensor_mul(
                gt1[:], oT[:, 512 * h:512 * (h + 1)], szT[:, 512 * h:512 * (h + 1)]), V)
            s.wait("vector", V, V.n)
            s.op("vector", lambda e: e.tensor_mul(gt1[:], gt1[:], pb0[:]), V)
            s.wait("vector", V, V.n)
            s.op("vector", lambda e, h=h: e.tensor_scalar(
                yslab[:, 512 * h:512 * (h + 1)], gt1[:], normw[:], None, ALU.mult), V)
        yV = V.n
        s.wait("gpsimd", V, yV)
        for h in range(4):
            s.op("gpsimd", lambda e, h=h, b=b: e.dma_start(
                out=H["yT_sh"][512 * (b // 4) + 128 * h: 512 * (b // 4) + 128 * (h + 1),
                               512 * (b % 4): 512 * (b % 4 + 1)],
                in_=yslab[:, 512 * h:512 * (h + 1)]), ST, 16)
        prevST = ST.n
        prevProjT = T.n
        prevHaloG = G.n
        prevAEnd = A.n
        prevRV = max(prevRV, yV)

    s.wait("gpsimd", ST, ST.n)
    if H.get("y_a2a") is not None:
        a2a_out, ccch = H["y_a2a"]
        s.op("gpsimd", lambda e, a2a_out=a2a_out: e.collective_compute(
            "AllToAll", ALU.bypass, replica_groups=RG8,
            ins=[H["yT_sh"][:, :].opt()], outs=[a2a_out[:, :].opt()]), ccch, 1)



def make_consts(nc):
    ident_b = nc.inline_tensor(np.eye(128, dtype=ml_dtypes.bfloat16), "c_identb")
    ident_f = nc.inline_tensor(np.eye(128, dtype=np.float32), "c_identf")
    ut_f = nc.inline_tensor(np.triu(np.ones((128, 128), np.float32)), "c_utf")
    isl_f = nc.inline_tensor(np.tril(np.ones((128, 128), np.float32), -1), "c_islf")
    ilc_f = nc.inline_tensor(np.triu(np.ones((128, 128), np.float32), 0), "c_ilcf")
    sel4 = np.zeros((4, 512), np.float32)
    for h in range(4):
        sel4[h, 128 * h:128 * (h + 1)] = 1.0
    sel4f = nc.inline_tensor(sel4, "c_sel4f")
    sel12 = np.zeros((12, 1536), np.float32)
    for q in range(3):
        for h in range(4):
            sel12[4 * q + h, 512 * q + 128 * h: 512 * q + 128 * (h + 1)] = 1.0
    sel12b = nc.inline_tensor(sel12.astype(ml_dtypes.bfloat16), "c_sel12b")
    cm = np.zeros((128, 4 * 512), np.float32)
    for tt in range(4):
        for pp2 in range(128):
            cm[pp2, 512 * tt: 512 * tt + 128 * tt + pp2 + 1] = 1.0
    cmask4 = nc.inline_tensor(cm.astype(ml_dtypes.bfloat16), "c_cmask4")
    s2 = np.zeros((2, 256), np.float32)
    s2[0, 0:128] = 1.0
    s2[1, 128:256] = 1.0
    sel2 = nc.inline_tensor(s2.astype(ml_dtypes.bfloat16), "c_sel2")
    return dict(identb=ident_b.ap(), identf=ident_f.ap(), utones_f=ut_f.ap(),
                isl_f=isl_f.ap(), ilc_f=ilc_f.ap(), sel4f=sel4f.ap(),
                sel12b=sel12b.ap(), cmask4=cmask4.ap(), sel2=sel2.ap())


def build_ph0_test():
    nc = bass.Bass("TRN2", target_bir_lowering=False)
    es = ExitStack()
    p = Prog(nc, es)
    H = make_consts(nc)
    x_rows = nc.dram_tensor("x_rows", [TPC, D], F32, kind="ExternalInput").ap()
    ln0T_sh = nc.dram_tensor("ln0T_sh", [D, TPC], BF16, kind="ExternalOutput").ap()
    with es:
        block = es.enter_context(nc.Block())
        emit_ph0(p, block, x_rows, ln0T_sh, H["identb"], ag=None)
    return nc


def build_ph1_test(dbg=True):
    nc = bass.Bass("TRN2", target_bir_lowering=False)
    es = ExitStack()
    p = Prog(nc, es)
    H = make_consts(nc)
    H["ln0T_ag"] = nc.dram_tensor("ln0T_ag", [NCORE * D, TPC], BF16, kind="ExternalInput").ap()
    H["wqkv_s"] = nc.dram_tensor("wqkv_s", [D, 1024], F32, kind="ExternalInput").ap()
    H["wz_s"] = nc.dram_tensor("wz_s", [D, 512], F32, kind="ExternalInput").ap()
    H["wab_s"] = nc.dram_tensor("wab_s", [D, 8], F32, kind="ExternalInput").ap()
    H["conv_s"] = nc.dram_tensor("conv_s", [1024, 4], F32, kind="ExternalInput").ap()
    H["alog_s"] = nc.dram_tensor("alog_s", [1, 4], F32, kind="ExternalInput").ap()
    H["dtb_s"] = nc.dram_tensor("dtb_s", [1, 4], F32, kind="ExternalInput").ap()
    H["normw_s"] = nc.dram_tensor("normw_s", [128, 1], F32, kind="ExternalInput").ap()
    H["yT_sh"] = nc.dram_tensor("yT_sh", [4096, TPC], BF16, kind="ExternalOutput").ap()
    H["y_a2a"] = None
    H["dbg"] = None if not dbg else {
        "qkT": nc.dram_tensor("dbg_qkT", [128, 2048], BF16, kind="ExternalOutput").ap(),
        "vT": nc.dram_tensor("dbg_vT", [128, 2048], BF16, kind="ExternalOutput").ap(),
        "szT": nc.dram_tensor("dbg_szT", [128, 2048], BF16, kind="ExternalOutput").ap(),
        "pk": nc.dram_tensor("dbg_pk", [128, 80], F32, kind="ExternalOutput").ap(),
        "oT": nc.dram_tensor("dbg_oT", [128, 2048], F32, kind="ExternalOutput").ap(),
        "W4": nc.dram_tensor("dbg_W4", [128, 512], F32, kind="ExternalOutput").ap(),
        "D4": nc.dram_tensor("dbg_D4", [128, 512], BF16, kind="ExternalOutput").ap(),
        "DT4": nc.dram_tensor("dbg_DT4", [128, 512], BF16, kind="ExternalOutput").ap(),
        "N0": nc.dram_tensor("dbg_N0", [128, 512], BF16, kind="ExternalOutput").ap(),
        "NT0": nc.dram_tensor("dbg_NT0", [128, 512], BF16, kind="ExternalOutput").ap(),
        "GT4": nc.dram_tensor("dbg_GT4", [128, 512], BF16, kind="ExternalOutput").ap(),
        "X": nc.dram_tensor("dbg_X", [128, 512], BF16, kind="ExternalOutput").ap(),
        "Rb": nc.dram_tensor("dbg_Rb", [128, 512], BF16, kind="ExternalOutput").ap(),
        "Kt4": nc.dram_tensor("dbg_Kt4", [128, 512], BF16, kind="ExternalOutput").ap(),
        "Qt4": nc.dram_tensor("dbg_Qt4", [128, 512], BF16, kind="ExternalOutput").ap(),
        "pb5": nc.dram_tensor("dbg_pb5", [128, 512], F32, kind="ExternalOutput").ap(),
        "rows12": nc.dram_tensor("dbg_rows12", [12, 128], BF16, kind="ExternalOutput").ap(),
        "Vtm": nc.dram_tensor("dbg_Vtm", [128, 512], BF16, kind="ExternalOutput").ap(),
        "NT1": nc.dram_tensor("dbg_NT1", [128, 512], BF16, kind="ExternalOutput").ap(),
        "NT2": nc.dram_tensor("dbg_NT2", [128, 512], BF16, kind="ExternalOutput").ap(),
        "NT6": nc.dram_tensor("dbg_NT6", [128, 512], BF16, kind="ExternalOutput").ap(),
    }
    with es:
        sched = Sched(p, "p1")
        block = es.enter_context(nc.Block())
        emit_ph1(p, sched, H)
        sched.emit(block)
    return nc



def emit_ph1b(p, s, H):
    p.push_scope()
    rLD = s.ch("R1LD")
    rLW = s.ch("R1LW")
    rV = s.ch("R1V")
    rT = s.ch("R1T")
    rST = s.ch("R1ST")
    CCR1 = s.ch("CCR1")
    wldb = p.sbuf("r1_wld", [128, 2048], F32)
    wob1 = p.sbuf("r1_wob", [128, 4 * 2048], BF16)
    ytb = [p.sbuf(f"r1_yt{i}", [128, 4 * 512], BF16) for i in range(2)]
    attp = [p.sbuf(f"r1_at{i}", [128, 2048], F32) for i in range(2)]
    pr = [p.psum(f"r1_p{i}", [128, 512], F32) for i in range(2)]

    for h in range(4):
        s.wait("sync", rV, rV.n)
        s.op("sync", lambda e, h=h: e.dma_start(
            out=wldb[:], in_=H["wout_s"][128 * h:128 * (h + 1), :]), rLW, 16)
        s.wait("vector", rLW, rLW.n)
        s.op("vector", lambda e, h=h: e.tensor_copy(
            wob1[:, 2048 * h:2048 * (h + 1)], wldb[:]), rV)
    wcast = rV.n

    evT = {}
    stDone = {0: 0, 1: 0}
    for blk in range(32):
        j, m = blk // 4, blk % 4
        pp = blk % 2
        s.wait("sync", rT, evT.get(blk - 2, 0))
        for h in range(4):
            s.op("sync", lambda e, j=j, m=m, h=h, pp=pp: e.dma_start(
                out=ytb[pp][:, 512 * h:512 * (h + 1)],
                in_=H["yT_sh"][512 * j + 128 * h: 512 * j + 128 * (h + 1),
                               512 * m:512 * (m + 1)]), rLD, 16)
        s.wait("tensor", rLD, rLD.n)
        s.wait("tensor", rV, wcast)
        for tt in range(4):
            tb = tt % 2
            s.wait("vector", rST, stDone[tb])
            for db in range(4):
                bank = pr[db % 2]
                s.wait("tensor", rV, rV.n)
                for h in range(4):
                    s.op("tensor", lambda e, h=h, tt=tt, db=db, pp=pp, bank=bank: e.matmul(
                        bank[:], ytb[pp][:, 512 * h + 128 * tt: 512 * h + 128 * (tt + 1)],
                        wob1[:, 2048 * h + 512 * db: 2048 * h + 512 * (db + 1)],
                        start=(h == 0), stop=(h == 3)), rT)
                s.wait("vector", rT, rT.n)
                s.op("vector", lambda e, db=db, tb=tb, bank=bank: e.tensor_copy(
                    attp[tb][:, 512 * db:512 * (db + 1)], bank[:]), rV)
            s.wait("gpsimd", rV, rV.n)
            s.op("gpsimd", lambda e, j=j, m=m, tt=tt, tb=tb: e.dma_start(
                out=H["rs_y"][2048 * j + 512 * m + 128 * tt:
                              2048 * j + 512 * m + 128 * (tt + 1), :],
                in_=attp[tb][:]), rST, 16)
            stDone[tb] = rST.n
        evT[blk] = rT.n
    s.wait("gpsimd", rST, rST.n)
    s.op("gpsimd", lambda e: e.collective_compute(
        "ReduceScatter", ALU.add, replica_groups=RG8,
        ins=[H["rs_y"][:, :].opt()], outs=[H["att_dram"][:, :].opt()]), CCR1, 1)
    p.pop_scope()
    return CCR1



def emit_y2sel(p, s, H, pre=()):
    for (ch, n) in pre:
        for eng in ENGS:
            s.wait(eng, ch, n)
    p.push_scope()
    qLD = s.ch("S2LD")
    qLW = s.ch("S2LW")
    qV = s.ch("S2V")
    qT = s.ch("S2T")
    qST = s.ch("S2ST")
    sldf = p.sbuf("s2_sldf", [128, 256], F32)
    selb = p.sbuf("s2_selb", [128, 16 * 256], BF16)
    y2blk = [p.sbuf(f"s2_yb{i}", [128, D], BF16) for i in range(2)]
    orow = p.sbuf("s2_orow", [128, 2 * D], BF16)
    pq = [p.psum(f"s2_p{i}", [128, 512], F32) for i in range(8)]
    for q in range(16):
        s.wait("sync", qV, qV.n)
        s.op("sync", lambda e, q=q: e.dma_start(
            out=sldf[:], in_=H["ysel_s"][128 * q:128 * (q + 1), :]), qLW, 16)
        s.wait("vector", qLW, qLW.n)
        s.op("vector", lambda e, q=q: e.tensor_copy(
            selb[:, 256 * q:256 * (q + 1)], sldf[:]), qV)
    scast = qV.n
    evR = {}
    stR = {}
    for r in range(8):
        s.wait("tensor", qV, scast)
        s.wait("tensor", qV, evR.get(r - 1, 0))
        for q in range(16):
            pp = q % 2
            if q >= 2:
                s.wait("sync", qT, stR.get((r, q - 2), 0))
            elif r > 0:
                s.wait("sync", qT, stR.get((r - 1, 14 + q), 0))
            s.op("sync", lambda e, r=r, q=q, pp=pp: e.dma_start(
                out=y2blk[pp][:],
                in_=H["y2_ag"][2048 * r + 128 * q: 2048 * r + 128 * (q + 1), :]), qLD, 16)
            s.wait("tensor", qLD, qLD.n)
            for half in range(2):
                for ct in range(4):
                    s.op("tensor", lambda e, q=q, half=half, ct=ct, pp=pp: e.matmul(
                        pq[4 * half + ct][:],
                        selb[:, 256 * q + 128 * half: 256 * q + 128 * (half + 1)],
                        y2blk[pp][:, 512 * ct:512 * (ct + 1)],
                        start=(q == 0), stop=(q == 15)), qT)
            stR[(r, q)] = qT.n
        s.wait("vector", qT, qT.n)
        s.wait("vector", qST, qST.n)
        for half in range(2):
            for ct in range(4):
                s.op("vector", lambda e, half=half, ct=ct: e.tensor_copy(
                    orow[:, 2048 * half + 512 * ct: 2048 * half + 512 * (ct + 1)],
                    pq[4 * half + ct][:]), qV)
        evR[r] = qV.n
        s.wait("gpsimd", qV, qV.n)
        for half in range(2):
            s.op("gpsimd", lambda e, r=r, half=half: e.dma_start(
                out=H["y2_loc"][256 * r + 128 * half: 256 * r + 128 * (half + 1), :],
                in_=orow[:, 2048 * half:2048 * (half + 1)]), qST, 16)
    s.wait("gpsimd", qST, qST.n)
    p.pop_scope()
    return qST



def emit_prologue(p, s, H, f32_ags=("wout", "wo", "core", "down", "up", "sh1", "sh2")):
    nc = p.nc
    p.push_scope()
    PCC = s.ch("PCC")
    PLD = s.ch("PLD")
    PV = s.ch("PV")
    PT = s.ch("PT")
    PE_ = s.ch("PEv")
    PST = s.ch("PST")

    PGD = s.ch("PGD")
    for nm in f32_ags:
        s.op("gpsimd", lambda e, nm=nm: e.dma_start(
            out=H[nm + "_bnc"], in_=H[nm + "_s"]), PGD, 16)
    s.wait("gpsimd", PGD, PGD.n)
    for nm in f32_ags:
        s.op("gpsimd", lambda e, nm=nm: e.collective_compute(
            "AllGather", ALU.bypass, replica_groups=RG8,
            ins=[H[nm + "_bnc"][:, :].opt()], outs=[H[nm + "_ag"][:, :].opt()]), PCC, 1)

    wslab = p.sbuf("pr_wslab", [128, 8192], F32)
    wslabb = p.sbuf("pr_wslabb", [128, 8192], BF16)
    tslab = p.sbuf("pr_tslab", [128, 2 * 8192], BF16)
    sidb = p.sbuf("pr_sidb", [128, 128], BF16)
    h128 = p.sbuf("pr_h128", [128, 128], BF16)
    h64b = p.sbuf("pr_h64b", [128, 128], BF16)
    ptp = p.psum("pr_ptp", [128, 512], BF16)

    PLC = s.ch("PLC")
    s.op("sync", lambda e: e.dma_start(out=sidb[:], in_=H["identb"]), PLC, 16)
    s.op("sync", lambda e: e.dma_start(out=h128[:], in_=H["h128pm"]), PLC, 16)
    s.op("sync", lambda e: e.dma_start(out=h64b[:], in_=H["h64blk"]), PLC, 16)
    cstP = PLC.n

    for wname, shname in (("W1_s", "w1t_sh"), ("W3_s", "w3t_sh")):
        for hb in range(64):
            s.wait("sync", PV, PV.n)
            s.op("sync", lambda e, hb=hb, wname=wname: e.dma_start(
                out=wslab[:, 0:256], in_=H[wname][128 * hb:(hb + 1) * 128, :]), PLD, 16)
            s.wait("vector", PLD, PLD.n)
            s.wait("vector", PT, PT.n)
            s.op("vector", lambda e: e.tensor_copy(wslabb[:, 0:256], wslab[:, 0:256]), PV)
            s.wait("tensor", PV, PV.n)
            s.wait("tensor", PLC, cstP)
            s.wait("tensor", PE_, PE_.n)
            for dh in range(2):
                s.op("tensor", lambda e, dh=dh: e.transpose(
                    ptp[:, 128 * dh:128 * (dh + 1)], wslabb[:, 128 * dh:128 * (dh + 1)],
                    sidb[:]), PT)
            s.wait("scalar", PT, PT.n)
            for dh in range(2):
                s.op("scalar", lambda e, dh=dh, hb=hb: e.copy(
                    tslab[:, 8192 * dh + 128 * hb: 8192 * dh + 128 * (hb + 1)],
                    ptp[:, 128 * dh:128 * (dh + 1)]), PE_)
        s.wait("gpsimd", PE_, PE_.n)
        for dh in range(2):
            s.op("gpsimd", lambda e, dh=dh, shname=shname: e.dma_start(
                out=H[shname][128 * dh:128 * (dh + 1), :],
                in_=tslab[:, 8192 * dh:8192 * (dh + 1)]), PST, 16)
        s.wait("gpsimd", PST, PST.n)
        s.op("gpsimd", lambda e, shname=shname: e.collective_compute(
            "AllGather", ALU.bypass, replica_groups=RG8,
            ins=[H[shname][:, :].opt()], outs=[H[shname[:-3] + "_ag"][:, :].opt()]), PCC, 1)
        s.wait("scalar", PST, PST.n)

    for dh in range(2):
        s.wait("sync", PV, PV.n)
        s.op("sync", lambda e, dh=dh: e.dma_start(
            out=wslab[:], in_=H["W2_s"][128 * dh:128 * (dh + 1), :]), PLD, 16)
        s.wait("vector", PLD, PLD.n)
        s.wait("vector", PT, PT.n)
        s.op("vector", lambda e: e.tensor_copy(wslabb[:], wslab[:]), PV)
        s.wait("tensor", PV, PV.n)
        for hb in range(64):
            s.wait("tensor", PE_, PE_.n)
            s.op("tensor", lambda e, hb=hb: e.transpose(
                ptp[:, 128 * (hb % 2):128 * (hb % 2 + 1)],
                wslabb[:, 128 * hb:128 * (hb + 1)], sidb[:]), PT)
            s.wait("scalar", PT, PT.n)
            s.op("scalar", lambda e, hb=hb, dh=dh: e.copy(
                tslab[:, 8192 * 0 + 0:0 + 128] if False else
                tslab[:, 128 * (hb % 4):128 * (hb % 4 + 1)],
                ptp[:, 128 * (hb % 2):128 * (hb % 2 + 1)]), PE_)
            s.wait("gpsimd", PE_, PE_.n)
            s.op("gpsimd", lambda e, hb=hb, dh=dh: e.dma_start(
                out=H["w2t_dram"][128 * hb:128 * (hb + 1), 128 * dh:128 * (dh + 1)],
                in_=tslab[:, 128 * (hb % 4):128 * (hb % 4 + 1)]), PST, 16)
            if hb % 4 == 3:
                s.wait("scalar", PST, PST.n)
    st1in = p.sbuf("pr_st1in", [128, 256], BF16)
    st1out = p.sbuf("pr_st1out", [128, 256], BF16)
    ps1 = p.psum("pr_ps1", [128, 256], F32)
    s.wait("sync", PST, PST.n)
    for pr in range(64):
        s.wait("sync", PV, PV.n if pr > 0 else PV.n)
        s.op("sync", lambda e, pr=pr: e.dma_start(
            out=st1in[:], in_=H["w2t_dram"][128 * pr:128 * (pr + 1), :]), PLD, 16)
        s.wait("tensor", PLD, PLD.n)
        s.wait("tensor", PV, PV.n)
        s.op("tensor", lambda e: e.matmul(ps1[:], h64b[:], st1in[:],
                                          start=True, stop=True), PT)
        s.wait("vector", PT, PT.n)
        s.op("vector", lambda e: e.tensor_copy(st1out[:], ps1[:]), PV)
        s.wait("gpsimd", PV, PV.n)
        s.op("gpsimd", lambda e, pr=pr: e.dma_start(
            out=H["s1_dram"][128 * pr:128 * (pr + 1), :], in_=st1out[:]), PST, 16)
        s.wait("sync", PST, PST.n)
    s1v = H["s1_dram"].rearrange("(j i) d -> i j d", i=64)
    w2fv = H["w2f_sh"].rearrange("(j i) d -> i j d", i=64)
    for i in range(64):
        s.wait("sync", PV, PV.n)
        s.op("sync", lambda e, i=i: e.dma_start(out=st1in[:], in_=s1v[i]), PLD, 16)
        s.wait("tensor", PLD, PLD.n)
        s.wait("tensor", PV, PV.n)
        s.op("tensor", lambda e: e.matmul(ps1[:], h128[:], st1in[:],
                                          start=True, stop=True), PT)
        s.wait("scalar", PT, PT.n)
        s.op("scalar", lambda e: e.activation(st1out[:], ps1[:], AF.Copy,
                                              scale=float(1.0 / np.sqrt(8192.0))), PV)
        s.wait("gpsimd", PV, PV.n)
        s.op("gpsimd", lambda e, i=i: e.dma_start(out=w2fv[i], in_=st1out[:]), PST, 16)
        s.wait("sync", PST, PST.n)
    s.wait("gpsimd", PST, PST.n)
    s.op("gpsimd", lambda e: e.collective_compute(
        "AllGather", ALU.bypass, replica_groups=RG8,
        ins=[H["w2f_sh"][:, :].opt()], outs=[H["w2f_ag"][:, :].opt()]), PCC, 1)
    p.pop_scope()
    return PCC



def emit_ln_rows(s, V, A, src, dst, stat, sq, epsc):
    s.op("vector", lambda e: e.reduce_sum(stat[:, 0:1], src, axis=AX), V)
    s.wait("vector", V, V.n)
    s.op("vector", lambda e: e.tensor_scalar(stat[:, 2:3], stat[:, 0:1], 1.0 / D, None, ALU.mult), V)
    s.wait("scalar", V, V.n)
    s.op("scalar", lambda e: e.activation(sq[:], src, AF.Square, accum_out=stat[:, 1:2]), A)
    s.wait("vector", A, A.n)
    s.op("vector", lambda e: e.tensor_scalar(stat[:, 3:4], stat[:, 1:2], 1.0 / D, None, ALU.mult), V)
    s.op("vector", lambda e: e.tensor_mul(stat[:, 5:6], stat[:, 2:3], stat[:, 2:3]), V)
    s.wait("vector", V, V.n)
    s.op("vector", lambda e: e.tensor_sub(stat[:, 4:5], stat[:, 3:4], stat[:, 5:6]), V)
    s.wait("scalar", V, V.n)
    s.op("scalar", lambda e: e.activation(stat[:, 6:7], stat[:, 4:5], AF.Sqrt, bias=epsc), A)
    s.wait("vector", A, A.n)
    s.op("vector", lambda e: e.reciprocal(stat[:, 7:8], stat[:, 6:7]), V)
    s.wait("vector", V, V.n)
    s.op("vector", lambda e: e.tensor_scalar(dst, src, stat[:, 2:3], stat[:, 7:8],
                                             ALU.subtract, ALU.mult), V)
    return V.n


def emit_ph2(p, s, H):
    for (ch, n) in H.get("pre_waits", []):
        for eng in ENGS:
            s.wait(eng, ch, n)
    LDy = None
    LDX = s.ch("2LDX")
    LDW = s.ch("2LDW")
    V = s.ch("2V")
    A = s.ch("2A")
    T = s.ch("2T")
    ST = s.ch("2ST")
    NT = TPC // 128

    if H.get("skip_pass1"):
        return _emit_ph2_tail(p, s, H, LDX, LDW, V, A, T, ST, NT)

    p.push_scope()
    LDy = [s.ch("2LDa"), s.ch("2LDb")]
    woutb = p.sbuf("p2_woutb", [128, 32 * 2048], BF16)
    wldf = p.sbuf("p2_wldf", [128, 2048], F32)
    ytile = [p.sbuf(f"p2_yt{i}", [128, 32 * 128], BF16) for i in range(2)]
    atb = [p.sbuf(f"p2_atb{i}", [128, D], F32) for i in range(2)]
    pb = [p.psum(f"p2_pb{i}", [128, 512], F32) for i in range(6)]
    for k in range(32):
        s.wait("sync", V, V.n)
        s.op("sync", lambda e, k=k: e.dma_start(
            out=wldf[:], in_=H["wout_ag"][128 * k:128 * (k + 1), :]), LDW, 16)
        s.wait("vector", LDW, LDW.n)
        s.op("vector", lambda e, k=k: e.tensor_copy(
            woutb[:, 2048 * k:2048 * (k + 1)], wldf[:]), V)
    wcastV = V.n
    attT = {}
    evV1 = {}
    for i in range(NT):
        pp = i % 2
        if i >= 2:
            s.wait("sync", T, attT[i - 2])
        for k in range(32):
            s.op("sync", lambda e, k=k, i=i, pp=pp: e.dma_start(
                out=ytile[pp][:, 128 * k:128 * (k + 1)],
                in_=H["yT_loc"][128 * k:128 * (k + 1), 128 * i:128 * (i + 1)]), LDy[pp], 16)
        s.wait("tensor", LDy[pp], LDy[pp].n)
        s.wait("tensor", V, wcastV if i == 0 else evV1.get(i - 1, 0))
        for db in range(4):
            for k in range(32):
                s.op("tensor", lambda e, db=db, k=k, pp=pp: e.matmul(
                    pb[db][:], ytile[pp][:, 128 * k:128 * (k + 1)],
                    woutb[:, 2048 * k + 512 * db: 2048 * k + 512 * (db + 1)],
                    start=(k == 0), stop=(k == 31)), T)
        attT[i] = T.n
        s.wait("vector", T, T.n)
        s.wait("vector", ST, ST.n)
        for db in range(4):
            s.op("vector", lambda e, db=db, pp=pp: e.tensor_copy(
                atb[pp][:, 512 * db:512 * (db + 1)], pb[db][:]), V)
        evV1[i] = V.n
        s.wait("gpsimd", V, V.n)
        s.op("gpsimd", lambda e, i=i, pp=pp: e.dma_start(
            out=H["att_dram"][128 * i:128 * (i + 1), :], in_=atb[pp][:]), ST, 16)
    s.barrier("B12", pre_dma_waits=[("gpsimd", ST, ST.n)])
    p.pop_scope()
    return _emit_ph2_tail(p, s, H, LDX, LDW, V, A, T, ST, NT)


def _emit_ph2_tail(p, s, H, LDX, LDW, V, A, T, ST, NT):
    p.push_scope()
    at1 = p.sbuf("p2_at1", [128, D], F32)
    at2 = p.sbuf("p2_at2", [128, D], F32)
    xrow = p.sbuf("p2_xrow", [128, D], F32)
    x1b = p.sbuf("p2_x1b", [128, D], F32)
    stat = p.sbuf("p2_stat", [128, 8], F32)
    sq = p.sbuf("p2_sq", [128, D], F32)
    epsc = p.sbuf("p2_eps", [128, 1], F32)
    lnb = p.sbuf("p2_lnb", [128, D], BF16)
    sid = p.sbuf("p2_sid", [128, 128], BF16)
    slab = p.sbuf("p2_slab", [128, 16 * 128], BF16)
    ptr = [p.psum(f"p2_ptr{i}", [128, 128], BF16) for i in range(2)]
    s.op("sync", lambda e: e.dma_start(out=sid[:], in_=H["identb"]), LDW, 16)
    s.op("vector", lambda e: e.memset(epsc[:], EPS_LN), V)
    ld2 = s.ch("2LD2")
    for i in range(NT):
        s.wait("sync", V, V.n)
        s.wait("sync", A, A.n)
        s.op("sync", lambda e, i=i: e.dma_start(
            out=at1[:], in_=H["att_dram"][128 * i:128 * (i + 1), :]), ld2, 16)
        s.op("sync", lambda e, i=i: e.dma_start(
            out=xrow[:], in_=H["x_rows"][128 * i:128 * (i + 1), :]), ld2, 16)
        s.wait("vector", ld2, ld2.n)
        s.wait("scalar", ld2, ld2.n)
        emit_ln_rows(s, V, A, at1[:], at2[:], stat, sq, epsc[:])
        if i == 0 and "p2stat" in H:
            s.wait("gpsimd", V, V.n)
            s.op("gpsimd", lambda e: e.dma_start(out=H["p2stat"], in_=stat[:]), ST, 16)
            s.op("gpsimd", lambda e: e.dma_start(out=H["p2at2"], in_=at2[:]), ST, 16)
            s.op("gpsimd", lambda e: e.dma_start(out=H["p2at1"], in_=at1[:]), ST, 16)
            s.op("gpsimd", lambda e: e.dma_start(out=H["p2sq"], in_=sq[:]), ST, 16)
            s.wait("vector", ST, ST.n)
            s.wait("scalar", ST, ST.n)
        s.wait("vector", V, V.n)
        s.op("vector", lambda e: e.tensor_add(x1b[:], at2[:], xrow[:]), V)
        s.wait("vector", V, V.n)
        s.wait("gpsimd", V, V.n)
        s.wait("gpsimd", ST, ST.n)
        s.op("gpsimd", lambda e, i=i: e.dma_start(
            out=H["x1_dram"][128 * i:128 * (i + 1), :], in_=x1b[:]), ST, 16)
        emit_ln_rows(s, V, A, x1b[:], lnb[:], stat, sq, epsc[:])
        lnV = V.n
        s.wait("vector", ST, ST.n)
        s.wait("tensor", V, lnV)
        s.wait("tensor", LDW, LDW.n)
        s.wait("tensor", ST, ST.n)
        for k in range(16):
            if k >= 2:
                s.wait("tensor", V, slV)
            s.op("tensor", lambda e, k=k: e.transpose(
                ptr[k % 2][:], lnb[:, 128 * k:128 * (k + 1)], sid[:]), T)
            s.wait("vector", T, T.n)
            s.op("vector", lambda e, k=k: e.tensor_copy(
                slab[:, 128 * k:128 * (k + 1)], ptr[k % 2][:]), V)
            slV = V.n
        s.wait("gpsimd", V, slV)
        s.op("gpsimd", lambda e, i=i: e.dma_start(
            out=H["ln1T_dram"].rearrange("(k p) t -> p k t", p=128)[:, :, 128 * i:128 * (i + 1)],
            in_=slab[:].rearrange("p (k t) -> p k t", k=16)), ST, 16)
    s.barrier("B23", pre_dma_waits=[("gpsimd", ST, ST.n)])
    p.pop_scope()

    p.push_scope()
    lnTb = p.sbuf("p2_lnTb", [128, 16 * 512], BF16)
    hidT = p.sbuf("p2_hidT", [128, 64 * 512], BF16)
    w1tile = [p.sbuf(f"p2_w1t{i}", [128, 16 * 128], BF16) for i in range(2)]
    w3tile = [p.sbuf(f"p2_w3t{i}", [128, 16 * 128], BF16) for i in range(2)]
    w2tile = [p.sbuf(f"p2_w2t{i}", [128, 512], BF16) for i in range(2)]
    sg = [p.sbuf(f"p2_sg{i}", [128, 512], BF16) for i in range(2)]
    fout = p.sbuf("p2_fout", [128, 4 * D], F32)
    fat1 = p.sbuf("p2_at1b", [128, D], F32)
    fxrow = p.sbuf("p2_xrowb", [128, D], F32)
    x2b = p.sbuf("p2_x2b", [128, D], F32)
    fstat = p.sbuf("p2_statb", [128, 8], F32)
    fsq = p.sbuf("p2_sqb", [128, D], F32)
    fepsc = p.sbuf("p2_epsb", [128, 1], F32)
    flnb = p.sbuf("p2_lnbb", [128, D], BF16)
    fsid = p.sbuf("p2_sidb", [128, 128], BF16)
    fslab = p.sbuf("p2_slabb", [128, 16 * 128], BF16)
    fpb = [p.psum(f"p2_qb{i}", [128, 512], F32) for i in range(6)]
    fptr = [p.psum(f"p2_qtr{i}", [128, 128], BF16) for i in range(2)]
    s.op("sync", lambda e: e.dma_start(out=fsid[:], in_=H["identb"]), LDW, 16)
    s.op("vector", lambda e: e.memset(fepsc[:], EPS_LN), V)
    ld3 = s.ch("2LD3")
    hmmT = 0
    w2mmT = 0
    l2evV = 0
    awA = {}
    evA = {}
    awB = {}
    hidline = H["ln1T_dram"]
    for tb in range(4):
        t0 = 512 * tb
        s.wait("sync", T, hmmT)
        for k in range(16):
            s.op("sync", lambda e, k=k, t0=t0: e.dma_start(
                out=lnTb[:, 512 * k:512 * (k + 1)],
                in_=hidline[128 * k:128 * (k + 1), t0:t0 + 512]), ld3, 16)
        lnTb_done = ld3.n
        l2evV_tb = l2evV
        for ht in range(64):
            pp = ht % 2
            q0 = 2 * pp
            g = 64 * tb + ht
            s.wait("sync", T, awA.get(g - 2, 0))
            for k in range(16):
                s.op("sync", lambda e, k=k, ht=ht, pp=pp: e.dma_start(
                    out=w1tile[pp][:, 128 * k:128 * (k + 1)],
                    in_=H["w1t_ag"][128 * k:128 * (k + 1), 128 * ht:128 * (ht + 1)]), LDW, 16)
                s.op("sync", lambda e, k=k, ht=ht, pp=pp: e.dma_start(
                    out=w3tile[pp][:, 128 * k:128 * (k + 1)],
                    in_=H["w3t_ag"][128 * k:128 * (k + 1), 128 * ht:128 * (ht + 1)]), LDW, 16)
            s.wait("tensor", LDW, LDW.n)
            s.wait("tensor", ld3, lnTb_done)
            s.wait("tensor", V, max(evA.get(g - 2, 0), l2evV_tb))
            for k in range(16):
                s.op("tensor", lambda e, k=k, pp=pp, q0=q0: e.matmul(
                    fpb[q0][:], w1tile[pp][:, 128 * k:128 * (k + 1)],
                    lnTb[:, 512 * k:512 * (k + 1)], start=(k == 0), stop=(k == 15)), T)
            for k in range(16):
                s.op("tensor", lambda e, k=k, pp=pp, q0=q0: e.matmul(
                    fpb[q0 + 1][:], w3tile[pp][:, 128 * k:128 * (k + 1)],
                    lnTb[:, 512 * k:512 * (k + 1)], start=(k == 0), stop=(k == 15)), T)
            hmmT = T.n
            awA[g] = T.n
            s.wait("scalar", T, hmmT)
            s.op("scalar", lambda e, pp=pp, q0=q0: e.activation(
                sg[pp][:], fpb[q0][:], AF.Silu), A)
            s.wait("vector", A, A.n)
            s.op("vector", lambda e, ht=ht, pp=pp, q0=q0: e.tensor_tensor(
                hidT[:, 512 * ht:512 * (ht + 1)], sg[pp][:], fpb[q0 + 1][:], ALU.mult), V)
            evA[g] = V.n
        hidV = V.n
        for db in range(4):
            for h in range(64):
                pp = h % 2
                gb = 256 * tb + 64 * db + h
                s.wait("sync", T, awB.get(gb - 2, 0))
                s.op("sync", lambda e, h=h, db=db, pp=pp: e.dma_start(
                    out=w2tile[pp][:].rearrange("p (r d) -> p r d", r=2),
                    in_=H["w2f_view"][128 * h:128 * (h + 1), 2 * db:2 * db + 2, :]), LDW, 16)
                s.wait("tensor", LDW, LDW.n)
                if h == 0:
                    s.wait("tensor", V, hidV)
                    s.wait("tensor", V, l2evV)
                for ts in range(4):
                    s.op("tensor", lambda e, h=h, ts=ts, pp=pp: e.matmul(
                        fpb[2 + ts][:],
                        hidT[:].rearrange("p (h t) -> p h t", h=64)[:, h, 128 * ts:128 * (ts + 1)],
                        w2tile[pp][:], start=(h == 0), stop=(h == 63)), T)
                w2mmT = T.n
                awB[gb] = T.n
            s.wait("vector", T, w2mmT)
            for ts in range(4):
                s.op("vector", lambda e, ts=ts, db=db: e.tensor_copy(
                    fout[:].rearrange("p (ts d) -> p ts d", ts=4)[:, ts, 512 * db:512 * (db + 1)],
                    fpb[2 + ts][:]), V)
            l2evV = V.n
        for ts in range(4):
            i = 4 * tb + ts
            s.wait("sync", V, V.n)
            s.op("sync", lambda e, i=i: e.dma_start(
                out=fxrow[:], in_=H["x1_dram"][128 * i:128 * (i + 1), :]), ld3, 16)
            fo = fout[:].rearrange("p (ts d) -> p ts d", ts=4)[:, ts, :]
            s.wait("vector", V, l2evV)
            s.wait("vector", ld3, ld3.n)
            emit_ln_rows(s, V, A, fo, fat1[:], fstat, fsq, fepsc[:])
            s.wait("vector", V, V.n)
            s.op("vector", lambda e: e.tensor_add(x2b[:], fat1[:], fxrow[:]), V)
            s.wait("vector", V, V.n)
            s.wait("gpsimd", V, V.n)
            s.wait("gpsimd", ST, ST.n)
            s.op("gpsimd", lambda e, i=i: e.dma_start(
                out=H["x2_dram"][128 * i:128 * (i + 1), :], in_=x2b[:]), ST, 16)
            emit_ln_rows(s, V, A, x2b[:], flnb[:], fstat, fsq, fepsc[:])
            ln2V = V.n
            s.wait("vector", ST, ST.n)
            s.wait("tensor", V, ln2V)
            s.wait("tensor", LDW, LDW.n)
            s.wait("tensor", ST, ST.n)
            for k in range(16):
                if k >= 2:
                    s.wait("tensor", V, slV2)
                s.op("tensor", lambda e, k=k: e.transpose(
                    fptr[k % 2][:], flnb[:, 128 * k:128 * (k + 1)], fsid[:]), T)
                s.wait("vector", T, T.n)
                s.op("vector", lambda e, k=k: e.tensor_copy(
                    fslab[:, 128 * k:128 * (k + 1)], fptr[k % 2][:]), V)
                slV2 = V.n
            s.wait("gpsimd", V, slV2)
            s.op("gpsimd", lambda e, i=i: e.dma_start(
                out=H["ln2T_sh"].rearrange("(k p) t -> p k t", p=128)[:, :, 128 * i:128 * (i + 1)],
                in_=fslab[:].rearrange("p (k t) -> p k t", k=16)), ST, 16)
    s.wait("gpsimd", ST, ST.n)
    p.pop_scope()
    if H.get("ln2_ag") is not None:
        ag_out, ccch = H["ln2_ag"]
        s.op("gpsimd", lambda e, ag_out=ag_out: e.collective_compute(
            "AllGather", ALU.bypass, replica_groups=RG8,
            ins=[H["ln2T_sh"][:, :].opt()], outs=[ag_out[:, :].opt()]), ccch, 1)


def declare_prologue_drams(nc, need=("wout", "wo", "core", "down", "up", "sh1", "sh2",
                                     "w1", "w3", "w2")):
    H = {}
    shapes = dict(wout=(512, 2048), wo=(256, 2048), core=(256, 2048),
                  down=(256, 1024), up=(128, 2048), sh1=(128, 1024), sh2=(128, 1024))
    shapes = {nm: v for nm, v in shapes.items() if nm in need}
    for nm, (r, c) in shapes.items():
        H[nm + "_s"] = nc.dram_tensor(nm + "_s", [r, c], F32, kind="ExternalInput").ap()
        H[nm + "_bnc"] = nc.dram_tensor(nm + "_bnc", [r, c], F32, kind="Internal").ap()
        H[nm + "_ag"] = nc.dram_tensor(nm + "_ag", [8 * r, c], F32, kind="Internal",
                                       addr_space="Shared").ap()
    H["W1_s"] = nc.dram_tensor("W1_s", [8192, 256], F32, kind="ExternalInput").ap()
    H["W3_s"] = nc.dram_tensor("W3_s", [8192, 256], F32, kind="ExternalInput").ap()
    H["W2_s"] = nc.dram_tensor("W2_s", [256, 8192], F32, kind="ExternalInput").ap()
    H["w1t_sh"] = nc.dram_tensor("w1t_sh", [256, 8192], BF16, kind="Internal").ap()
    H["w3t_sh"] = nc.dram_tensor("w3t_sh", [256, 8192], BF16, kind="Internal").ap()
    H["w1t_ag"] = nc.dram_tensor("w1t_ag", [2048, 8192], BF16, kind="Internal",
                                 addr_space="Shared").ap()
    H["w3t_ag"] = nc.dram_tensor("w3t_ag", [2048, 8192], BF16, kind="Internal",
                                 addr_space="Shared").ap()
    H["w2t_dram"] = nc.dram_tensor("w2t_dram", [8192, 256], BF16, kind="Internal").ap()
    H["s1_dram"] = nc.dram_tensor("s1_dram", [8192, 256], BF16, kind="Internal").ap()
    H["w2f_sh"] = nc.dram_tensor("w2f_sh", [8192, 256], BF16, kind="Internal").ap()
    H["w2f_ag"] = nc.dram_tensor("w2f_ag", [65536, 256], BF16, kind="Internal",
                                 addr_space="Shared").ap()
    H["h128pm"] = nc.inline_tensor(hadamard_pm(128).astype(ml_dtypes.bfloat16), "c_h128").ap()
    h64 = hadamard_pm(64)
    blk = np.zeros((128, 128), np.float32)
    blk[:64, :64] = h64
    blk[64:, 64:] = h64
    H["h64blk"] = nc.inline_tensor(blk.astype(ml_dtypes.bfloat16), "c_h64b").ap()
    return H


def slice_prologue_inputs(inp, c):
    return {
        "wout_s": np.ascontiguousarray(inp["L0_attn_Wout"][512 * c:512 * (c + 1), :]),
        "wo_s": np.ascontiguousarray(inp["L1_attn_Wo"][256 * c:256 * (c + 1), :]),
        "core_s": np.ascontiguousarray(inp["L1_ffn_core_w"][256 * c:256 * (c + 1), :]),
        "down_s": np.ascontiguousarray(inp["L1_ffn_down_w"][256 * c:256 * (c + 1), :]),
        "up_s": np.ascontiguousarray(inp["L1_ffn_up_w"][128 * c:128 * (c + 1), :]),
        "sh1_s": np.ascontiguousarray(inp["L1_ffn_shared_w1"][128 * c:128 * (c + 1), :]),
        "sh2_s": np.ascontiguousarray(inp["L1_ffn_shared_w2"][128 * c:128 * (c + 1), :]),
        "W1_s": np.ascontiguousarray(inp["L0_ffn_W1"][:, 256 * c:256 * (c + 1)]),
        "W3_s": np.ascontiguousarray(inp["L0_ffn_W3"][:, 256 * c:256 * (c + 1)]),
        "W2_s": np.ascontiguousarray(inp["L0_ffn_W2"][256 * c:256 * (c + 1), :]),
    }


def build_ph2_test(no_prologue=False, skip_pass3=False, lean=False):
    nc = bass.Bass("TRN2", target_bir_lowering=False)
    es = ExitStack()
    p = Prog(nc, es)
    H = make_consts(nc)
    ik = "Internal" if lean else "ExternalOutput"
    if no_prologue:
        H["wout_ag"] = nc.dram_tensor("wout_ag", [4096, 2048], F32, kind="ExternalInput").ap()
        H["w1t_ag"] = nc.dram_tensor("w1t_ag", [2048, 8192], BF16, kind="ExternalInput").ap()
        H["w3t_ag"] = nc.dram_tensor("w3t_ag", [2048, 8192], BF16, kind="ExternalInput").ap()
        H["w2f_ag"] = nc.dram_tensor("w2f_ag", [65536, 256], BF16, kind="ExternalInput").ap()
    else:
        H.update(declare_prologue_drams(nc))
    H["yT_loc"] = nc.dram_tensor("yT_loc", [4096, TPC], BF16, kind="ExternalInput").ap()
    H["x_rows"] = nc.dram_tensor("x_rows", [TPC, D], F32, kind="ExternalInput").ap()
    H["x1_dram"] = nc.dram_tensor("x1_dram", [TPC, D], F32, kind=ik).ap()
    H["att_dram"] = nc.dram_tensor("att_dram", [TPC, D], F32, kind=ik).ap()
    H["ln1T_dram"] = nc.dram_tensor("ln1T_dram", [D, TPC], BF16, kind=ik).ap()
    H["x2_dram"] = nc.dram_tensor("x2_dram", [TPC, D], F32, kind="ExternalOutput").ap()
    H["ln2T_sh"] = nc.dram_tensor("ln2T_sh", [D, TPC], BF16, kind="ExternalOutput").ap()
    if not lean:
        H["p2stat"] = nc.dram_tensor("p2stat", [128, 8], F32, kind="ExternalOutput").ap()
        H["p2at2"] = nc.dram_tensor("p2at2", [128, D], F32, kind="ExternalOutput").ap()
        H["p2at1"] = nc.dram_tensor("p2at1", [128, D], F32, kind="ExternalOutput").ap()
        H["p2sq"] = nc.dram_tensor("p2sq", [128, D], F32, kind="ExternalOutput").ap()
    H["ln2_ag"] = None
    with es:
        sched = Sched(p, "pp")
        block = es.enter_context(nc.Block())
        if not no_prologue:
            PCC = emit_prologue(p, sched, H)
            H["pre_waits"] = [(PCC, 10)]
        H["w2f_view"] = H["w2f_ag"].rearrange("(r h) d -> h r d", r=8)
        H["skip_pass3"] = skip_pass3
        emit_ph2(p, sched, H)
        sched.emit(block)
    return nc


def build_pass2_test(fake1=0):
    nc = bass.Bass("TRN2", target_bir_lowering=False)
    es = ExitStack()
    p = Prog(nc, es)
    H = make_consts(nc)
    H["wout_ag"] = nc.dram_tensor("wout_ag", [4096, 2048], F32, kind="ExternalInput").ap()
    H["att_dram"] = nc.dram_tensor("att_in", [TPC, D], F32, kind="ExternalInput").ap()
    H["att_dram2"] = nc.dram_tensor("att_d2", [TPC, D], F32, kind="ExternalOutput").ap()
    H["yT_loc"] = nc.dram_tensor("yT_loc", [4096, TPC], BF16, kind="ExternalInput").ap()
    H["x_rows"] = nc.dram_tensor("x_rows", [TPC, D], F32, kind="ExternalInput").ap()
    H["x1_dram"] = nc.dram_tensor("x1_dram", [TPC, D], F32, kind="ExternalOutput").ap()
    H["ln1T_dram"] = nc.dram_tensor("ln1T_dram", [D, TPC], BF16, kind="ExternalOutput").ap()
    H["p2stat"] = nc.dram_tensor("p2stat", [128, 8], F32, kind="ExternalOutput").ap()
    H["p2at2"] = nc.dram_tensor("p2at2", [128, D], F32, kind="ExternalOutput").ap()
    H["p2at1"] = nc.dram_tensor("p2at1", [128, D], F32, kind="ExternalOutput").ap()
    H["p2sq"] = nc.dram_tensor("p2sq", [128, D], F32, kind="ExternalOutput").ap()
    with es:
        s = Sched(p, "q2")
        block = es.enter_context(nc.Block())
        LDW = s.ch("2LDW")
        V = s.ch("2V")
        A = s.ch("2A")
        T = s.ch("2T")
        ST = s.ch("2ST")
        if fake1 >= 1:
            p.push_scope()
            woutb = p.sbuf("p2_woutb", [128, 32 * 2048], BF16)
            wldf = p.sbuf("p2_wldf", [128, 2048], F32)
            for k in range(32):
                s.wait("sync", V, V.n)
                s.op("sync", lambda e, k=k: e.dma_start(
                    out=wldf[:], in_=H["wout_ag"][128 * k:128 * (k + 1), :]), LDW, 16)
                s.wait("vector", LDW, LDW.n)
                s.op("vector", lambda e, k=k: e.tensor_copy(
                    woutb[:, 2048 * k:2048 * (k + 1)], wldf[:]), V)
            if fake1 >= 2:
                LDy = [s.ch("2LDa"), s.ch("2LDb")]
                ytile = [p.sbuf(f"p2_yt{i}", [128, 32 * 128], BF16) for i in range(2)]
                atb = [p.sbuf(f"p2_atb{i}", [128, D], F32) for i in range(2)]
                pb = [p.psum(f"p2_pb{i}", [128, 512], F32) for i in range(6)]
                wcastV = V.n
                attT = {}
                evV1 = {}
                NTf = TPC // 128
                for i in range(NTf):
                    pp = i % 2
                    if i >= 2:
                        s.wait("sync", T, attT[i - 2])
                    for k in range(32):
                        s.op("sync", lambda e, k=k, i=i, pp=pp: e.dma_start(
                            out=ytile[pp][:, 128 * k:128 * (k + 1)],
                            in_=H["yT_loc"][128 * k:128 * (k + 1), 128 * i:128 * (i + 1)]), LDy[pp], 16)
                    s.wait("tensor", LDy[pp], LDy[pp].n)
                    s.wait("tensor", V, wcastV if i == 0 else evV1.get(i - 1, 0))
                    for db in range(4):
                        for k in range(32):
                            s.op("tensor", lambda e, db=db, k=k, pp=pp: e.matmul(
                                pb[db][:], ytile[pp][:, 128 * k:128 * (k + 1)],
                                woutb[:, 2048 * k + 512 * db: 2048 * k + 512 * (db + 1)],
                                start=(k == 0), stop=(k == 31)), T)
                    attT[i] = T.n
                    s.wait("vector", T, T.n)
                    s.wait("vector", ST, ST.n)
                    for db in range(4):
                        s.op("vector", lambda e, db=db, pp=pp: e.tensor_copy(
                            atb[pp][:, 512 * db:512 * (db + 1)], pb[db][:]), V)
                    evV1[i] = V.n
                    s.wait("gpsimd", V, V.n)
                    s.op("gpsimd", lambda e, i=i, pp=pp: e.dma_start(
                        out=H["att_dram2"][128 * i:128 * (i + 1), :], in_=atb[pp][:]), ST, 16)
            s.wait("vector", ST, ST.n)
            s.wait("tensor", ST, ST.n)
            s.wait("scalar", ST, ST.n)
            s.wait("sync", ST, ST.n)
            s.wait("gpsimd", ST, ST.n)
            s.wait("vector", V, V.n)
            s.wait("sync", V, V.n)
            s.wait("scalar", V, V.n)
            s.wait("tensor", V, V.n)
            s.wait("gpsimd", V, V.n)
            p.pop_scope()
        p.push_scope()
        if fake1 >= 3:
            H["att_dram"] = H["att_dram2"]
        at1 = p.sbuf("p2_at1", [128, D], F32)
        at2 = p.sbuf("p2_at2", [128, D], F32)
        xrow = p.sbuf("p2_xrow", [128, D], F32)
        x1b = p.sbuf("p2_x1b", [128, D], F32)
        stat = p.sbuf("p2_stat", [128, 8], F32)
        sq = p.sbuf("p2_sq", [128, D], F32)
        epsc = p.sbuf("p2_eps", [128, 1], F32)
        lnb = p.sbuf("p2_lnb", [128, D], BF16)
        sid = p.sbuf("p2_sid", [128, 128], BF16)
        slab = p.sbuf("p2_slab", [128, 16 * 128], BF16)
        ptr = [p.psum(f"p2_ptr{i}", [128, 128], BF16) for i in range(2)]
        s.op("sync", lambda e: e.dma_start(out=sid[:], in_=H["identb"]), LDW, 16)
        s.op("vector", lambda e: e.memset(epsc[:], EPS_LN), V)
        ld2 = s.ch("2LD2")
        NT = TPC // 128
        for i in range(NT):
            s.wait("sync", V, V.n)
            s.wait("sync", A, A.n)
            s.op("sync", lambda e, i=i: e.dma_start(
                out=at1[:], in_=H["att_dram"][128 * i:128 * (i + 1), :]), ld2, 16)
            s.op("sync", lambda e, i=i: e.dma_start(
                out=xrow[:], in_=H["x_rows"][128 * i:128 * (i + 1), :]), ld2, 16)
            s.wait("vector", ld2, ld2.n)
            s.wait("scalar", ld2, ld2.n)
            emit_ln_rows(s, V, A, at1[:], at2[:], stat, sq, epsc[:])
            if i == 0:
                s.wait("gpsimd", V, V.n)
                s.op("gpsimd", lambda e: e.dma_start(out=H["p2stat"], in_=stat[:]), ST, 16)
                s.op("gpsimd", lambda e: e.dma_start(out=H["p2at2"], in_=at2[:]), ST, 16)
                s.op("gpsimd", lambda e: e.dma_start(out=H["p2at1"], in_=at1[:]), ST, 16)
                s.op("gpsimd", lambda e: e.dma_start(out=H["p2sq"], in_=sq[:]), ST, 16)
                s.wait("vector", ST, ST.n)
                s.wait("scalar", ST, ST.n)
            s.wait("vector", V, V.n)
            s.op("vector", lambda e: e.tensor_add(x1b[:], at2[:], xrow[:]), V)
            s.wait("vector", V, V.n)
            s.wait("gpsimd", V, V.n)
            s.wait("gpsimd", ST, ST.n)
            s.op("gpsimd", lambda e, i=i: e.dma_start(
                out=H["x1_dram"][128 * i:128 * (i + 1), :], in_=x1b[:]), ST, 16)
            emit_ln_rows(s, V, A, x1b[:], lnb[:], stat, sq, epsc[:])
            lnV = V.n
            s.wait("vector", ST, ST.n)
            s.wait("tensor", V, lnV)
            s.wait("tensor", LDW, 16)
            s.wait("tensor", ST, ST.n)
            for k in range(16):
                if k >= 2:
                    s.wait("tensor", V, slV)
                s.op("tensor", lambda e, k=k: e.transpose(
                    ptr[k % 2][:], lnb[:, 128 * k:128 * (k + 1)], sid[:]), T)
                s.wait("vector", T, T.n)
                s.op("vector", lambda e, k=k: e.tensor_copy(
                    slab[:, 128 * k:128 * (k + 1)], ptr[k % 2][:]), V)
                slV = V.n
            s.wait("gpsimd", V, slV)
            s.op("gpsimd", lambda e, i=i: e.dma_start(
                out=H["ln1T_dram"].rearrange("(k p) t -> p k t", p=128)[:, :, 128 * i:128 * (i + 1)],
                in_=slab[:].rearrange("p (k t) -> p k t", k=16)), ST, 16)
        s.wait("gpsimd", ST, ST.n)
        p.pop_scope()
        s.emit(block)
    return nc



def emit_ph3(p, s, H):
    for (ch, n) in H.get("pre_waits3", []):
        for eng in ENGS:
            s.wait(eng, ch, n)
    p.push_scope()
    gLD = s.ch("3LD")
    gLDW = s.ch("3LDW")
    gV = s.ch("3V")
    gA = s.ch("3A")
    gT = s.ch("3T")
    gST = s.ch("3ST")

    wqb = p.sbuf("p3_wqb", [128, 16 * 256], BF16)
    wkb = p.sbuf("p3_wkb", [128, 16 * 256], BF16)
    wvb = p.sbuf("p3_wvb", [128, 16 * 256], BF16)
    wldf3 = p.sbuf("p3_wldf", [128, 256], F32)
    lnb3 = p.sbuf("p3_lnb", [128, 16 * 512], BF16)
    qT3 = p.sbuf("p3_qT", [128, 2 * 512], BF16)
    kT3 = p.sbuf("p3_kT", [128, 2 * 512], BF16)
    vT3 = p.sbuf("p3_vT", [128, 2 * 512], BF16)
    sqT3 = p.sbuf("p3_sqT", [128, 2 * 512], BF16)
    skT3 = p.sbuf("p3_skT", [128, 2 * 512], BF16)
    el1 = p.sbuf("p3_el1", [128, 512], F32)
    el2 = p.sbuf("p3_el2", [128, 512], F32)
    Vtm3 = p.sbuf("p3_Vtm", [128, 2 * 512], BF16)
    sktm3 = p.sbuf("p3_sktm", [128, 512], BF16)
    Aexp = p.sbuf("p3_Aexp", [128, 4 * 512], BF16)
    rmax = p.sbuf("p3_rmax", [128, 8], F32)
    rsum = p.sbuf("p3_rsum", [128, 8], F32)
    mem_f = [p.sbuf(f"p3_memf{h}", [128, 128], F32) for h in range(2)]
    mem_b = [p.sbuf(f"p3_memb{h}", [128, 128], BF16) for h in range(2)]
    z_f = [p.sbuf(f"p3_zf{h}", [128, 1], F32) for h in range(2)]
    z_b = [p.sbuf(f"p3_zb{h}", [128, 1], BF16) for h in range(2)]
    beta_t = p.sbuf("p3_betat", [128, 2 * 128], F32)
    betc_t = p.sbuf("p3_betct", [128, 2 * 128], F32)
    bsig2 = p.sbuf("p3_bsig2", [2, 128], F32)
    bsig2b = p.sbuf("p3_bsig2b", [2, 128], BF16)
    sel2b = p.sbuf("p3_sel2b", [2, 256], BF16)
    ones1b3 = p.sbuf("p3_ones1b", [1, 128], BF16)
    onescol3 = p.sbuf("p3_onescol", [128, 1], BF16)
    sid3 = p.sbuf("p3_sid", [128, 128], BF16)
    cmask = p.sbuf("p3_cmask", [128, 4 * 512], BF16)
    dcol = p.sbuf("p3_dcol", [128, 16], F32)
    attn_s = p.sbuf("p3_attn", [128, 4 * 128], BF16)
    dmem = p.sbuf("p3_dmem", [128, 128], BF16)
    dmm = p.sbuf("p3_dmm", [128, 128], F32)
    pA = p.psum("p3_pA", [128, 512], F32)
    pB = p.psum("p3_pB", [128, 512], F32)
    pC = p.psum("p3_pC", [128, 512], F32)
    pD = p.psum("p3_pD", [128, 512], F32)
    pE = p.psum("p3_pE", [128, 512], F32)
    pF = p.psum("p3_pF", [128, 512], BF16)
    pG = p.psum("p3_pG", [128, 512], F32)

    s.op("sync", lambda e: e.dma_start(out=sid3[:], in_=H["identb"]), gLDW, 16)
    s.op("sync", lambda e: e.dma_start(out=cmask[:], in_=H["cmask4"]), gLDW, 16)
    s.op("sync", lambda e: e.dma_start(out=bsig2[:], in_=H["betas_s"]), gLDW, 16)
    s.op("sync", lambda e: e.dma_start(out=sel2b[:], in_=H["sel2"]), gLDW, 16)
    s.op("vector", lambda e: e.memset(ones1b3[:], 1.0), gV)
    s.op("vector", lambda e: e.memset(onescol3[:], 1.0), gV)
    for k3 in range(16):
        s.wait("sync", gV, gV.n)
        s.op("sync", lambda e, k3=k3: e.dma_start(
            out=wldf3[:], in_=H["wq_s"][128 * k3:128 * (k3 + 1), :]), gLDW, 16)
        s.wait("vector", gLDW, gLDW.n)
        s.op("vector", lambda e, k3=k3: e.tensor_copy(wqb[:, 256 * k3:256 * (k3 + 1)], wldf3[:]), gV)
        s.wait("sync", gV, gV.n)
        s.op("sync", lambda e, k3=k3: e.dma_start(
            out=wldf3[:], in_=H["wk_s"][128 * k3:128 * (k3 + 1), :]), gLDW, 16)
        s.wait("vector", gLDW, gLDW.n)
        s.op("vector", lambda e, k3=k3: e.tensor_copy(wkb[:, 256 * k3:256 * (k3 + 1)], wldf3[:]), gV)
        s.wait("sync", gV, gV.n)
        s.op("sync", lambda e, k3=k3: e.dma_start(
            out=wldf3[:], in_=H["wv_s"][128 * k3:128 * (k3 + 1), :]), gLDW, 16)
        s.wait("vector", gLDW, gLDW.n)
        s.op("vector", lambda e, k3=k3: e.tensor_copy(wvb[:, 256 * k3:256 * (k3 + 1)], wldf3[:]), gV)
    wcast3 = gV.n
    s.wait("scalar", gLDW, gLDW.n)
    s.op("scalar", lambda e: e.activation(bsig2b[:], bsig2[:], AF.Sigmoid), gA)
    s.wait("tensor", gA, gA.n)
    for h3 in range(2):
        s.op("tensor", lambda e, h3=h3: e.matmul(
            pA[:, 128 * h3:128 * (h3 + 1)], sel2b[:, 128 * h3:128 * (h3 + 1)],
            bsig2b[:], start=True, stop=True), gT)
    s.wait("vector", gT, gT.n)
    s.op("vector", lambda e: e.tensor_copy(beta_t[:], pA[:, 0:256]), gV)
    s.wait("vector", gV, gV.n)
    s.op("vector", lambda e: e.tensor_scalar(betc_t[:], beta_t[:], -1.0, 1.0,
                                             ALU.mult, ALU.add), gV)
    binitV = gV.n
    for h3 in range(2):
        s.op("vector", lambda e, h3=h3: e.memset(mem_f[h3][:], 0.0), gV)
        s.op("vector", lambda e, h3=h3: e.memset(mem_b[h3][:], 0.0), gV)
        s.op("vector", lambda e, h3=h3: e.memset(z_f[h3][:], 1.0 / 128.0), gV)
        s.op("vector", lambda e, h3=h3: e.memset(z_b[h3][:], 1.0 / 128.0), gV)
    s.wait("vector", gV, gV.n)

    projT3 = 0
    segVdone = gV.n
    prevStT = 0
    for sg3 in range(int(_os.environ.get("PH3_NSEG", 32))):
        r3, t03 = sg3 // 4, (sg3 % 4) * 512
        s.wait("sync", gT, projT3)
        for k3 in range(16):
            s.op("sync", lambda e, k3=k3, r3=r3, t03=t03: e.dma_start(
                out=lnb3[:, 512 * k3:512 * (k3 + 1)],
                in_=H["ln2T_ag"][2048 * r3 + 128 * k3: 2048 * r3 + 128 * (k3 + 1),
                                 t03:t03 + 512]), gLD, 16)
        ld3done = gLD.n
        s.wait("tensor", gLD, ld3done)
        if sg3 == 0:
            s.wait("tensor", gV, wcast3)
        s.wait("tensor", gV, segVdone)
        for h3 in range(2):
            for k3 in range(16):
                s.op("tensor", lambda e, h3=h3, k3=k3: e.matmul(
                    pG[:], wqb[:, 256 * k3 + 128 * h3: 256 * k3 + 128 * (h3 + 1)],
                    lnb3[:, 512 * k3:512 * (k3 + 1)], start=(k3 == 0), stop=(k3 == 15)), gT)
            s.wait("vector", gT, gT.n)
            s.op("vector", lambda e, h3=h3: e.tensor_copy(qT3[:, 512 * h3:512 * (h3 + 1)], pG[:]), gV)
            s.wait("tensor", gV, gV.n)
            for k3 in range(16):
                s.op("tensor", lambda e, h3=h3, k3=k3: e.matmul(
                    pG[:], wkb[:, 256 * k3 + 128 * h3: 256 * k3 + 128 * (h3 + 1)],
                    lnb3[:, 512 * k3:512 * (k3 + 1)], start=(k3 == 0), stop=(k3 == 15)), gT)
            s.wait("vector", gT, gT.n)
            s.op("vector", lambda e, h3=h3: e.tensor_copy(kT3[:, 512 * h3:512 * (h3 + 1)], pG[:]), gV)
            s.wait("tensor", gV, gV.n)
            for k3 in range(16):
                s.op("tensor", lambda e, h3=h3, k3=k3: e.matmul(
                    pG[:], wvb[:, 256 * k3 + 128 * h3: 256 * k3 + 128 * (h3 + 1)],
                    lnb3[:, 512 * k3:512 * (k3 + 1)], start=(k3 == 0), stop=(k3 == 15)), gT)
            s.wait("vector", gT, gT.n)
            s.op("vector", lambda e, h3=h3: e.tensor_copy(vT3[:, 512 * h3:512 * (h3 + 1)], pG[:]), gV)
            s.wait("tensor", gV, gV.n)
        projT3 = gT.n
        projV3 = gV.n
        _st3 = int(_os.environ.get("PH3_STAGE", 9))
        if _st3 < 2:
            continue
        for (srcb, dstb) in ((qT3, sqT3), (kT3, skT3)):
            for h3 in range(2):
                sl3 = lambda b: b[:, 512 * h3:512 * (h3 + 1)]
                s.wait("vector", gV, gV.n)
                s.wait("vector", gA, gA.n)
                s.op("vector", lambda e, srcb=srcb, h3=h3: e.tensor_scalar(
                    el1[:], srcb[:, 512 * h3:512 * (h3 + 1)], 0.0, None, ALU.min), gV)
                s.wait("scalar", gV, gV.n)
                s.op("scalar", lambda e: e.activation(el2[:], el1[:], AF.Exp), gA)
                s.wait("vector", gA, gA.n)
                s.op("vector", lambda e, srcb=srcb, h3=h3: e.tensor_sub(
                    el1[:], srcb[:, 512 * h3:512 * (h3 + 1)], el1[:]), gV)
                s.wait("vector", gV, gV.n)
                s.op("vector", lambda e, dstb=dstb, h3=h3: e.tensor_add(
                    dstb[:, 512 * h3:512 * (h3 + 1)], el2[:], el1[:]), gV)
        eluV = gV.n
        if _st3 < 3:
            continue
        for h3 in range(2):
            qh = lambda: qT3[:, 512 * h3:512 * (h3 + 1)]
            kh = lambda: kT3[:, 512 * h3:512 * (h3 + 1)]
            vh = lambda: vT3[:, 512 * h3:512 * (h3 + 1)]
            sqh = lambda: sqT3[:, 512 * h3:512 * (h3 + 1)]
            skh = lambda: skT3[:, 512 * h3:512 * (h3 + 1)]
            s.wait("tensor", gV, gV.n)
            for c3 in range(4):
                s.op("tensor", lambda e, h3=h3, c3=c3: e.transpose(
                    pF[:, 128 * c3:128 * (c3 + 1)],
                    vT3[:, 512 * h3 + 128 * c3: 512 * h3 + 128 * (c3 + 1)], sid3[:]), gT)
            s.wait("vector", gT, gT.n)
            s.op("vector", lambda e, h3=h3: e.tensor_copy(Vtm3[:, 512 * h3:512 * (h3 + 1)], pF[:]), gV)
            vtmV = gV.n
            s.wait("tensor", gV, vtmV)
            for tt3 in range(4):
                s.op("tensor", lambda e, h3=h3, tt3=tt3: e.matmul(
                    pA[:], qT3[:, 512 * h3 + 128 * tt3: 512 * h3 + 128 * (tt3 + 1)],
                    kT3[:, 512 * h3:512 * (h3 + 1)], start=True, stop=True), gT)
                s.wait("vector", gT, gT.n)
                s.op("vector", lambda e, tt3=tt3: e.reduce_max(
                    rmax[:, tt3:tt3 + 1], pA[:], axis=AX), gV)
                s.wait("vector", gV, gV.n)
                s.op("vector", lambda e, tt3=tt3: e.tensor_scalar(
                    rmax[:, 4 + tt3:5 + tt3], rmax[:, tt3:tt3 + 1],
                    -(128.0 ** -0.5), None, ALU.mult), gV)
                s.wait("scalar", gV, gV.n)
                s.op("scalar", lambda e, tt3=tt3: e.activation(
                    el1[:], pA[:], AF.Exp, bias=rmax[:, 4 + tt3:5 + tt3],
                    scale=float(128.0 ** -0.5)), gA)
                s.wait("vector", gA, gA.n)
                s.op("vector", lambda e, tt3=tt3: e.tensor_tensor(
                    Aexp[:, 512 * tt3:512 * (tt3 + 1)], el1[:],
                    cmask[:, 512 * tt3:512 * (tt3 + 1)], ALU.mult), gV)
                s.wait("vector", gV, gV.n)
                s.op("vector", lambda e, tt3=tt3: e.reduce_sum(
                    rsum[:, tt3:tt3 + 1], Aexp[:, 512 * tt3:512 * (tt3 + 1)], axis=AX), gV)
                s.wait("vector", gV, gV.n)
                s.op("vector", lambda e, tt3=tt3: e.reciprocal(
                    rsum[:, 4 + tt3:5 + tt3], rsum[:, tt3:tt3 + 1]), gV)
                s.wait("tensor", gV, gV.n)
            expV = gV.n
            if _st3 < 4:
                continue
            for tt3 in range(4):
                for kt3 in range(4):
                    s.wait("tensor", gV, gV.n)
                    s.op("tensor", lambda e, tt3=tt3, kt3=kt3: e.transpose(
                        pF[:, 128 * (kt3 % 2):128 * (kt3 % 2 + 1)],
                        Aexp[:, 512 * tt3 + 128 * kt3: 512 * tt3 + 128 * (kt3 + 1)],
                        sid3[:]), gT)
                    s.wait("vector", gT, gT.n)
                    s.op("vector", lambda e, kt3=kt3: e.tensor_copy(
                        dmem[:], pF[:, 128 * (kt3 % 2):128 * (kt3 % 2 + 1)]), gV)
                    s.wait("tensor", gV, gV.n)
                    s.op("tensor", lambda e, h3=h3, tt3=tt3, kt3=kt3: e.matmul(
                        pB[:, 128 * tt3:128 * (tt3 + 1)], dmem[:],
                        Vtm3[:, 512 * h3 + 128 * kt3: 512 * h3 + 128 * (kt3 + 1)],
                        start=(kt3 == 0), stop=(kt3 == 3)), gT)
            adotT = gT.n
            if _st3 < 5:
                continue
            s.wait("tensor", gV, eluV)
            for tt3 in range(4):
                s.op("tensor", lambda e, h3=h3, tt3=tt3: e.matmul(
                    pC[:, 128 * tt3:128 * (tt3 + 1)],
                    sqT3[:, 512 * h3 + 128 * tt3: 512 * h3 + 128 * (tt3 + 1)],
                    mem_b[h3][:], start=True, stop=True), gT)
            for tt3 in range(4):
                s.op("tensor", lambda e, h3=h3, tt3=tt3: e.matmul(
                    pD[:, tt3:tt3 + 1],
                    sqT3[:, 512 * h3 + 128 * tt3: 512 * h3 + 128 * (tt3 + 1)],
                    z_b[h3][:], start=True, stop=True), gT)
            amemT = gT.n
            s.wait("vector", gT, amemT)
            s.wait("vector", gST, gST.n)
            s.op("vector", lambda e: e.tensor_copy(dcol[:, 0:4], pD[:, 0:4]), gV)
            s.wait("vector", gV, gV.n)
            s.op("vector", lambda e: e.reciprocal(dcol[:, 4:8], dcol[:, 0:4]), gV)
            s.wait("vector", gV, gV.n)
            for tt3 in range(4):
                s.op("vector", lambda e, tt3=tt3: e.tensor_scalar(
                    el1[:, 0:128], pB[:, 128 * tt3:128 * (tt3 + 1)],
                    rsum[:, 4 + tt3:5 + tt3], None, ALU.mult), gV)
                s.wait("vector", gV, gV.n)
                s.op("vector", lambda e, h3=h3, tt3=tt3: e.tensor_tensor(
                    el1[:, 128:256], el1[:, 0:128],
                    betc_t[:, 128 * h3:128 * (h3 + 1)], ALU.mult), gV)
                s.op("vector", lambda e, tt3=tt3: e.tensor_scalar(
                    el1[:, 256:384], pC[:, 128 * tt3:128 * (tt3 + 1)],
                    dcol[:, 4 + tt3:5 + tt3], None, ALU.mult), gV)
                s.wait("vector", gV, gV.n)
                s.op("vector", lambda e, h3=h3, tt3=tt3: e.tensor_tensor(
                    el2[:, 128:256], el1[:, 256:384],
                    beta_t[:, 128 * h3:128 * (h3 + 1)], ALU.mult), gV)
                s.wait("vector", gV, gV.n)
                s.op("vector", lambda e, tt3=tt3: e.tensor_add(
                    attn_s[:, 128 * tt3:128 * (tt3 + 1)], el1[:, 128:256], el2[:, 128:256]), gV)
                s.wait("vector", gV, gV.n)
            attV = gV.n
            s.wait("gpsimd", gV, attV)
            s.wait("gpsimd", gST, prevStT)
            for tt3 in range(4):
                base = 256 * (sg3 // 4) + 64 * (sg3 % 4) + 32 * h3 + 8 * tt3
                dview = H["y2_sh"][base:base + 8, :].rearrange("j (m d) -> (j m) d", m=16)
                s.op("gpsimd", lambda e, dview=dview, tt3=tt3: e.dma_start(
                    out=dview, in_=attn_s[:, 128 * tt3:128 * (tt3 + 1)]), gST, 16)
            if _st3 < 6:
                continue
            s.wait("tensor", gV, attV)
            for kt3 in range(4):
                s.op("tensor", lambda e, h3=h3, kt3=kt3: e.matmul(
                    pD[:, 128 + 128 * kt3 if False else 128 * kt3:128 * (kt3 + 1)] if False else pE[:, 128 * kt3:128 * (kt3 + 1)],
                    skT3[:, 512 * h3 + 128 * kt3: 512 * h3 + 128 * (kt3 + 1)],
                    mem_b[h3][:], start=True, stop=True), gT)
            for kt3 in range(4):
                s.op("tensor", lambda e, h3=h3, kt3=kt3: e.matmul(
                    pD[:, 8 + kt3: 9 + kt3],
                    skT3[:, 512 * h3 + 128 * kt3: 512 * h3 + 128 * (kt3 + 1)],
                    z_b[h3][:], start=True, stop=True), gT)
            dmT = gT.n
            s.wait("vector", gT, dmT)
            s.op("vector", lambda e: e.tensor_copy(dcol[:, 8:12], pD[:, 8:12]), gV)
            s.wait("vector", gV, gV.n)
            s.op("vector", lambda e: e.reciprocal(dcol[:, 12:16], dcol[:, 8:12]), gV)
            s.wait("vector", gV, gV.n)
            for kt3 in range(4):
                s.wait("tensor", gV, gV.n)
                s.op("tensor", lambda e, h3=h3, kt3=kt3: e.transpose(
                    pF[:, 256 + 128 * (kt3 % 2): 256 + 128 * (kt3 % 2 + 1)],
                    skT3[:, 512 * h3 + 128 * kt3: 512 * h3 + 128 * (kt3 + 1)], sid3[:]), gT)
                s.wait("vector", gT, gT.n)
                s.op("vector", lambda e, kt3=kt3: e.tensor_copy(
                    sktm3[:, 128 * (kt3 % 2):128 * (kt3 % 2 + 1)],
                    pF[:, 256 + 128 * (kt3 % 2): 256 + 128 * (kt3 % 2 + 1)]), gV)
                s.op("vector", lambda e, kt3=kt3: e.tensor_scalar(
                    dmm[:], pE[:, 128 * kt3:128 * (kt3 + 1)],
                    dcol[:, 12 + kt3:13 + kt3], None, ALU.mult), gV)
                s.wait("vector", gV, gV.n)
                s.op("vector", lambda e, h3=h3, kt3=kt3: e.tensor_sub(
                    dmem[:], Vtm3[:, 512 * h3 + 128 * kt3: 512 * h3 + 128 * (kt3 + 1)],
                    dmm[:]), gV)
                s.wait("tensor", gV, gV.n)
                s.op("tensor", lambda e, kt3=kt3: e.matmul(
                    pA[:, 0:128], sktm3[:, 128 * (kt3 % 2):128 * (kt3 % 2 + 1)],
                    dmem[:], start=(kt3 == 0), stop=False), gT)
                s.op("tensor", lambda e, kt3=kt3: e.matmul(
                    pC[:, 0:1], sktm3[:, 128 * (kt3 % 2):128 * (kt3 % 2 + 1)],
                    onescol3[:], start=(kt3 == 0), stop=False), gT)
                s.wait("vector", gT, gT.n)
            s.op("tensor", lambda e, h3=h3: e.matmul(
                pA[:, 0:128], sid3[:], mem_b[h3][:], start=False, stop=True), gT)
            s.op("tensor", lambda e, h3=h3: e.matmul(
                pC[:, 0:1], sid3[:], z_b[h3][:], start=False, stop=True), gT)
            memT = gT.n
            s.wait("vector", gT, memT)
            s.op("vector", lambda e, h3=h3: e.tensor_copy(mem_f[h3][:], pA[:, 0:128]), gV)
            s.op("vector", lambda e, h3=h3: e.tensor_copy(z_f[h3][:], pC[:, 0:1]), gV)
            s.wait("scalar", gT, memT)
            s.wait("scalar", gV, gV.n)
            s.op("scalar", lambda e, h3=h3: e.copy(mem_b[h3][:], pA[:, 0:128]), gA)
            s.op("scalar", lambda e, h3=h3: e.copy(z_b[h3][:], pC[:, 0:1]), gA)
            s.wait("tensor", gA, gA.n)
            s.wait("vector", gV, gV.n)
        segVdone = gV.n
        prevStT = gST.n
    s.wait("gpsimd", gST, gST.n)
    p.pop_scope()
    if H.get("y2_a2a") is not None:
        a2a_out, ccch = H["y2_a2a"]
        s.op("gpsimd", lambda e, a2a_out=a2a_out: e.collective_compute(
            "AllToAll", ALU.bypass, replica_groups=RG8,
            ins=[H["y2_sh"][:, :].opt()], outs=[a2a_out[:, :].opt()]), ccch, 1)


def build_ph3_test():
    nc = bass.Bass("TRN2", target_bir_lowering=False)
    es = ExitStack()
    p = Prog(nc, es)
    H = make_consts(nc)
    H["ln2T_ag"] = nc.dram_tensor("ln2T_ag", [NCORE * D, TPC], BF16, kind="ExternalInput").ap()
    H["wq_s"] = nc.dram_tensor("wq_s", [D, 256], F32, kind="ExternalInput").ap()
    H["wk_s"] = nc.dram_tensor("wk_s", [D, 256], F32, kind="ExternalInput").ap()
    H["wv_s"] = nc.dram_tensor("wv_s", [D, 256], F32, kind="ExternalInput").ap()
    H["betas_s"] = nc.dram_tensor("betas_s", [2, 128], F32, kind="ExternalInput").ap()
    H["y2_sh"] = nc.dram_tensor("y2_sh", [TPC, D], BF16, kind="ExternalOutput").ap()
    H["y2_a2a"] = None
    with es:
        sched = Sched(p, "p3")
        block = es.enter_context(nc.Block())
        emit_ph3(p, sched, H)
        sched.emit(block)
    return nc



def emit_ph4(p, s, H):
    for (ch, n) in H.get("pre_waits4", []):
        for eng in ENGS:
            s.wait(eng, ch, n)
    mLD = s.ch("4LD")
    mLDW = s.ch("4LDW")
    mLDX = s.ch("4LDX")
    mLDR = s.ch("4LDR")
    mV = s.ch("4V")
    mA = s.ch("4A")
    mT = s.ch("4T")
    mST = s.ch("4ST")
    mGD = s.ch("4GD")
    mCC = s.ch("4CC")
    NT = TPC // 128

    p.push_scope()
    ln3T = p.sbuf("p4_ln3T", [128, 16 * TPC], BF16, side="right")
    p.push_scope()
    wob = p.sbuf("p4_wob", [128, 16 * 2048], BF16)
    wldf4 = p.sbuf("p4_wldf", [128, 2048], F32)
    y2t = p.sbuf("p4_y2t", [128, 16 * 128], BF16)
    y2T = p.sbuf("p4_y2T", [128, 16 * 128], BF16)
    a4 = p.sbuf("p4_a4", [128, D], F32)
    b4 = p.sbuf("p4_b4", [128, D], F32)
    xr4 = p.sbuf("p4_xr4", [128, D], F32)
    st4 = p.sbuf("p4_st4", [128, 8], F32)
    sq4 = p.sbuf("p4_sq4", [128, D], F32)
    ep4 = p.sbuf("p4_ep4", [128, 1], F32)
    lb4 = p.sbuf("p4_lb4", [128, D], BF16)
    si4 = p.sbuf("p4_si4", [128, 128], BF16)
    glnsl = p.sbuf("p4_glnsl", [128, 512], BF16)
    pq = [p.psum(f"p4_pq{i}", [128, 512], F32) for i in range(6)]
    pt4 = [p.psum(f"p4_pt{i}", [128, 128], BF16) for i in range(2)]

    s.op("sync", lambda e: e.dma_start(out=si4[:], in_=H["identb"]), mLDW, 16)
    s.op("vector", lambda e: e.memset(ep4[:], EPS_LN), mV)
    for k4 in range(16):
        s.wait("sync", mV, mV.n)
        s.op("sync", lambda e, k4=k4: e.dma_start(
            out=wldf4[:], in_=H["wo_ag"][128 * k4:128 * (k4 + 1), :]), mLDW, 16)
        s.wait("vector", mLDW, mLDW.n)
        s.op("vector", lambda e, k4=k4: e.tensor_copy(
            wob[:, 2048 * k4:2048 * (k4 + 1)], wldf4[:]), mV)
    for k4 in range(16):
        s.wait("sync", mV, mV.n)
        s.op("sync", lambda e, k4=k4: e.dma_start(
            out=wldf4[:], in_=H["core_ag"][128 * k4:128 * (k4 + 1), :]), mLDW, 16)
        s.wait("vector", mLDW, mLDW.n)
        s.op("vector", lambda e: e.tensor_copy(b4[:].bitcast(BF16)[:, 0:2048], wldf4[:]), mV)
        s.wait("gpsimd", mV, mV.n)
        s.op("gpsimd", lambda e, k4=k4: e.dma_start(
            out=H["coreb_dram"][128 * k4:128 * (k4 + 1), :],
            in_=b4[:].bitcast(BF16)[:, 0:2048]), mGD, 16)
        s.wait("vector", mGD, mGD.n)
    wcast4 = mV.n

    y2v = H["y2_loc"].rearrange("(A s b j) d -> A s b j d", A=8, s=4, b=2)
    woT4 = 0
    evV4 = wcast4
    lnslab4 = 0
    for i4 in range(NT):
        s.wait("sync", mT, woT4)
        Ab = 2 * (i4 % 4)
        s.op("sync", lambda e, i4=i4, Ab=Ab: e.dma_start(
            out=y2t[:], in_=y2v[Ab:Ab + 2, i4 // 4, :, :, :]), mLD, 16)
        s.wait("tensor", mLD, mLD.n)
        s.wait("tensor", mV, evV4)
        for k4 in range(16):
            if k4 >= 2:
                s.wait("tensor", mV, y2evV)
            s.op("tensor", lambda e, k4=k4: e.transpose(
                pt4[k4 % 2][:], y2t[:, 128 * k4:128 * (k4 + 1)], si4[:]), mT)
            s.wait("vector", mT, mT.n)
            s.op("vector", lambda e, k4=k4: e.tensor_copy(
                y2T[:, 128 * k4:128 * (k4 + 1)], pt4[k4 % 2][:]), mV)
            y2evV = mV.n
        s.wait("tensor", mV, y2evV)
        for db4 in range(4):
            for k4 in range(16):
                s.op("tensor", lambda e, db4=db4, k4=k4: e.matmul(
                    pq[db4][:], y2T[:, 128 * k4:128 * (k4 + 1)],
                    wob[:, 2048 * k4 + 512 * db4: 2048 * k4 + 512 * (db4 + 1)],
                    start=(k4 == 0), stop=(k4 == 15)), mT)
        woT4 = mT.n
        s.wait("vector", mT, woT4)
        for db4 in range(4):
            s.op("vector", lambda e, db4=db4: e.tensor_copy(
                a4[:, 512 * db4:512 * (db4 + 1)], pq[db4][:]), mV)
        s.wait("vector", mV, mV.n)
        s.wait("sync", mV, mV.n)
        s.op("sync", lambda e, i4=i4: e.dma_start(
            out=xr4[:], in_=H["x2_dram"][128 * i4:128 * (i4 + 1), :]), mLDX, 16)
        emit_ln_rows(s, mV, mA, a4[:], b4[:], st4, sq4, ep4[:])
        s.wait("vector", mV, mV.n)
        s.wait("vector", mLDX, mLDX.n)
        s.op("vector", lambda e: e.tensor_add(a4[:], b4[:], xr4[:]), mV)
        s.wait("vector", mV, mV.n)
        s.wait("gpsimd", mV, mV.n)
        s.wait("gpsimd", mST, mST.n)
        s.op("gpsimd", lambda e, i4=i4: e.dma_start(
            out=H["x3_dram"][128 * i4:128 * (i4 + 1), :], in_=a4[:]), mST, 16)
        emit_ln_rows(s, mV, mA, a4[:], lb4[:], st4, sq4, ep4[:])
        ln3V = mV.n
        s.wait("vector", mST, mST.n)
        s.wait("tensor", mV, ln3V)
        for k4 in range(16):
            if k4 >= 2:
                s.wait("tensor", mV, lnevV)
            s.op("tensor", lambda e, k4=k4: e.transpose(
                pt4[k4 % 2][:], lb4[:, 128 * k4:128 * (k4 + 1)], si4[:]), mT)
            s.wait("vector", mT, mT.n)
            s.op("vector", lambda e, k4=k4, i4=i4: e.tensor_copy(
                ln3T[:, TPC * k4 + 128 * i4: TPC * k4 + 128 * (i4 + 1)],
                pt4[k4 % 2][:]), mV)
            lnevV = mV.n
        evV4 = mV.n
    s.wait("scalar", mV, mV.n)
    for k4 in range(16):
        for tb4 in range(4):
            s.wait("scalar", mGD, mGD.n)
            s.op("scalar", lambda e, k4=k4, tb4=tb4: e.activation(
                glnsl[:], ln3T[:, TPC * k4 + 512 * tb4: TPC * k4 + 512 * (tb4 + 1)],
                AF.Gelu), mA)
            s.wait("gpsimd", mA, mA.n)
            s.op("gpsimd", lambda e, k4=k4, tb4=tb4: e.dma_start(
                out=H["gln_dram"][128 * k4:128 * (k4 + 1), 512 * tb4:512 * (tb4 + 1)],
                in_=glnsl[:]), mGD, 16)
    glnA = mA.n
    s.barrier("B4A", pre_dma_waits=[("gpsimd", mST, mST.n), ("gpsimd", mGD, mGD.n)])
    p.pop_scope()

    p.push_scope()
    dwb = p.sbuf("p4_dwb", [128, 16 * 1024], BF16)
    dnb = p.sbuf("p4_dnb", [128, 8], F32)
    rwb = p.sbuf("p4_rwb", [128, 16 * 8], BF16)
    btmpB = p.sbuf("p4_btmpB", [20, 128], F32)
    sifB = p.sbuf("p4_sifB", [128, 128], F32)
    rbrow = p.sbuf("p4_rbrow", [1, 8], F32)
    rbrow_b = p.sbuf("p4_rbrowb", [1, 8], BF16)
    rcost = p.sbuf("p4_rcost", [1, 8], F32)
    one1b4 = p.sbuf("p4_one1b", [1, 128], BF16)
    ghT = p.sbuf("p4_ghT", [128, 8 * TPC], BF16, side="right")
    wl4 = p.sbuf("p4_wl4", [128, 1024], F32)
    rt = p.sbuf("p4_rt", [128, 8], F32)
    rp = p.sbuf("p4_rp", [128, 8], F32)
    rs1 = p.sbuf("p4_rs1", [128, 8], F32)
    rs2 = p.sbuf("p4_rs2", [128, 16], F32)
    w9b = p.sbuf("p4_w9b", [128, 16], F32)
    pq2 = [p.psum(f"p4_q2{i}", [128, 512], F32) for i in range(4)]

    s.op("vector", lambda e: e.memset(one1b4[:], 1.0), mV)
    s.op("sync", lambda e: e.dma_start(out=rcost[:], in_=H["costrow"]), mLDW, 16)
    s.op("sync", lambda e: e.dma_start(out=rbrow[:], in_=H["router_b"]), mLDW, 16)
    s.op("sync", lambda e: e.dma_start(
        out=btmpB[0:8, :], in_=H["down_b"].rearrange("(c p) -> c p", p=128)), mLDW, 16)
    s.op("sync", lambda e: e.dma_start(out=sifB[:], in_=H["identf"]), mLDW, 16)
    for k4 in range(16):
        s.wait("sync", mV, mV.n)
        s.op("sync", lambda e, k4=k4: e.dma_start(
            out=wl4[:], in_=H["down_ag"][128 * k4:128 * (k4 + 1), :]), mLDW, 16)
        s.wait("vector", mLDW, mLDW.n)
        s.op("vector", lambda e, k4=k4: e.tensor_copy(
            dwb[:, 1024 * k4:1024 * (k4 + 1)], wl4[:]), mV)
        s.wait("sync", mV, mV.n)
        s.op("sync", lambda e, k4=k4: e.dma_start(
            out=wl4[:, 0:8], in_=H["router_w"][128 * k4:128 * (k4 + 1), :]), mLDW, 16)
        s.wait("vector", mLDW, mLDW.n)
        s.op("vector", lambda e, k4=k4: e.tensor_copy(
            rwb[:, 8 * k4:8 * (k4 + 1)], wl4[:, 0:8]), mV)
    s.wait("tensor", mLDW, mLDW.n)
    s.op("tensor", lambda e: e.transpose(pq2[3][:, 0:8], btmpB[0:8, :], sifB[0:8, 0:8]), mT)
    s.wait("vector", mT, mT.n)
    s.op("vector", lambda e: e.tensor_copy(dnb[:], pq2[3][:, 0:8]), mV)
    s.wait("vector", mLDW, mLDW.n)
    s.op("vector", lambda e: e.tensor_sub(rbrow[:], rbrow[:], rcost[:]), mV)
    s.wait("vector", mV, mV.n)
    s.op("vector", lambda e: e.tensor_copy(rbrow_b[:], rbrow[:]), mV)
    wcastB = mV.n
    ghT_T = 0
    gh_hist = {}
    ghit = 0
    for hc in range(8):
        for tb4 in range(4):
            bank = pq2[tb4 % 2]
            s.wait("tensor", mV, wcastB)
            s.wait("tensor", mA, gh_hist.get(ghit - 2, 0))
            for k4 in range(16):
                s.op("tensor", lambda e, hc=hc, tb4=tb4, k4=k4, bank=bank: e.matmul(
                    bank[:], dwb[:, 1024 * k4 + 128 * hc: 1024 * k4 + 128 * (hc + 1)],
                    ln3T[:, TPC * k4 + 512 * tb4: TPC * k4 + 512 * (tb4 + 1)],
                    start=(k4 == 0), stop=(k4 == 15)), mT)
            s.wait("scalar", mT, mT.n)
            s.op("scalar", lambda e, hc=hc, tb4=tb4, bank=bank: e.activation(
                ghT[:, TPC * hc + 512 * tb4: TPC * hc + 512 * (tb4 + 1)],
                bank[:], AF.Gelu, bias=dnb[:, hc:hc + 1]), mA)
            gh_hist[ghit] = mA.n
            ghit += 1
    ghA = mA.n
    s.wait("gpsimd", mA, ghA)
    for hc in range(8):
        s.op("gpsimd", lambda e, hc=hc: e.dma_start(
            out=H["gh_bnc"][128 * hc:128 * (hc + 1), :],
            in_=ghT[:, TPC * hc:TPC * (hc + 1)]), mGD, 16)
    s.wait("gpsimd", mGD, mGD.n)
    s.op("gpsimd", lambda e: e.collective_compute(
        "AllGather", ALU.bypass, replica_groups=RG8,
        ins=[H["gh_bnc"][:, :].opt()], outs=[H["gh_ag"][:, :].opt()]), mCC, 1)
    w9V = 0
    for i4 in range(NT):
        bank = pq2[2]
        s.wait("tensor", mV, w9V)
        for k4 in range(16):
            s.op("tensor", lambda e, i4=i4, k4=k4, bank=bank: e.matmul(
                bank[:, 0:8], ln3T[:, TPC * k4 + 128 * i4: TPC * k4 + 128 * (i4 + 1)],
                rwb[:, 8 * k4:8 * (k4 + 1)], start=(k4 == 0), stop=False), mT)
        s.op("tensor", lambda e, bank=bank: e.matmul(
            bank[:, 0:8], one1b4[:], rbrow_b[:], start=False, stop=True), mT)
        rT = mT.n
        s.wait("vector", mT, rT)
        s.op("vector", lambda e, bank=bank: e.tensor_copy(rt[:], bank[:, 0:8]), mV)
        s.wait("vector", mV, mV.n)
        s.op("vector", lambda e: e.reduce_max(rs1[:, 0:1], rt[:], axis=AX), mV)
        s.wait("vector", mV, mV.n)
        s.op("vector", lambda e: e.tensor_scalar(rs1[:, 1:2], rs1[:, 0:1], -1.0, None, ALU.mult), mV)
        s.wait("scalar", mV, mV.n)
        s.op("scalar", lambda e: e.activation(rp[:], rt[:], AF.Exp, bias=rs1[:, 1:2],
                                              accum_out=rs1[:, 2:3]), mA)
        s.wait("vector", mA, mA.n)
        s.op("vector", lambda e: e.reciprocal(rs1[:, 3:4], rs1[:, 2:3]), mV)
        s.wait("vector", mV, mV.n)
        s.op("vector", lambda e: e.tensor_scalar(rp[:], rp[:], rs1[:, 3:4], None, ALU.mult), mV)
        s.wait("vector", mV, mV.n)
        s.op("vector", lambda e: e.reduce_max(rs1[:, 4:5], rp[:], axis=AX), mV)
        s.wait("vector", mV, mV.n)
        s.op("vector", lambda e: e.tensor_scalar(rs2[:, 0:8], rp[:], rs1[:, 4:5], None, ALU.is_ge), mV)
        s.wait("vector", mV, mV.n)
        s.op("vector", lambda e: e.tensor_scalar(rs2[:, 8:16], rs2[:, 0:8], -1.0, 1.0, ALU.mult, ALU.add), mV)
        s.wait("vector", mV, mV.n)
        s.op("vector", lambda e: e.tensor_mul(rt[:], rp[:], rs2[:, 8:16]), mV)
        s.wait("vector", mV, mV.n)
        s.op("vector", lambda e: e.reduce_max(rs1[:, 5:6], rt[:], axis=AX), mV)
        s.wait("vector", mV, mV.n)
        s.op("vector", lambda e: e.tensor_scalar(rs2[:, 8:16], rt[:], rs1[:, 5:6], None, ALU.is_ge), mV)
        s.wait("vector", mV, mV.n)
        s.op("vector", lambda e: e.tensor_sub(rs1[:, 6:7], rs1[:, 5:6], rs1[:, 4:5]), mV)
        s.wait("scalar", mV, mV.n)
        s.op("scalar", lambda e: e.activation(rs1[:, 7:8], rs1[:, 6:7], AF.Exp), mA)
        s.wait("vector", mA, mA.n)
        s.op("vector", lambda e: e.tensor_scalar(rs1[:, 6:7], rs1[:, 7:8], 1.0, None, ALU.add), mV)
        s.wait("vector", mV, mV.n)
        s.op("vector", lambda e: e.reciprocal(rs1[:, 6:7], rs1[:, 6:7]), mV)
        s.wait("vector", mV, mV.n)
        s.op("vector", lambda e: e.tensor_mul(rs1[:, 7:8], rs1[:, 7:8], rs1[:, 6:7]), mV)
        s.wait("vector", mV, mV.n)
        s.op("vector", lambda e: e.tensor_scalar(w9b[:, 0:8], rs2[:, 0:8], rs1[:, 6:7], None, ALU.mult), mV)
        s.op("vector", lambda e: e.memset(w9b[:, 9:16], 0.0), mV)
        s.wait("vector", mV, mV.n)
        s.op("vector", lambda e: e.tensor_scalar(rs2[:, 8:16], rs2[:, 8:16], rs1[:, 7:8], None, ALU.mult), mV)
        s.wait("vector", mV, mV.n)
        s.op("vector", lambda e: e.tensor_add(w9b[:, 0:8], w9b[:, 0:8], rs2[:, 8:16]), mV)
        s.wait("vector", mV, mV.n)
        s.op("vector", lambda e: e.memset(w9b[:, 8:9], 1.0), mV)
        s.wait("gpsimd", mV, mV.n)
        s.op("gpsimd", lambda e, i4=i4: e.dma_start(
            out=H["w9_bnc"][128 * i4:128 * (i4 + 1), :], in_=w9b[:]), mGD, 16)
        s.wait("vector", mGD, mGD.n)
        w9V = mV.n
    s.wait("gpsimd", mGD, mGD.n)
    s.op("gpsimd", lambda e: e.collective_compute(
        "AllGather", ALU.bypass, replica_groups=RG8,
        ins=[H["w9_bnc"][:, :].opt()], outs=[H["w9_ag"][:, :].opt()]), mCC, 1)
    s.barrier("B4B", pre_dma_waits=[("gpsimd", mGD, mGD.n)])
    p.pop_scope()
    p.pop_scope()

    p.push_scope()
    ew1b = p.sbuf("p4_ew1b", [128, 8 * 2560], BF16)
    ew2b = p.sbuf("p4_ew2b", [128, 20 * 1024], BF16)
    sel8b = p.sbuf("p4_sel8b", [8, 2560], BF16)
    eb1c = p.sbuf("p4_eb1c", [128, 20], F32)
    geb1 = p.sbuf("p4_geb1", [128, 20], F32)
    wle = p.sbuf("p4_wle", [128, 2560], F32)
    ew2s = p.sbuf("p4_ew2s", [128, 1024], BF16)
    v8f = p.sbuf("p4_v8f", [8, 1024], F32)
    v8nb = p.sbuf("p4_v8nb", [8, 1024], BF16)
    sumr = p.sbuf("p4_sumr", [1, 1024], F32)
    sumrb = p.sbuf("p4_sumrb", [1, 1024], BF16)
    ones8 = p.sbuf("p4_ones8", [8, 1], BF16)
    b9 = p.sbuf("p4_b9", [9, 1024], BF16)
    ghblk = p.sbuf("p4_ghblk", [128, 8 * 512], BF16)
    w9l = p.sbuf("p4_w9l", [128, 16], F32)
    w9lb = p.sbuf("p4_w9lb", [128, 16], BF16)
    w9T = p.sbuf("p4_w9T", [16, 4 * 128], BF16)
    mtile = p.sbuf("p4_mtile", [128, 512], BF16)
    mt_blk = p.sbuf("p4_mtblk", [128, 20 * 512], BF16)
    rsrow = p.sbuf("p4_rsrow", [128, 1024], F32)
    pm4 = [p.psum(f"p4_pm{i}", [128, 512], F32) for i in range(2)]
    pw4 = p.psum("p4_pw4", [128, 512], F32)
    pp4 = [p.psum(f"p4_pp{i}", [128, 512], F32) for i in range(4)]
    ptT4 = p.psum("p4_ptT4", [16, 128], BF16)

    s.op("vector", lambda e: e.memset(ones8[:], 1.0), mV)
    s.op("sync", lambda e: e.dma_start(out=si4c[:], in_=H["identb"]), mLDW, 16)
    for k4 in range(8):
        s.wait("sync", mV, mV.n)
        s.op("sync", lambda e, k4=k4: e.dma_start(
            out=wle[:], in_=H["ew1_s"][128 * k4:128 * (k4 + 1), :]), mLDW, 16)
        s.wait("vector", mLDW, mLDW.n)
        s.op("vector", lambda e, k4=k4: e.tensor_copy(
            ew1b[:, 2560 * k4:2560 * (k4 + 1)], wle[:]), mV)
    for g4 in range(20):
        s.wait("sync", mV, mV.n)
        s.op("sync", lambda e, g4=g4: e.dma_start(
            out=wle[:, 0:1024], in_=H["ew2_s"][128 * g4:128 * (g4 + 1), :]), mLDW, 16)
        s.wait("vector", mLDW, mLDW.n)
        s.op("vector", lambda e, g4=g4: e.tensor_copy(
            ew2b[:, 1024 * g4:1024 * (g4 + 1)], wle[:, 0:1024]), mV)
    s.wait("sync", mV, mV.n)
    s.op("sync", lambda e: e.dma_start(out=wle[0:8, :], in_=H["sel8_s"]), mLDW, 16)
    s.wait("vector", mLDW, mLDW.n)
    s.op("vector", lambda e: e.tensor_copy(sel8b[:], wle[0:8, :]), mV)
    s.wait("sync", mV, mV.n)
    s.op("sync", lambda e: e.dma_start(
        out=btmpC[:], in_=H["eb1_s"].rearrange("(c p) -> c p", p=128)), mLDW, 16)
    s.op("sync", lambda e: e.dma_start(out=sifC[:], in_=H["identf"]), mLDW, 16)
    s.wait("tensor", mLDW, mLDW.n)
    s.op("tensor", lambda e: e.transpose(pw4[:, 0:20], btmpC[:], sifC[0:20, 0:20]), mT)
    s.wait("vector", mT, mT.n)
    s.op("vector", lambda e: e.tensor_copy(eb1c[:], pw4[:, 0:20]), mV)
    s.wait("scalar", mV, mV.n)
    s.op("scalar", lambda e: e.activation(geb1[:], eb1c[:], AF.Gelu), mA)
    ecastV = mV.n
    selTb = p.sbuf("p4_selTb", [128, 20 * 8], BF16)
    si4c = p.sbuf("p4_si4c", [128, 128], BF16)
    btmpC = p.sbuf("p4_btmpC", [20, 128], F32)
    sifC = p.sbuf("p4_sifC", [128, 128], F32)
    v8pb = p.sbuf("p4_v8pb", [8, 1024], BF16)
    for g4 in range(20):
        s.wait("sync", mV, mV.n)
        s.op("sync", lambda e, g4=g4: e.dma_start(
            out=wle[:, 0:8], in_=H["selT_s"][128 * g4:128 * (g4 + 1), :]), mLDW, 16)
        s.wait("vector", mLDW, mLDW.n)
        s.op("vector", lambda e, g4=g4: e.tensor_copy(
            selTb[:, 8 * g4:8 * (g4 + 1)], wle[:, 0:8]), mV)
    s.wait("vector", mA, mA.n)
    for g4 in range(20):
        s.wait("vector", mT, mT.n)
        s.op("vector", lambda e, g4=g4: e.tensor_scalar(
            ew2s[:], ew2b[:, 1024 * g4:1024 * (g4 + 1)], geb1[:, g4:g4 + 1],
            None, ALU.mult), mV)
        s.wait("tensor", mV, mV.n)
        for dh4 in range(2):
            s.op("tensor", lambda e, g4=g4, dh4=dh4: e.matmul(
                pp4[2 + dh4][0:8, :], selTb[:, 8 * g4:8 * (g4 + 1)],
                ew2s[:, 512 * dh4:512 * (dh4 + 1)],
                start=(g4 == 0), stop=(g4 == 19)), mT)
    v8T = mT.n
    s.wait("vector", mT, v8T)
    for dh4 in range(2):
        s.op("vector", lambda e, dh4=dh4: e.tensor_copy(
            v8f[:, 512 * dh4:512 * (dh4 + 1)], pp4[2 + dh4][0:8, :]), mV)
    s.wait("vector", mV, mV.n)
    s.op("vector", lambda e: e.tensor_scalar(v8nb[:], v8f[:], -1.0, None, ALU.mult), mV)
    s.wait("vector", mV, mV.n)
    s.op("vector", lambda e: e.tensor_copy(v8pb[:], v8f[:]), mV)
    posV = mV.n
    s.wait("tensor", mV, posV)
    for dh4 in range(2):
        s.wait("tensor", mV, mV.n)
        s.op("tensor", lambda e, dh4=dh4: e.matmul(
            pw4[0:1, :], ones8[:], v8pb[:, 512 * dh4:512 * (dh4 + 1)],
            start=True, stop=True), mT)
        s.wait("vector", mT, mT.n)
        s.op("vector", lambda e, dh4=dh4: e.tensor_copy(
            sumr[:, 512 * dh4:512 * (dh4 + 1)], pw4[0:1, :]), mV)
    s.wait("vector", mV, mV.n)
    s.op("vector", lambda e: e.tensor_copy(sumrb[:], sumr[:]), mV)
    s.wait("gpsimd", mV, mV.n)
    s.op("gpsimd", lambda e: e.dma_start(out=b9[0:8, :], in_=v8nb[:]), mGD, 16)
    s.op("gpsimd", lambda e: e.dma_start(out=b9[8:9, :], in_=sumrb[:]), mGD, 16)
    s.wait("tensor", mGD, mGD.n)
    s.wait("vector", mGD, mGD.n)

    s.wait("sync", mCC, 2)
    s.wait("gpsimd", mCC, 2)
    w9v4 = H["w9_ag"].rearrange("(r t) c -> r t c", r=8)
    ghv4 = H["gh_ag"].rearrange("(r h) t -> r h t", r=8)
    pDone = 0
    mtDone = mV.n
    for tb4 in range(32):
        rr4, tt04 = tb4 // 4, (tb4 % 4) * 512
        s.wait("sync", mT, pDone)
        for k4 in range(8):
            s.op("sync", lambda e, k4=k4, rr4=rr4, tt04=tt04: e.dma_start(
                out=ghblk[:, 512 * k4:512 * (k4 + 1)],
                in_=ghv4[rr4, 128 * k4:128 * (k4 + 1), tt04:tt04 + 512]), mLD, 16)
        for ts4 in range(4):
            s.op("sync", lambda e, ts4=ts4, rr4=rr4, tt04=tt04: e.dma_start(
                out=w9l[:],
                in_=w9v4[rr4, tt04 + 128 * ts4: tt04 + 128 * (ts4 + 1), :]), mLD, 16)
            s.wait("vector", mLD, mLD.n)
            s.op("vector", lambda e, ts4=ts4: e.tensor_copy(w9lb[:], w9l[:]), mV)
            s.wait("tensor", mV, mV.n)
            s.op("tensor", lambda e, ts4=ts4: e.transpose(ptT4[:], w9lb[:], si4c[:]), mT)
            s.wait("vector", mT, mT.n)
            s.op("vector", lambda e, ts4=ts4: e.tensor_copy(
                w9T[:, 128 * ts4:128 * (ts4 + 1)], ptT4[:]), mV)
            s.wait("sync", mV, mV.n)
        ghdone = mLD.n
        w9ready = mV.n
        for g4 in range(20):
            bank = pm4[g4 % 2]
            s.wait("tensor", mLD, ghdone)
            s.wait("tensor", mV, mtDone)
            s.wait("tensor", mA, mA.n)
            for k4 in range(8):
                s.op("tensor", lambda e, g4=g4, k4=k4, bank=bank: e.matmul(
                    bank[:], ew1b[:, 2560 * k4 + 128 * g4: 2560 * k4 + 128 * (g4 + 1)],
                    ghblk[:, 512 * k4:512 * (k4 + 1)],
                    start=(k4 == 0), stop=(k4 == 7)), mT)
            s.wait("scalar", mT, mT.n)
            s.wait("scalar", mV, mV.n)
            s.op("scalar", lambda e, g4=g4, bank=bank: e.activation(
                mtile[:], bank[:], AF.Gelu, bias=eb1c[:, g4:g4 + 1]), mA)
            s.wait("tensor", mV, mV.n)
            s.op("tensor", lambda e, g4=g4: e.matmul(
                pw4[:], sel8b[:, 128 * g4:128 * (g4 + 1)], w9T[0:8, :],
                start=True, stop=True), mT)
            s.wait("vector", mT, mT.n)
            s.wait("vector", mA, mA.n)
            s.op("vector", lambda e, g4=g4: e.tensor_tensor(
                mt_blk[:, 512 * g4:512 * (g4 + 1)], mtile[:], pw4[:], ALU.mult), mV)
        mtDone = mV.n
        s.wait("tensor", mV, mtDone)
        for dh4 in range(2):
            s.wait("tensor", mV, mV.n)
            for ts4 in range(4):
                s.op("tensor", lambda e, ts4=ts4, dh4=dh4: e.matmul(
                    pp4[ts4][:], w9T[0:9, 128 * ts4:128 * (ts4 + 1)],
                    b9[:, 512 * dh4:512 * (dh4 + 1)], start=True, stop=False), mT)
                for g4 in range(20):
                    s.op("tensor", lambda e, ts4=ts4, dh4=dh4, g4=g4: e.matmul(
                        pp4[ts4][:],
                        mt_blk[:, 512 * g4 + 128 * ts4: 512 * g4 + 128 * (ts4 + 1)],
                        ew2b[:, 1024 * g4 + 512 * dh4: 1024 * g4 + 512 * (dh4 + 1)],
                        start=False, stop=(g4 == 19)), mT)
            pT4 = mT.n
            s.wait("vector", mT, pT4)
            s.wait("vector", mST, mST.n)
            for ts4 in range(4):
                s.op("vector", lambda e, ts4=ts4, dh4=dh4: e.tensor_copy(
                    rsrow[:, 512 * dh4:512 * (dh4 + 1)], pp4[ts4][:]), mV)
                s.wait("gpsimd", mV, mV.n)
                s.op("gpsimd", lambda e, ts4=ts4, dh4=dh4, tb4=tb4: e.dma_start(
                    out=H["rs_in"][512 * tb4 + 128 * ts4: 512 * tb4 + 128 * (ts4 + 1),
                                   512 * dh4:512 * (dh4 + 1)],
                    in_=rsrow[:, 512 * dh4:512 * (dh4 + 1)]), mST, 16)
                s.wait("vector", mST, mST.n)
        pDone = mT.n
    s.wait("gpsimd", mST, mST.n)
    s.op("gpsimd", lambda e: e.collective_compute(
        "ReduceScatter", ALU.add, replica_groups=RG8,
        ins=[H["rs_in"][:, :].opt()], outs=[H["rs_out"][:, :].opt()]), mCC, 1)
    s.barrier("B4C", pre_dma_waits=[("gpsimd", mST, mST.n), ("gpsimd", mGD, mGD.n)])
    p.pop_scope()

    p.push_scope()
    sh1b4 = p.sbuf("p4_sh1b", [128, 8 * 1024], BF16)
    sh2b4 = p.sbuf("p4_sh2b", [128, 8 * 1024], BF16)
    upb4 = p.sbuf("p4_upb", [128, 8 * 2048], BF16)
    corsl0 = p.sbuf("p4_corsl0", [128, 16 * 512], BF16)
    corsl = [corsl0, corsl0]
    glsl = p.sbuf("p4_glsl", [128, 16 * 128], BF16)
    rsb = p.sbuf("p4_rsb", [128, 1024], BF16)
    wl5 = p.sbuf("p4_wl5", [128, 2048], F32)
    ghlo = p.sbuf("p4_ghlo", [128, 8 * 512], BF16)
    msb = p.sbuf("p4_msb", [128, 8 * 512], BF16)
    psT = p.sbuf("p4_psT", [128, 8 * 512], BF16)
    rsT = p.sbuf("p4_rsT", [128, 8 * 512], BF16)
    accT = p.sbuf("p4_accT", [128, 8 * 512], BF16)
    s1c = p.sbuf("p4_s1c", [128, 8], F32)
    s2c = p.sbuf("p4_s2c", [128, 8], F32)
    ubrow = p.sbuf("p4_ubrow", [1, 2048], F32)
    cbrow = p.sbuf("p4_cbrow", [1, 2048], F32)
    eb2b = p.sbuf("p4_eb2b", [8, 1024], BF16)
    ones8e = p.sbuf("p4_ones8e", [8, 1], BF16)
    sb2c = p.sbuf("p4_sb2c", [128, 8], F32)
    bsum = p.sbuf("p4_bsum", [1, 2048], BF16)
    o5 = p.sbuf("p4_o5", [128, 2048], F32)
    l5 = p.sbuf("p4_l5", [128, 2048], F32)
    x5 = p.sbuf("p4_x5", [128, 2048], F32)
    st5 = p.sbuf("p4_st5", [128, 8], F32)
    sq5 = p.sbuf("p4_sq5", [128, 2048], F32)
    ep5 = p.sbuf("p4_ep5", [128, 1], F32)
    one5 = p.sbuf("p4_one5", [1, 128], BF16)
    si5 = p.sbuf("p4_si5", [128, 128], BF16)
    btmpE = p.sbuf("p4_btmpE", [16, 128], F32)
    sifE = p.sbuf("p4_sifE", [128, 128], F32)
    xr4e = p.sbuf("p4_xr4e", [128, 2048], F32)
    pe5 = [p.psum(f"p4_pe5{i}", [128, 512], F32) for i in range(6)]
    pt5 = [p.psum(f"p4_pt5{i}", [128, 128], BF16) for i in range(2)]

    rsl = wl5[:, 0:1024]
    s.op("vector", lambda e: e.memset(ep5[:], EPS_LN), mV)
    s.op("vector", lambda e: e.memset(one5[:], 1.0), mV)
    s.op("sync", lambda e: e.dma_start(out=si5[:], in_=H["identb"]), mLDW, 16)
    s.op("sync", lambda e: e.dma_start(out=ubrow[:], in_=H["up_b"]), mLDW, 16)
    s.op("sync", lambda e: e.dma_start(out=cbrow[:], in_=H["core_b"]), mLDW, 16)
    s.op("sync", lambda e: e.dma_start(
        out=btmpE[0:8, :], in_=H["sh1_b"].rearrange("(c p) -> c p", p=128)), mLDW, 16)
    s.op("sync", lambda e: e.dma_start(
        out=btmpE[8:16, :], in_=H["sh2_b"].rearrange("(c p) -> c p", p=128)), mLDW, 16)
    s.op("sync", lambda e: e.dma_start(out=sifE[:], in_=H["identf"]), mLDW, 16)
    s.wait("tensor", mLDW, mLDW.n)
    s.op("tensor", lambda e: e.transpose(pe5[4][:, 0:16], btmpE[:], sifE[0:16, 0:16]), mT)
    s.wait("vector", mT, mT.n)
    s.op("vector", lambda e: e.tensor_copy(s1c[:], pe5[4][:, 0:8]), mV)
    s.op("vector", lambda e: e.tensor_copy(s2c[:], pe5[4][:, 8:16]), mV)
    s.op("vector", lambda e: e.memset(ones8e[:], 1.0), mV)
    s.wait("sync", mV, mV.n)
    s.op("sync", lambda e: e.dma_start(out=wl5[0:8, 0:1024], in_=H["eb2_stack"]), mLDW, 16)
    s.wait("vector", mLDW, mLDW.n)
    s.op("vector", lambda e: e.tensor_copy(eb2b[:], wl5[0:8, 0:1024]), mV)
    s.wait("tensor", mV, mV.n)
    for mc in range(8):
        s.op("tensor", lambda e, mc=mc: e.matmul(
            pe5[5][:, mc:mc + 1], eb2b[:, 128 * mc:128 * (mc + 1)], ones8e[:],
            start=True, stop=True), mT)
    s.wait("vector", mT, mT.n)
    s.op("vector", lambda e: e.tensor_copy(sb2c[:], pe5[5][:, 0:8]), mV)
    for k4 in range(8):
        s.wait("sync", mV, mV.n)
        s.op("sync", lambda e, k4=k4: e.dma_start(
            out=wl5[:, 0:1024], in_=H["sh1_ag"][128 * k4:128 * (k4 + 1), :]), mLDW, 16)
        s.wait("vector", mLDW, mLDW.n)
        s.op("vector", lambda e, k4=k4: e.tensor_copy(
            sh1b4[:, 1024 * k4:1024 * (k4 + 1)], wl5[:, 0:1024]), mV)
        s.wait("sync", mV, mV.n)
        s.op("sync", lambda e, k4=k4: e.dma_start(
            out=wl5[:, 0:1024], in_=H["sh2_ag"][128 * k4:128 * (k4 + 1), :]), mLDW, 16)
        s.wait("vector", mLDW, mLDW.n)
        s.op("vector", lambda e, k4=k4: e.tensor_copy(
            sh2b4[:, 1024 * k4:1024 * (k4 + 1)], wl5[:, 0:1024]), mV)
        s.wait("sync", mV, mV.n)
        s.op("sync", lambda e, k4=k4: e.dma_start(
            out=wl5[:], in_=H["up_ag"][128 * k4:128 * (k4 + 1), :]), mLDW, 16)
        s.wait("vector", mLDW, mLDW.n)
        s.op("vector", lambda e, k4=k4: e.tensor_copy(
            upb4[:, 2048 * k4:2048 * (k4 + 1)], wl5[:]), mV)
    s.wait("vector", mLDW, mLDW.n)
    s.op("vector", lambda e: e.tensor_add(ubrow[:], ubrow[:], cbrow[:]), mV)
    s.wait("vector", mV, mV.n)
    s.op("vector", lambda e: e.tensor_copy(bsum[:], ubrow[:]), mV)
    s.op("vector", lambda e: e.tensor_scalar(s2c[:], s2c[:], 0.1, None, ALU.mult), mV)
    s.wait("vector", mV, mV.n)
    s.op("vector", lambda e: e.tensor_add(s2c[:], s2c[:], sb2c[:]), mV)
    wcastE = mV.n

    finT = 0
    accV = wcastE
    ghmsT = 0
    s.wait("sync", mCC, 3)
    if "rs_dbg" in H:
        s.wait("gpsimd", mCC, 3)
        s.op("gpsimd", lambda e: e.dma_start(out=H["rs_dbg"], in_=H["rs_out"]), mGD, 16)
    for tb4 in range(4):
        s.wait("sync", mT, ghmsT)
        for k4 in range(8):
            s.op("sync", lambda e, k4=k4, tb4=tb4: e.dma_start(
                out=ghlo[:, 512 * k4:512 * (k4 + 1)],
                in_=H["gh_bnc"][128 * k4:128 * (k4 + 1), 512 * tb4:512 * (tb4 + 1)]), mLD, 16)
        ghloD = mLD.n
        s.wait("tensor", mLD, ghloD)
        s.wait("tensor", mV, accV)
        for oc in range(8):
            bank = pe5[oc % 2]
            s.wait("tensor", mA, mA.n)
            for k4 in range(8):
                s.op("tensor", lambda e, oc=oc, k4=k4, bank=bank: e.matmul(
                    bank[:], sh1b4[:, 1024 * k4 + 128 * oc: 1024 * k4 + 128 * (oc + 1)],
                    ghlo[:, 512 * k4:512 * (k4 + 1)], start=(k4 == 0), stop=(k4 == 7)), mT)
            s.wait("scalar", mT, mT.n)
            s.op("scalar", lambda e, oc=oc, bank=bank: e.activation(
                msb[:, 512 * oc:512 * (oc + 1)], bank[:], AF.Gelu,
                bias=s1c[:, oc:oc + 1]), mA)
        msA = mA.n
        s.wait("tensor", mA, msA)
        for oc in range(8):
            bank = pe5[2 + oc % 2]
            s.wait("tensor", mV, mV.n)
            for k4 in range(8):
                s.op("tensor", lambda e, oc=oc, k4=k4, bank=bank: e.matmul(
                    bank[:], sh2b4[:, 1024 * k4 + 128 * oc: 1024 * k4 + 128 * (oc + 1)],
                    msb[:, 512 * k4:512 * (k4 + 1)], start=(k4 == 0), stop=(k4 == 7)), mT)
            s.wait("vector", mT, mT.n)
            s.op("vector", lambda e, oc=oc, bank=bank: e.tensor_copy(
                psT[:, 512 * oc:512 * (oc + 1)], bank[:]), mV)
        ghmsT = mT.n
        for ts4 in range(4):
            s.wait("sync", mV, mV.n)
            s.op("sync", lambda e, tb4=tb4, ts4=ts4: e.dma_start(
                out=rsl, in_=H["rs_out"][512 * tb4 + 128 * ts4: 512 * tb4 + 128 * (ts4 + 1), :]), mLDR, 16)
            s.wait("vector", mLDR, mLDR.n)
            s.op("vector", lambda e: e.tensor_copy(rsb[:], rsl), mV)
            s.wait("tensor", mV, mV.n)
            for mc in range(8):
                if mc >= 2:
                    s.wait("tensor", mV, mV.n)
                s.op("tensor", lambda e, mc=mc: e.transpose(
                    pt5[mc % 2][:], rsb[:, 128 * mc:128 * (mc + 1)], si5[:]), mT)
                s.wait("vector", mT, mT.n)
                s.op("vector", lambda e, mc=mc, ts4=ts4: e.tensor_copy(
                    rsT[:, 512 * mc + 128 * ts4: 512 * mc + 128 * (ts4 + 1)],
                    pt5[mc % 2][:]), mV)
        s.wait("vector", mV, mV.n)
        for mc in range(8):
            s.op("vector", lambda e, mc=mc: e.scalar_tensor_tensor(
                accT[:, 512 * mc:512 * (mc + 1)], psT[:, 512 * mc:512 * (mc + 1)],
                0.1, rsT[:, 512 * mc:512 * (mc + 1)], ALU.mult, ALU.add), mV)
        s.wait("vector", mV, mV.n)
        for mc in range(8):
            s.op("vector", lambda e, mc=mc: e.tensor_scalar(
                accT[:, 512 * mc:512 * (mc + 1)], accT[:, 512 * mc:512 * (mc + 1)],
                s2c[:, mc:mc + 1], None, ALU.add), mV)
        accV = mV.n
        for ts4 in range(4):
            i4g = 4 * tb4 + ts4
            s.wait("sync", mT, finT)
            s.op("sync", lambda e, i4g=i4g: e.dma_start(
                out=glsl[:].rearrange("p (k t) -> p k t", k=16),
                in_=H["gln_dram"].rearrange("(k p) t -> p k t", p=128)[
                    :, :, 128 * i4g:128 * (i4g + 1)]), mLD, 16)
            s.wait("sync", mV, mV.n)
            s.op("sync", lambda e, i4g=i4g: e.dma_start(
                out=xr4e[:], in_=H["x3_dram"][128 * i4g:128 * (i4g + 1), :]), mLDX, 16)
            glD = mLD.n
            s.wait("tensor", mLD, glD)
            s.wait("tensor", mV, accV)
            for db4 in range(4):
                pp = (tb4 * 4 + ts4) % 2
                s.wait("sync", mT, finT)
                s.op("sync", lambda e, db4=db4, pp=pp: e.dma_start(
                    out=corsl[pp][:].rearrange("p (k d) -> p k d", k=16),
                    in_=H["coreb_dram"].rearrange("(k p) d -> p k d", p=128)[
                        :, :, 512 * db4:512 * (db4 + 1)]), mLDW, 16)
                s.wait("tensor", mLDW, mLDW.n)
                s.op("tensor", lambda e, db4=db4: e.matmul(
                    pe5[db4][:], one5[:], bsum[0:1, 512 * db4:512 * (db4 + 1)],
                    start=True, stop=False), mT)
                for mc in range(8):
                    s.op("tensor", lambda e, mc=mc, ts4=ts4, db4=db4: e.matmul(
                        pe5[db4][:], accT[:, 512 * mc + 128 * ts4: 512 * mc + 128 * (ts4 + 1)],
                        upb4[:, 2048 * mc + 512 * db4: 2048 * mc + 512 * (db4 + 1)],
                        start=False, stop=False), mT)
                for dc in range(16):
                    s.op("tensor", lambda e, dc=dc, db4=db4, pp=pp: e.matmul(
                        pe5[db4][:], glsl[:, 128 * dc:128 * (dc + 1)],
                        corsl[pp][:, 512 * dc:512 * (dc + 1)],
                        start=False, stop=(dc == 15)), mT)
                finT = mT.n
            s.wait("vector", mT, finT)
            for db4 in range(4):
                s.op("vector", lambda e, db4=db4: e.tensor_copy(
                    o5[:, 512 * db4:512 * (db4 + 1)], pe5[db4][:]), mV)
            s.wait("vector", mV, mV.n)
            emit_ln_rows(s, mV, mA, o5[:], l5[:], st5, sq5, ep5[:])
            s.wait("vector", mV, mV.n)
            s.wait("vector", mLDX, mLDX.n)
            s.op("vector", lambda e: e.tensor_add(x5[:], l5[:], xr4e[:]), mV)
            s.wait("vector", mV, mV.n)
            s.wait("gpsimd", mV, mV.n)
            s.wait("gpsimd", mST, mST.n)
            s.op("gpsimd", lambda e, i4g=i4g: e.dma_start(
                out=H["out_rows"][128 * i4g:128 * (i4g + 1), :], in_=x5[:]), mST, 16)
            s.wait("vector", mST, mST.n)
    s.wait("gpsimd", mST, mST.n)
    p.pop_scope()


MOE_HID = [1024, 2048, 3072, 4096, 1024, 2048, 3072, 4096]
MOE_COST = [2 * 1024 * h for h in MOE_HID]


def declare_ph4_drams(nc, weights_external=False, dbg=False):
    H = {}
    dk = "ExternalOutput" if dbg else "Internal"
    H["y2_loc"] = nc.dram_tensor("y2_loc", [TPC, D], BF16,
                                 kind="ExternalInput" if weights_external else "Internal").ap()
    H["x3_dram"] = nc.dram_tensor("x3_dram", [TPC, D], F32, kind=dk).ap()
    H["out_rows"] = nc.dram_tensor("out_rows", [TPC, D], F32, kind="ExternalOutput").ap()
    H["gh_bnc"] = nc.dram_tensor("gh_bnc", [1024, TPC], BF16, kind="Internal").ap()
    H["gh_ag"] = nc.dram_tensor("gh_ag", [8 * 1024, TPC], BF16, kind="Internal",
                                addr_space="Shared").ap()
    H["w9_bnc"] = nc.dram_tensor("w9_bnc", [TPC, 16], F32, kind="Internal").ap()
    H["w9_ag"] = nc.dram_tensor("w9_ag", [8 * TPC, 16], F32, kind="Internal",
                                addr_space="Shared").ap()
    H["rs_in"] = nc.dram_tensor("rs_in", [L, 1024], F32, kind="Internal").ap()
    H["rs_out"] = nc.dram_tensor("rs_out", [TPC, 1024], F32, kind="Internal").ap()
    H["gln_dram"] = nc.dram_tensor("gln_dram", [D, TPC], BF16, kind=dk).ap()
    H["coreb_dram"] = nc.dram_tensor("coreb_dram", [D, D], BF16, kind=dk).ap()
    if dbg:
        H["rs_dbg"] = nc.dram_tensor("rs_dbg", [TPC, 1024], F32, kind="ExternalOutput").ap()
    H["costrow"] = nc.inline_tensor(
        (np.array(MOE_COST, np.float32) * 1e-7)[None, :], "c_cost").ap()
    H["router_w"] = nc.dram_tensor("router_w", [D, 8], F32, kind="ExternalInput").ap()
    H["router_b"] = nc.dram_tensor("router_b", [1, 8], F32, kind="ExternalInput").ap()
    H["down_b"] = nc.dram_tensor("down_b", [1024], F32, kind="ExternalInput").ap()
    H["up_b"] = nc.dram_tensor("up_b", [1, 2048], F32, kind="ExternalInput").ap()
    H["core_b"] = nc.dram_tensor("core_b", [1, 2048], F32, kind="ExternalInput").ap()
    H["sh1_b"] = nc.dram_tensor("sh1_b", [1024], F32, kind="ExternalInput").ap()
    H["sh2_b"] = nc.dram_tensor("sh2_b", [1024], F32, kind="ExternalInput").ap()
    H["ew1_s"] = nc.dram_tensor("ew1_s", [1024, 2560], F32, kind="ExternalInput").ap()
    H["ew2_s"] = nc.dram_tensor("ew2_s", [2560, 1024], F32, kind="ExternalInput").ap()
    H["eb1_s"] = nc.dram_tensor("eb1_s", [2560], F32, kind="ExternalInput").ap()
    H["sel8_s"] = nc.dram_tensor("sel8_s", [8, 2560], F32, kind="ExternalInput").ap()
    H["selT_s"] = nc.dram_tensor("selT_s", [2560, 8], F32, kind="ExternalInput").ap()
    H["eb2_stack"] = nc.dram_tensor("eb2_stack", [8, 1024], F32, kind="ExternalInput").ap()
    if weights_external:
        for nm, shp in (("wo_ag", [D, D]), ("core_ag", [D, D]), ("down_ag", [D, 1024]),
                        ("up_ag", [1024, D]), ("sh1_ag", [1024, 1024]), ("sh2_ag", [1024, 1024])):
            H[nm] = nc.dram_tensor(nm, shp, F32, kind="ExternalInput").ap()
        H["x2_dram"] = nc.dram_tensor("x2_in", [TPC, D], F32, kind="ExternalInput").ap()
    return H


def slice_ph4_inputs(inp, c):
    ew1 = np.concatenate([inp[f"L1_ffn_e{e}_w1"] for e in range(8)], 1)
    eb1 = np.concatenate([inp[f"L1_ffn_e{e}_b1"] for e in range(8)], 0)
    ew2 = np.concatenate([inp[f"L1_ffn_e{e}_w2"] for e in range(8)], 0)
    eid = np.concatenate([np.full(MOE_HID[e], e) for e in range(8)])
    sl = slice(2560 * c, 2560 * (c + 1))
    sel8 = (eid[None, sl] == np.arange(8)[:, None]).astype(np.float32)
    return {
        "ew1_s": np.ascontiguousarray(ew1[:, sl]).astype(np.float32),
        "eb1_s": np.ascontiguousarray(eb1[sl]).astype(np.float32),
        "ew2_s": np.ascontiguousarray(ew2[sl, :]).astype(np.float32),
        "sel8_s": np.ascontiguousarray(sel8),
        "selT_s": np.ascontiguousarray(sel8.T),
        "router_w": inp["L1_ffn_router_w"].astype(np.float32),
        "router_b": inp["L1_ffn_router_b"][None, :].astype(np.float32),
        "down_b": inp["L1_ffn_down_b"].astype(np.float32),
        "up_b": inp["L1_ffn_up_b"][None, :].astype(np.float32),
        "core_b": inp["L1_ffn_core_b"][None, :].astype(np.float32),
        "sh1_b": inp["L1_ffn_shared_b1"].astype(np.float32),
        "sh2_b": inp["L1_ffn_shared_b2"].astype(np.float32),
        "eb2_stack": np.stack([inp[f"L1_ffn_e{e}_b2"] for e in range(8)], 0).astype(np.float32),
    }


def build_ph4_test():
    nc = bass.Bass("TRN2", target_bir_lowering=False)
    es = ExitStack()
    p = Prog(nc, es)
    H = make_consts(nc)
    H.update(declare_ph4_drams(nc, weights_external=True,
                               dbg=_os.environ.get("PH4_DBG", "0") == "1"))
    with es:
        sched = Sched(p, "p4")
        block = es.enter_context(nc.Block())
        emit_ph4(p, sched, H)
        sched.emit(block)
    return nc



def slice_ph1_inputs(inp, c):
    Wqkv = inp["L0_attn_Wqkv"]
    Wa, Wb = inp["L0_attn_Wa"], inp["L0_attn_Wb"]
    qcols = list(range(128 * 2 * c, 128 * (2 * c + 2)))
    kcols = [2048 + i for i in qcols]
    vcols = list(range(4096 + 512 * c, 4096 + 512 * (c + 1)))
    cols = qcols + kcols + vcols
    return {
        "wqkv_s": np.ascontiguousarray(Wqkv[:, cols]),
        "wz_s": np.ascontiguousarray(inp["L0_attn_Wz"][:, 512 * c:512 * (c + 1)]),
        "wab_s": np.ascontiguousarray(np.concatenate(
            [Wa[:, 4 * c:4 * (c + 1)], Wb[:, 4 * c:4 * (c + 1)]], 1)),
        "conv_s": np.ascontiguousarray(inp["L0_attn_conv_w"][cols, :]),
        "alog_s": np.ascontiguousarray(inp["L0_attn_A_log"][4 * c:4 * (c + 1)][None, :]),
        "dtb_s": np.ascontiguousarray(inp["L0_attn_dt_bias"][4 * c:4 * (c + 1)][None, :]),
        "normw_s": np.ascontiguousarray(inp["L0_attn_norm_w"][:, None]),
    }


def slice_ph3_inputs(inp, c):
    return {
        "wq_s": np.ascontiguousarray(inp["L1_attn_Wq"][:, 256 * c:256 * (c + 1)]),
        "wk_s": np.ascontiguousarray(inp["L1_attn_Wk"][:, 256 * c:256 * (c + 1)]),
        "wv_s": np.ascontiguousarray(inp["L1_attn_Wv"][:, 256 * c:256 * (c + 1)]),
        "betas_s": np.ascontiguousarray(inp["L1_attn_betas"][0, 2 * c:2 * (c + 1), 0, :]),
    }


def kernel(**inputs):
    cores = list(range(NCORE))
    in_maps = [slice_fused_inputs(inputs, c) for c in cores]
    r = run_bass_kernel_spmd(build_fused(), in_maps, core_ids=cores).results
    out = np.concatenate([np.asarray(r[c]["out_rows"], np.float32) for c in cores], 0)
    return out[None]


def kernel_multilaunch(**inputs):
    inp = inputs
    x = np.asarray(inp["L0_attn_x"])[0]
    cores = list(range(NCORE))
    rows = lambda a, c: np.ascontiguousarray(a[TPC * c:TPC * (c + 1)])

    in1 = [{"x_rows": rows(x, c)} for c in cores]
    r1 = run_bass_kernel_spmd(build_ph0_test(), in1, core_ids=cores).results
    ln0T_ag = np.concatenate([np.asarray(r1[c]["ln0T_sh"]) for c in cores], 0)

    in2 = [dict(slice_ph1_inputs(inp, c), ln0T_ag=ln0T_ag) for c in cores]
    r2 = run_bass_kernel_spmd(build_ph1_test(dbg=False), in2, core_ids=cores).results
    yT = [np.asarray(r2[c]["yT_sh"]) for c in cores]

    in3 = []
    for c in cores:
        m3 = slice_prologue_inputs(inp, c)
        m3["yT_loc"] = np.concatenate([yT[r][512 * c:512 * (c + 1)] for r in cores], 0)
        m3["x_rows"] = rows(x, c)
        in3.append(m3)
    r3 = run_bass_kernel_spmd(build_ph2_test(lean=True), in3, core_ids=cores).results
    ln2T_ag = np.concatenate([np.asarray(r3[c]["ln2T_sh"]) for c in cores], 0)

    in4 = [dict(slice_ph3_inputs(inp, c), ln2T_ag=ln2T_ag) for c in cores]
    r4 = run_bass_kernel_spmd(build_ph3_test(), in4, core_ids=cores).results
    y2 = [np.asarray(r4[c]["y2_sh"]) for c in cores]

    in5 = []
    for c in cores:
        m5 = slice_ph4_inputs(inp, c)
        m5["y2_loc"] = np.concatenate([y2[r][256 * c:256 * (c + 1)] for r in cores], 0)
        m5["x2_in"] = np.asarray(r3[c]["x2_dram"], np.float32)
        m5["wo_ag"] = np.asarray(inp["L1_attn_Wo"], np.float32)
        m5["core_ag"] = np.asarray(inp["L1_ffn_core_w"], np.float32)
        m5["down_ag"] = np.asarray(inp["L1_ffn_down_w"], np.float32)
        m5["up_ag"] = np.asarray(inp["L1_ffn_up_w"], np.float32)
        m5["sh1_ag"] = np.asarray(inp["L1_ffn_shared_w1"], np.float32)
        m5["sh2_ag"] = np.asarray(inp["L1_ffn_shared_w2"], np.float32)
        in5.append(m5)
    r5 = run_bass_kernel_spmd(build_ph4_test(), in5, core_ids=cores).results

    out = np.concatenate([np.asarray(r5[c]["out_rows"], np.float32) for c in cores], 0)
    return out[None]



def build_fused():
    nc = bass.Bass("TRN2", target_bir_lowering=False)
    es = ExitStack()
    p = Prog(nc, es)
    H = make_consts(nc)
    x_rows = nc.dram_tensor("x_rows", [TPC, D], F32, kind="ExternalInput").ap()
    H["x_rows"] = x_rows
    for nm, shp in (("wqkv_s", [D, 1024]), ("wz_s", [D, 512]), ("wab_s", [D, 8]),
                    ("conv_s", [1024, 4]), ("alog_s", [1, 4]), ("dtb_s", [1, 4]),
                    ("normw_s", [128, 1]), ("wout_s", [512, 2048]),
                    ("wq_s", [D, 256]), ("wk_s", [D, 256]), ("wv_s", [D, 256]),
                    ("betas_s", [2, 128])):
        H[nm] = nc.dram_tensor(nm, shp, F32, kind="ExternalInput").ap()
    H["ysel_s"] = nc.dram_tensor("ysel_s", [TPC, 256], F32, kind="ExternalInput").ap()
    H.update(declare_prologue_drams(nc, need=("wo", "core", "down", "up", "sh1", "sh2",
                                              "w1", "w3", "w2")))
    H.update(declare_ph4_drams(nc, weights_external=False))
    ln0T_sh = nc.dram_tensor("ln0T_sh", [D, TPC], BF16, kind="Internal").ap()
    H["ln0T_ag"] = nc.dram_tensor("ln0T_ag", [NCORE * D, TPC], BF16, kind="Internal",
                                  addr_space="Shared").ap()
    H["yT_sh"] = nc.dram_tensor("yT_sh", [4096, TPC], BF16, kind="Internal").ap()
    H["rs_y"] = nc.dram_tensor("rs_y", [L, D], F32, kind="Internal").ap()
    H["x1_dram"] = nc.dram_tensor("x1_dram", [TPC, D], F32, kind="Internal").ap()
    H["att_dram"] = nc.dram_tensor("att_dram", [TPC, D], F32, kind="Internal").ap()
    H["ln1T_dram"] = nc.dram_tensor("ln1T_dram", [D, TPC], BF16, kind="Internal").ap()
    H["x2_dram"] = nc.dram_tensor("x2_dram", [TPC, D], F32, kind="Internal").ap()
    H["ln2T_sh"] = nc.dram_tensor("ln2T_sh", [D, TPC], BF16, kind="Internal").ap()
    H["ln2T_ag"] = nc.dram_tensor("ln2T_ag", [NCORE * D, TPC], BF16, kind="Internal",
                                  addr_space="Shared").ap()
    H["y2_sh"] = nc.dram_tensor("y2_sh", [TPC, D], BF16, kind="Internal").ap()
    H["y2_ag"] = nc.dram_tensor("y2_ag", [NCORE * TPC, D], BF16, kind="Internal",
                                addr_space="Shared").ap()
    H["dbg"] = None
    H["skip_pass3"] = False
    with es:
        s = Sched(p, "f")
        block = es.enter_context(nc.Block())
        DONE0 = s.ch("D0")
        CC0 = s.ch("CC0")
        CCL2 = s.ch("CCL2")
        CCY2 = s.ch("CCY2")
        p.push_scope()
        emit_ph0(p, block, x_rows, ln0T_sh, H["identb"],
                 ag=(H["ln0T_ag"], CC0.s), done=DONE0.s)
        p.pop_scope()
        CC0.n = 1
        DONE0.n = 5
        PCC = emit_prologue(p, s, H, f32_ags=("wo", "core", "down", "up", "sh1", "sh2"))
        s.barrier("Bpro")
        for eng in ENGS:
            s.wait(eng, DONE0, 5)
        H["pre_waits1"] = [(CC0, 1)]
        H["y_a2a"] = None
        p.push_scope()
        emit_ph1(p, s, H)
        p.pop_scope()
        s.barrier("Bp1")
        CCR1 = emit_ph1b(p, s, H)
        s.barrier("Bp1b")
        H["w2f_view"] = H["w2f_ag"].rearrange("(r h) d -> h r d", r=8)
        H["ln2_ag"] = (H["ln2T_ag"], CCL2)
        H["skip_pass1"] = True
        H["pre_waits"] = [(PCC, PCC.n), (CCR1, 1)]
        emit_ph2(p, s, H)
        s.barrier("Bp2")
        H["pre_waits3"] = [(CCL2, 1)]
        H["y2_a2a"] = None
        emit_ph3(p, s, H)
        s.op("gpsimd", lambda e: e.collective_compute(
            "AllGather", ALU.bypass, replica_groups=RG8,
            ins=[H["y2_sh"][:, :].opt()], outs=[H["y2_ag"][:, :].opt()]), CCY2, 1)
        s.barrier("Bp3")
        emit_y2sel(p, s, H, pre=[(CCY2, 1)])
        s.barrier("Bsel")
        H["pre_waits4"] = [(PCC, PCC.n)]
        emit_ph4(p, s, H)
        s.emit(block)
    return nc


def slice_fused_inputs(inp, c):
    x = np.asarray(inp["L0_attn_x"])[0]
    m = {"x_rows": np.ascontiguousarray(x[TPC * c:TPC * (c + 1)])}
    m.update(slice_ph1_inputs(inp, c))
    m.update(slice_prologue_inputs(inp, c))
    m.update(slice_ph3_inputs(inp, c))
    m.update(slice_ph4_inputs(inp, c))
    ysel = np.zeros((TPC, 256), np.float32)
    ysel[np.arange(256) + 256 * c, np.arange(256)] = 1.0
    m["ysel_s"] = ysel
    return m


def kernel_fused(**inputs):
    cores = list(range(NCORE))
    in_maps = [slice_fused_inputs(inputs, c) for c in cores]
    r = run_bass_kernel_spmd(build_fused(), in_maps, core_ids=cores).results
    out = np.concatenate([np.asarray(r[c]["out_rows"], np.float32) for c in cores], 0)
    return out[None]


# version 05 (final)
# speedup vs baseline: 1.0814x; 1.0814x over previous
import numpy as np
import ml_dtypes
from contextlib import ExitStack
import concourse.bass as bass
import concourse.mybir as mybir
from concourse.bass_utils import run_bass_kernel_spmd

F32 = mybir.dt.float32
BF16 = mybir.dt.bfloat16
AF = mybir.ActivationFunctionType
ALU = mybir.AluOpType
AX = mybir.AxisListType.X
RG8 = [[0, 1, 2, 3, 4, 5, 6, 7]]

NCORE = 8
L = 16384
D = 2048
TPC = L // NCORE
EPS_LN = 1e-5
EPS_L0 = 1e-6


class Prog:
    def __init__(self, nc, es):
        self.nc = nc
        self.es = es
        self._scopes = []

    def _cur(self):
        return self._scopes[-1] if self._scopes else self.es

    def push_scope(self):
        st = ExitStack()
        self.es.enter_context(st)
        self._scopes.append(st)
        return st

    def pop_scope(self):
        st = self._scopes.pop()
        st.close()

    def sbuf(self, name, shape, dtype=F32, **kw):
        return self._cur().enter_context(self.nc.sbuf_tensor(name, shape, dtype, **kw))

    def psum(self, name, shape, dtype=F32):
        return self._cur().enter_context(self.nc.psum_tensor(name, shape, dtype))

    def sem(self, name):
        return self.es.enter_context(self.nc.semaphore(name))


class Ch:

    def __init__(self, sem):
        self.s = sem
        self.n = 0

    def inc(self, instr):
        instr.then_inc(self.s, 1)
        self.n += 1
        return self.n


def hadamard_pm(n):
    H = np.array([[1.0]], np.float32)
    while H.shape[0] < n:
        H = np.block([[H, H], [H, -H]])
    return H



def emit_ph0(p, block, x_rows, ln0T_sh, ident_bf, ag=None, tag="p0", done=None):
    NT = TPC // 128
    ND = D // 128

    sx = [p.sbuf(f"{tag}_sx{i}", [128, D], F32) for i in range(2)]
    sq = p.sbuf(f"{tag}_sq", [128, D], F32)
    lnb = p.sbuf(f"{tag}_lnb", [128, D], BF16)
    sid = p.sbuf(f"{tag}_sid", [128, 128], BF16)
    stat = p.sbuf(f"{tag}_stat", [128, 8], F32)
    epsc = p.sbuf(f"{tag}_eps", [128, 1], F32)
    slabs = p.sbuf(f"{tag}_slabs", [128, ND * TPC], BF16)
    ptr = [p.psum(f"{tag}_ptr{i}", [128, 128], BF16) for i in range(2)]

    ldc = p.sem(tag + "_ldc")
    ldp = [p.sem(tag + "_ld0"), p.sem(tag + "_ld1")]
    A = Ch(p.sem(tag + "_A"))
    V = Ch(p.sem(tag + "_V"))
    T = Ch(p.sem(tag + "_T"))
    E = Ch(p.sem(tag + "_E"))
    st = p.sem(tag + "_st")

    VT = 7

    @block.sync
    def _(sync):
        sync.dma_start(out=sid[:], in_=ident_bf).then_inc(ldc, 16)
        for i in range(NT):
            if i >= 2:
                sync.wait_ge(V.s, VT * (i - 1))
            sync.dma_start(out=sx[i % 2][:],
                           in_=x_rows[i * 128:(i + 1) * 128, :]).then_inc(ldp[i % 2], 16)
        if done is not None:
            sync.wait_ge(E.s, NT * ND)
            sync.wait_ge(st, 16 * ND)
            sync.sem_inc(done, 1)

    @block.scalar
    def _(scalar):
        for i in range(NT):
            scalar.wait_ge(ldp[i % 2], 16 * (i // 2 + 1))
            A.inc(scalar.activation(sq[:], sx[i % 2][:], AF.Square,
                                    accum_out=stat[:, 1:2]))
            scalar.wait_ge(V.s, VT * i + 5)
            A.inc(scalar.activation(stat[:, 6:7], stat[:, 4:5], AF.Sqrt,
                                    bias=epsc[:]))
            for k in range(ND):
                scalar.wait_ge(T.s, i * ND + k + 1)
                E.inc(scalar.copy(
                    slabs[:, k * TPC + i * 128: k * TPC + (i + 1) * 128],
                    ptr[k % 2][:]))
        if done is not None:
            scalar.wait_ge(st, 16 * ND)
            scalar.sem_inc(done, 1)

    @block.vector
    def _(vector):
        vector.memset(epsc[:], EPS_LN)
        for i in range(NT):
            b = VT * i
            vector.wait_ge(ldp[i % 2], 16 * (i // 2 + 1))
            s = sx[i % 2]
            V.inc(vector.reduce_sum(stat[:, 0:1], s[:], axis=AX))
            vector.wait_ge(V.s, b + 1)
            V.inc(vector.tensor_scalar(stat[:, 2:3], stat[:, 0:1],
                                       1.0 / D, None, ALU.mult))
            vector.wait_ge(A.s, 2 * i + 1)
            V.inc(vector.tensor_scalar(stat[:, 3:4], stat[:, 1:2],
                                       1.0 / D, None, ALU.mult))
            vector.wait_ge(V.s, b + 2)
            V.inc(vector.tensor_mul(stat[:, 5:6], stat[:, 2:3], stat[:, 2:3]))
            vector.wait_ge(V.s, b + 4)
            V.inc(vector.tensor_sub(stat[:, 4:5], stat[:, 3:4], stat[:, 5:6]))
            vector.wait_ge(A.s, 2 * i + 2)
            V.inc(vector.reciprocal(stat[:, 7:8], stat[:, 6:7]))
            vector.wait_ge(V.s, b + 6)
            if i >= 1:
                vector.wait_ge(T.s, i * ND)
            V.inc(vector.tensor_scalar(lnb[:], s[:], stat[:, 2:3], stat[:, 7:8],
                                       ALU.subtract, ALU.mult))
        if done is not None:
            vector.wait_ge(E.s, NT * ND)
            vector.wait_ge(st, 16 * ND)
            vector.sem_inc(done, 1)

    @block.tensor
    def _(tensor):
        tensor.wait_ge(ldc, 16)
        for i in range(NT):
            tensor.wait_ge(V.s, VT * i + 7)
            for k in range(ND):
                g = i * ND + k
                if g >= 2:
                    tensor.wait_ge(E.s, g - 1)
                T.inc(tensor.transpose(ptr[k % 2][:],
                                       lnb[:, k * 128:(k + 1) * 128], sid[:]))
        if done is not None:
            tensor.wait_ge(E.s, NT * ND)
            tensor.wait_ge(st, 16 * ND)
            tensor.sem_inc(done, 1)

    @block.gpsimd
    def _(gpsimd):
        gpsimd.wait_ge(E.s, NT * ND)
        for k in range(ND):
            gpsimd.dma_start(out=ln0T_sh[k * 128:(k + 1) * 128, :],
                             in_=slabs[:, k * TPC:(k + 1) * TPC]).then_inc(st, 16)
        gpsimd.wait_ge(st, 16 * ND)
        if ag is not None:
            ag_out, cc = ag
            gpsimd.collective_compute(
                "AllGather", ALU.bypass, replica_groups=RG8,
                ins=[ln0T_sh[:, :].opt()], outs=[ag_out[:, :].opt()],
            ).then_inc(cc, 1)
        if done is not None:
            gpsimd.sem_inc(done, 1)



ENGS = ("sync", "scalar", "vector", "tensor", "gpsimd")


class Sched:
    def __init__(self, p, tag):
        self.p = p
        self.tag = tag
        self.lists = {e: [] for e in ENGS}
        self.chains = {}

    def ch(self, name):
        if name not in self.chains:
            self.chains[name] = Ch(self.p.sem(f"{self.tag}_{name}"))
        return self.chains[name]

    def wait(self, eng, ch, n):
        if n <= 0:
            return
        sem = ch.s if isinstance(ch, Ch) else ch
        self.lists[eng].append(("w", sem, n))

    def op(self, eng, fn, ch=None, inc=1):
        sem = ch.s if ch is not None else None
        self.lists[eng].append(("o", fn, sem, inc))
        if ch is not None:
            ch.n += inc
            return ch.n
        return None

    def barrier(self, name, pre_dma_waits=()):
        G = self.ch(name)
        for (eng, ch, n) in pre_dma_waits:
            self.wait(eng, ch, n)
        base = G.n
        for eng in ENGS:
            self.lists[eng].append(("o", lambda e: e.sem_inc(G.s, 1), None, 0))
        G.n += 5
        for eng in ENGS:
            self.wait(eng, G, base + 5)

    def emit(self, block):
        def mk(name):
            def body(eng):
                for it in self.lists[name]:
                    if it[0] == "w":
                        eng.wait_ge(it[1], it[2])
                    else:
                        instr = it[1](eng)
                        if it[2] is not None:
                            instr.then_inc(it[2], it[3])
            return body
        block.sync(mk("sync"))
        block.scalar(mk("scalar"))
        block.vector(mk("vector"))
        block.tensor(mk("tensor"))
        block.gpsimd(mk("gpsimd"))



HK_C, HV_C, DK, DV, KC = 2, 4, 128, 128, 4
CHUNK = 128
import os as _os
NBLK = int(_os.environ.get("PH1_NBLK", L // 512))
CPB = 4
RESET_CHUNKS = {0, 16, 32, 48}


def emit_ph1(p, sched, H):
    s = sched
    nc = p.nc

    wqkvb = p.sbuf("p1_wqkvb", [128, 16 * 1024], BF16)
    wzb = p.sbuf("p1_wzb", [128, 16 * 512], BF16)
    wabb = p.sbuf("p1_wabb", [128, 16 * 8], BF16)
    wld = p.sbuf("p1_wld", [128, 1024], F32)
    convc = p.sbuf("p1_convc", [128, 32], F32)
    normw = p.sbuf("p1_normw", [128, 1], F32)
    dtb_t = p.sbuf("p1_dtbt", [128, 16], F32)
    nega_t = p.sbuf("p1_negat", [128, 16], F32)
    row16 = p.sbuf("p1_row16", [1, 16], F32)
    onecol = p.sbuf("p1_onecol", [128, 1], F32)
    onescol_b = p.sbuf("p1_onescolb", [128, 1], BF16)
    ones1b = p.sbuf("p1_ones1b", [1, 128], BF16)

    sid_b = p.sbuf("p1_sidb", [128, 128], BF16)
    sid_f = p.sbuf("p1_sidf", [128, 128], F32)
    ut_f = p.sbuf("p1_utf", [128, 128], F32)
    isl_f = p.sbuf("p1_islf", [128, 128], F32)
    ilc_f = p.sbuf("p1_ilcf", [128, 128], F32)
    sel4f = p.sbuf("p1_sel4f", [4, 512], F32)
    sel12b = p.sbuf("p1_sel12b", [12, 1536], BF16)

    ldbuf = p.sbuf("p1_ldbuf", [128, 16 * 512], BF16)
    qkvh = [p.sbuf(f"p1_qkvh{j}", [128, 516], F32) for j in range(8)]
    convacc = p.sbuf("p1_convacc", [128, 512], F32)
    qksilu = p.sbuf("p1_qksilu", [128, 4 * 512], F32)
    sqb = p.sbuf("p1_sqb", [128, 512], BF16)
    nrm = p.sbuf("p1_nrm", [1, 2048], F32)
    nrmb = p.sbuf("p1_nrmb", [1, 2048], BF16)
    qkT = p.sbuf("p1_qkT", [128, 4 * 512], BF16)
    vT = p.sbuf("p1_vT", [128, 4 * 512], BF16)
    szT = p.sbuf("p1_szT", [128, 4 * 512], BF16)
    abf = p.sbuf("p1_abf", [8, 512], F32)
    ab_tm = p.sbuf("p1_abtm", [128, 32], F32)
    gsc = p.sbuf("p1_gsc", [128, 16], F32)
    gsc2 = p.sbuf("p1_gsc2", [128, 16], F32)
    pk = p.sbuf("p1_pk", [128, 80], F32)
    pkb = p.sbuf("p1_pkb", [128, 48], BF16)
    rows4 = p.sbuf("p1_rows4", [4, 128], F32)
    rows12 = p.sbuf("p1_rows12", [12, 128], BF16)
    W4 = p.sbuf("p1_W4", [128, 512], F32)
    Wn = p.sbuf("p1_Wn", [128, 512], F32)
    D4 = p.sbuf("p1_D4", [128, 512], BF16)
    DT4 = p.sbuf("p1_DT4", [128, 512], BF16)
    tmpM = p.sbuf("p1_tmpM", [128, 512], BF16)
    Ncur = [p.sbuf(f"p1_Ncur{i}", [128, 512], BF16) for i in range(2)]
    NTst = p.sbuf("p1_NTst", [128, 7 * 512], BF16)
    GT4 = p.sbuf("p1_GT4", [128, 512], BF16)
    Kt4 = p.sbuf("p1_Kt4", [128, 512], BF16)
    Qt4 = p.sbuf("p1_Qt4", [128, 512], BF16)
    Kh4 = p.sbuf("p1_Kh4", [128, 512], BF16)
    Vtm = p.sbuf("p1_Vtm", [128, 512], BF16)
    Vsc = p.sbuf("p1_Vsc", [128, 512], BF16)
    Ktm = p.sbuf("p1_Ktm", [128, 256], BF16)
    IL4 = p.sbuf("p1_IL4", [128, 512], BF16)
    Xb = [p.sbuf(f"p1_X{i}", [128, 512], BF16) for i in range(2)]
    Rb = p.sbuf("p1_Rb", [128, 512], BF16)
    S0f = p.sbuf("p1_S0f", [128, 512], F32)
    S0b = p.sbuf("p1_S0b", [128, 512], BF16)
    oT = p.sbuf("p1_oT", [128, 2048], F32)
    gt1 = p.sbuf("p1_gt1", [128, 512], F32)
    gsq = p.sbuf("p1_gsq", [128, 512], BF16)
    grow = p.sbuf("p1_grow", [1, 2048], F32)
    growb = p.sbuf("p1_growb", [1, 2048], BF16)
    yslab = p.sbuf("p1_yslab", [128, 2048], BF16)

    pb0 = p.psum("p1_pb0", [128, 512], F32)
    pbT4 = p.psum("p1_pbT4", [4, 128], F32)
    pbT12 = p.psum("p1_pbT12", [12, 128], BF16)
    pb3 = p.psum("p1_pb3", [128, 512], F32)
    pb4 = p.psum("p1_pb4", [128, 512], F32)
    pb5 = p.psum("p1_pb5", [128, 512], F32)
    pb6 = p.psum("p1_pb6", [128, 512], F32)
    pb7 = p.psum("p1_pb7", [128, 512], BF16)

    LD = s.ch("LD")
    LDC = s.ch("LDC")
    LDW = s.ch("LDW")
    T = s.ch("T")
    V = s.ch("V")
    A = s.ch("A")
    G = s.ch("G")
    ST = s.ch("ST")

    for (ch, n) in H.get("pre_waits1", []):
        for eng in ENGS:
            s.wait(eng, ch, n)

    def view_b(buf, cols, c, n):
        return buf[:, cols * c: cols * c + n]

    s.op("sync", lambda e: e.dma_start(out=sid_b[:], in_=H["identb"]), LDC, 16)
    s.op("sync", lambda e: e.dma_start(out=sid_f[:], in_=H["identf"]), LDC, 16)
    s.op("sync", lambda e: e.dma_start(out=ut_f[:], in_=H["utones_f"]), LDC, 16)
    s.op("sync", lambda e: e.dma_start(out=isl_f[:], in_=H["isl_f"]), LDC, 16)
    s.op("sync", lambda e: e.dma_start(out=ilc_f[:], in_=H["ilc_f"]), LDC, 16)
    s.op("sync", lambda e: e.dma_start(out=sel4f[:], in_=H["sel4f"]), LDC, 16)
    s.op("sync", lambda e: e.dma_start(out=sel12b[:], in_=H["sel12b"]), LDC, 16)
    s.op("sync", lambda e: e.dma_start(
        out=convc[:].rearrange("p (j l) -> p j l", j=8),
        in_=H["conv_s"].rearrange("(j p) l -> p j l", p=128)), LDC, 16)
    s.op("sync", lambda e: e.dma_start(out=normw[:], in_=H["normw_s"]), LDC, 16)
    s.op("sync", lambda e: e.dma_start(out=row16[0:1, 0:4], in_=H["alog_s"]), LDC, 16)
    s.op("sync", lambda e: e.dma_start(out=row16[0:1, 4:8], in_=H["dtb_s"]), LDC, 16)
    nconst = LDC.n

    for k in range(16):
        s.wait("sync", V, 0)
        s.op("sync", lambda e, k=k: e.dma_start(
            out=wld[:], in_=H["wqkv_s"][k * 128:(k + 1) * 128, :]), LDW, 16)
        s.wait("vector", LDW, LDW.n)
        s.op("vector", lambda e, k=k: e.tensor_copy(wqkvb[:, k * 1024:(k + 1) * 1024], wld[:]), V)
        s.wait("sync", V, V.n)
    for k in range(16):
        s.op("sync", lambda e, k=k: e.dma_start(
            out=wld[:, 0:512], in_=H["wz_s"][k * 128:(k + 1) * 128, :]), LDW, 16)
        s.wait("vector", LDW, LDW.n)
        s.op("vector", lambda e, k=k: e.tensor_copy(wzb[:, k * 512:(k + 1) * 512], wld[:, 0:512]), V)
        s.wait("sync", V, V.n)
    for k in range(16):
        s.op("sync", lambda e, k=k: e.dma_start(
            out=wld[:, 0:8], in_=H["wab_s"][k * 128:(k + 1) * 128, :]), LDW, 16)
        s.wait("vector", LDW, LDW.n)
        s.op("vector", lambda e, k=k: e.tensor_copy(wabb[:, k * 8:(k + 1) * 8], wld[:, 0:8]), V)
        s.wait("sync", V, V.n)

    onesrow_f = p.sbuf("p1_onesrowf", [1, 128], F32)
    s.wait("vector", LDC, nconst)
    s.op("vector", lambda e: e.memset(onecol[:], 1.0), V)
    s.op("vector", lambda e: e.memset(onescol_b[:], 1.0), V)
    s.op("vector", lambda e: e.memset(ones1b[:], 1.0), V)
    s.op("vector", lambda e: e.memset(onesrow_f[:], 1.0), V)
    s.wait("scalar", LDC, nconst)
    s.op("scalar", lambda e: e.activation(row16[0:1, 8:12], row16[0:1, 0:4], AF.Exp), A)
    s.wait("vector", A, A.n)
    s.op("vector", lambda e: e.tensor_scalar(row16[0:1, 8:12], row16[0:1, 8:12], -1.0, None, ALU.mult), V)
    nneg = V.n
    s.wait("vector", V, nneg)
    s.op("vector", lambda e: e.tensor_copy(
        gsc[0:1, 0:16].rearrange("p (c h) -> p c h", c=4),
        row16[0:1, 8:12].unsqueeze(1).broadcast_to([1, 4, 4])), V)
    s.op("vector", lambda e: e.tensor_copy(
        gsc2[0:1, 0:16].rearrange("p (c h) -> p c h", c=4),
        row16[0:1, 4:8].unsqueeze(1).broadcast_to([1, 4, 4])), V)
    s.wait("tensor", V, V.n)
    s.wait("tensor", LDC, nconst)
    s.op("tensor", lambda e: e.matmul(pb3[:, 0:16], onesrow_f[:], gsc[0:1, 0:16], start=True, stop=True), T)
    s.op("tensor", lambda e: e.matmul(pb4[:, 0:16], onesrow_f[:], gsc2[0:1, 0:16], start=True, stop=True), T)
    s.wait("vector", T, T.n)
    s.op("vector", lambda e: e.tensor_copy(nega_t[:], pb3[:, 0:16]), V)
    s.op("vector", lambda e: e.tensor_copy(dtb_t[:], pb4[:, 0:16]), V)
    wdone_V, wdone_T = V.n, T.n

    prevRV = V.n
    prevRowsVcopy = 0
    prevRowsUseT = 0
    prevPBfreeV = V.n
    prevVtmUseV = 0
    prevPb7FreeV = 0
    prevST = 0
    prevProjT = T.n
    prevHaloG = 0
    prevAEnd = A.n
    for b in range(NBLK):
        r, t0 = b // 4, (b % 4) * 512

        s.wait("sync", T, prevProjT)
        ld_base = LD.n
        for k in range(16):
            s.op("sync", lambda e, k=k, r=r, t0=t0: e.dma_start(
                out=ldbuf[:, k * 512:(k + 1) * 512],
                in_=H["ln0T_ag"][2048 * r + 128 * k: 2048 * r + 128 * (k + 1),
                                 t0:t0 + 512]), LD, 16)
        ld_done = LD.n

        if b == 0:
            s.wait("tensor", V, wdone_V)
        s.wait("tensor", LD, ld_done)
        s.wait("tensor", V, prevPBfreeV)
        s.wait("tensor", A, prevAEnd)
        s.wait("vector", G, prevHaloG)
        pbanks = [pb3, pb4, pb5, pb6]
        evV = []
        for g in range(13):
            bank = pbanks[g % 4]
            if g >= 4:
                ch4, n4 = evV[g - 4]
                s.wait("tensor", ch4, n4)
            for k in range(16):
                if g < 8:
                    lhs = lambda e, k=k, g=g: wqkvb[:, 1024 * k + 128 * g: 1024 * k + 128 * (g + 1)]
                    ncols = 128
                elif g < 12:
                    lhs = lambda e, k=k, g=g: wzb[:, 512 * k + 128 * (g - 8): 512 * k + 128 * (g - 7)]
                    ncols = 128
                else:
                    lhs = lambda e, k=k: wabb[:, 8 * k: 8 * (k + 1)]
                    ncols = 8
                s.op("tensor", lambda e, k=k, g=g, bank=bank, lhs=lhs, ncols=ncols: e.matmul(
                    bank[0:ncols, :], lhs(e), ldbuf[:, k * 512:(k + 1) * 512],
                    start=(k == 0), stop=(k == 15)), T)
            tdone = T.n
            if g < 8:
                s.wait("vector", T, tdone)
                s.op("vector", lambda e, g=g, bank=bank: e.tensor_copy(
                    qkvh[g][:, 3:515], bank[:]), V)
                evV.append((V, V.n))
            elif g < 12:
                s.wait("scalar", T, tdone)
                s.op("scalar", lambda e, g=g, bank=bank: e.activation(
                    szT[:, (g - 8) * 512:(g - 7) * 512], bank[:], AF.Silu), A)
                evV.append((A, A.n))
            else:
                s.wait("vector", T, tdone)
                s.op("vector", lambda e, bank=bank: e.tensor_copy(abf[:], bank[0:8, :]), V)
                evV.append((V, V.n))
        projA = A.n

        for j in range(8):
            s.wait("vector", evV[j][0], evV[j][1])
            if b in (0, 4, 8, 12):
                s.op("vector", lambda e, j=j: e.memset(qkvh[j][:, 0:3], 0.0), V)
            cc0 = V.n
            s.op("vector", lambda e, j=j: e.tensor_scalar(
                convacc[:], qkvh[j][:, 3:515], convc[:, 4 * j + 3: 4 * j + 4],
                None, ALU.mult), V)
            for d2 in (1, 2, 3):
                s.wait("vector", V, V.n)
                s.op("vector", lambda e, j=j, d2=d2: e.scalar_tensor_tensor(
                    convacc[:], qkvh[j][:, 3 - d2:515 - d2],
                    convc[:, 4 * j + 3 - d2: 4 * j + 4 - d2], convacc[:],
                    ALU.mult, ALU.add), V)
            cdone = V.n
            s.wait("scalar", V, cdone)
            if j < 4:
                s.op("scalar", lambda e, j=j: e.activation(
                    qksilu[:, j * 512:(j + 1) * 512], convacc[:], AF.Silu), A)
            else:
                s.op("scalar", lambda e, j=j: e.activation(
                    vT[:, (j - 4) * 512:(j - 3) * 512], convacc[:], AF.Silu), A)
            s.wait("gpsimd", V, cdone)
            s.op("gpsimd", lambda e, j=j: e.tensor_copy(
                qkvh[j][:, 0:3], qkvh[j][:, 512:515]), G)
            s.wait("vector", A, A.n)
        silA = A.n

        s.wait("tensor", V, prevRV)
        for j in range(4):
            if j > 0:
                s.wait("scalar", T, T.n)
                s.wait("tensor", V, V.n)
            s.op("scalar", lambda e, j=j: e.activation(
                sqb[:], qksilu[:, j * 512:(j + 1) * 512], AF.Square), A)
            s.wait("tensor", A, A.n)
            s.op("tensor", lambda e: e.matmul(pb0[0:1, :], onescol_b[:], sqb[:],
                                              start=True, stop=True), T)
            s.wait("vector", T, T.n)
            s.op("vector", lambda e, j=j: e.tensor_copy(nrm[0:1, j * 512:(j + 1) * 512],
                                                        pb0[0:1, :]), V)
        s.wait("vector", V, V.n)
        s.op("vector", lambda e: e.tensor_scalar(nrm[0:1, :], nrm[0:1, :], EPS_L0, None, ALU.add), V)
        s.wait("scalar", V, V.n)
        s.op("scalar", lambda e: e.activation(nrm[0:1, :], nrm[0:1, :], AF.Sqrt), A)
        s.wait("vector", A, A.n)
        s.op("vector", lambda e: e.reciprocal(nrm[0:1, :], nrm[0:1, :]), V)
        s.wait("vector", V, V.n)
        s.op("vector", lambda e: e.tensor_scalar(nrm[0:1, 0:1024], nrm[0:1, 0:1024],
                                                 DK ** -0.5, None, ALU.mult), V)
        s.wait("vector", V, V.n)
        s.op("vector", lambda e: e.tensor_copy(nrmb[0:1, :], nrm[0:1, :]), V)
        nrmV = V.n
        for j in range(4):
            s.wait("tensor", V, nrmV if j == 0 else V.n)
            s.op("tensor", lambda e, j=j: e.matmul(
                pb0[:], ones1b[:], nrmb[0:1, j * 512:(j + 1) * 512],
                start=True, stop=True), T)
            s.wait("vector", T, T.n)
            s.op("vector", lambda e, j=j: e.tensor_mul(
                qkT[:, j * 512:(j + 1) * 512], qksilu[:, j * 512:(j + 1) * 512],
                pb0[:]), V)
        qkV = V.n
        if b == 0 and H.get("dbg"):
            s.wait("gpsimd", V, qkV)
            s.wait("gpsimd", A, silA)
            s.op("gpsimd", lambda e: e.dma_start(out=H["dbg"]["qkT"], in_=qkT[:]), ST, 16)
            s.op("gpsimd", lambda e: e.dma_start(out=H["dbg"]["vT"], in_=vT[:]), ST, 16)
            s.op("gpsimd", lambda e: e.dma_start(out=H["dbg"]["szT"], in_=szT[:]), ST, 16)

        s.wait("tensor", evV[12][0], evV[12][1])
        s.wait("tensor", V, qkV)
        for c in range(4):
            s.op("tensor", lambda e, c=c: e.transpose(
                pb0[:, 8 * c: 8 * (c + 1)], abf[:, c * 128:(c + 1) * 128],
                sid_f[0:8, 0:8]), T)
        abT = T.n
        s.wait("vector", T, abT)
        s.op("vector", lambda e: e.tensor_copy(ab_tm[:], pb0[:, 0:32]), V)
        s.wait("vector", V, V.n)
        av = ab_tm[:].rearrange("p (c q) -> p c q", c=4)[:, :, 0:4]
        bv = ab_tm[:].rearrange("p (c q) -> p c q", c=4)[:, :, 4:8]
        pkv = pk[:].rearrange("p (c q) -> p c q", c=4)
        dtbv = dtb_t[:].rearrange("p (c h) -> p c h", c=4)
        s.op("vector", lambda e, av=av, dtbv=dtbv: e.tensor_tensor(
            gsc[:].rearrange("p (c h) -> p c h", c=4), av, dtbv, ALU.add), V)
        s.wait("scalar", V, V.n)
        s.op("scalar", lambda e: e.activation(gsc2[:], gsc[:], AF.Abs), A)
        s.wait("scalar", A, A.n)
        s.op("scalar", lambda e: e.activation(gsc2[:], gsc2[:], AF.Exp, scale=-1.0), A)
        s.wait("scalar", A, A.n)
        s.op("scalar", lambda e: e.activation(gsc2[:], gsc2[:], AF.Ln, bias=onecol[:]), A)
        s.wait("vector", A, A.n)
        s.op("vector", lambda e: e.tensor_relu(W4[:, 0:16], gsc[:]), V)
        s.wait("vector", V, V.n)
        s.op("vector", lambda e: e.tensor_add(gsc2[:], gsc2[:], W4[:, 0:16]), V)
        s.wait("vector", V, V.n)
        s.op("vector", lambda e: e.tensor_mul(gsc[:], gsc2[:], nega_t[:]), V)
        gV = V.n
        s.wait("tensor", V, gV)
        s.op("tensor", lambda e: e.matmul(pb0[:, 0:16], ut_f[:], gsc[:],
                                          start=True, stop=True), T)
        s.wait("vector", T, T.n)
        s.op("vector", lambda e, pkv=pkv: e.tensor_copy(
            pkv[:, :, 0:4], pb0[:, 0:16].rearrange("p (c h) -> p c h", c=4)), V)
        cumV = V.n
        s.wait("scalar", V, cumV)
        s.op("scalar", lambda e, pkv=pkv: e.activation(pkv[:, :, 12:16],
                                                       pkv[:, :, 0:4], AF.Exp), A)
        s.op("scalar", lambda e, pkv=pkv, bv=bv: e.activation(pkv[:, :, 16:20],
                                                              bv, AF.Sigmoid), A)
        betA = A.n
        s.wait("vector", A, betA)
        s.op("vector", lambda e, pkv=pkv: e.tensor_scalar(
            pkv[:, :, 4:8], pkv[:, :, 16:20], -1.0, None, ALU.mult), V)
        s.op("vector", lambda e, pkv=pkv: e.tensor_tensor(
            pkv[:, :, 8:12], pkv[:, :, 16:20], pkv[:, :, 12:16], ALU.mult), V)
        s.wait("vector", V, V.n)
        s.op("vector", lambda e, pkv=pkv: e.tensor_copy(
            pkb[:].rearrange("p (c q) -> p c q", c=4), pkv[:, :, 4:16]), V)
        pkV = V.n
        if b == 0 and H.get("dbg"):
            s.wait("gpsimd", V, pkV)
            s.op("gpsimd", lambda e: e.dma_start(out=H["dbg"]["pk"], in_=pk[:]), ST, 16)

        for c in range(CPB):
            gchunk = b * CPB + c
            qk_c = lambda kh, cc=c: qkT[:, 512 * kh + 128 * cc: 512 * kh + 128 * (cc + 1)]
            kk_c = lambda kh, cc=c: qkT[:, 512 * (2 + kh) + 128 * cc: 512 * (2 + kh) + 128 * (cc + 1)]
            vv_c = lambda h, cc=c: vT[:, 512 * h + 128 * cc: 512 * h + 128 * (cc + 1)]
            cumcol = pk[:, 20 * c + 0: 20 * c + 4]
            negbcol = pk[:, 20 * c + 4: 20 * c + 8]
            betacol = pk[:, 20 * c + 16: 20 * c + 20]

            if gchunk in RESET_CHUNKS:
                s.wait("vector", T, T.n)
                s.wait("vector", A, A.n)
                s.op("vector", lambda e: e.memset(S0f[:], 0.0), V)
                s.wait("vector", V, V.n)
                s.op("vector", lambda e: e.tensor_copy(S0b[:], S0f[:]), V)
                s.wait("tensor", V, V.n)

            s.wait("tensor", V, max(prevRV, pkV))
            s.wait("tensor", V, qkV)
            for kh in range(2):
                kap, qap = kk_c(kh), qk_c(kh)
                s.op("tensor", lambda e, kh=kh, kap=kap: e.matmul(
                    pb0[:, 128 * kh:128 * (kh + 1)], kap, kap,
                    start=True, stop=True), T)
                s.op("tensor", lambda e, kh=kh, kap=kap, qap=qap: e.matmul(
                    pb0[:, 256 + 128 * kh:256 + 128 * (kh + 1)], kap, qap,
                    start=True, stop=True), T)
            pT = T.n

            s.wait("tensor", V, pkV)
            s.wait("tensor", V, prevRowsVcopy)
            s.op("tensor", lambda e, c=c: e.transpose(
                pbT4[:], pk[:, 20 * c: 20 * c + 4], sid_f[:]), T)
            cumT_T = T.n
            s.op("tensor", lambda e, c=c: e.transpose(
                pbT12[:], pkb[:, 12 * c: 12 * c + 12], sid_b[:]), T)
            r12T = T.n
            s.wait("vector", T, cumT_T)
            s.wait("vector", T, prevRowsUseT)
            s.op("vector", lambda e: e.tensor_copy(rows4[:], pbT4[:]), V)
            s.wait("vector", T, r12T)
            s.op("vector", lambda e: e.tensor_copy(rows12[:], pbT12[:]), V)
            rowsV = V.n
            prevRowsVcopy = rowsV

            s.wait("tensor", V, rowsV)
            s.wait("tensor", V, prevPBfreeV)
            for h in range(4):
                s.op("tensor", lambda e, h=h: e.matmul(
                    pb3[:, 128 * h:128 * (h + 1)], sel4f[:, 128 * h:128 * (h + 1)],
                    rows4[:], start=True, stop=True), T)
            for h in range(4):
                s.op("tensor", lambda e, h=h: e.matmul(
                    pb5[:, 128 * h:128 * (h + 1)], sel12b[:, 512 + 128 * h: 640 + 128 * h],
                    rows12[:], start=True, stop=True), T)
            for h in range(4):
                s.op("tensor", lambda e, h=h: e.matmul(
                    pb6[:, 128 * h:128 * (h + 1)], sel12b[:, 1024 + 128 * h: 1152 + 128 * h],
                    rows12[:], start=True, stop=True), T)
            selT = T.n
            prevRowsUseT = selT

            if b == 0 and c == 0 and H.get("dbg"):
                s.wait("vector", T, selT)
                s.op("vector", lambda e: e.tensor_copy(gt1[:], pb5[:]), V)
                s.wait("gpsimd", V, V.n)
                s.op("gpsimd", lambda e: e.dma_start(out=H["dbg"]["pb5"], in_=gt1[:]), ST, 16)
                s.op("gpsimd", lambda e: e.dma_start(out=H["dbg"]["rows12"], in_=rows12[:]), ST, 16)
                s.wait("vector", ST, ST.n)
            cumv = cumcol.unsqueeze(2).broadcast_to([128, 4, 128])
            s.wait("vector", T, selT)
            s.op("vector", lambda e, cumv=cumv: e.tensor_tensor(
                W4[:].rearrange("p (h t) -> p h t", h=4),
                pb3[:].rearrange("p (h t) -> p h t", h=4), cumv, ALU.subtract), V)
            s.wait("vector", V, V.n)
            s.op("vector", lambda e: e.tensor_scalar(Wn[:], W4[:], -1.0, 0.0,
                                                     ALU.mult, ALU.min), V)
            s.wait("scalar", V, V.n)
            s.op("scalar", lambda e: e.activation(D4[:], Wn[:], AF.Exp), A)
            s.wait("vector", A, A.n)
            islv = isl_f[:].unsqueeze(1).broadcast_to([128, 4, 128])
            s.op("vector", lambda e, islv=islv: e.tensor_tensor(
                D4[:].rearrange("p (h t) -> p h t", h=4),
                D4[:].rearrange("p (h t) -> p h t", h=4), islv, ALU.mult), V)
            dV = V.n
            s.wait("vector", V, dV)
            s.op("vector", lambda e: e.tensor_scalar(Wn[:], W4[:], 0.0, None,
                                                     ALU.min), V)
            s.wait("scalar", V, V.n)
            s.op("scalar", lambda e: e.activation(DT4[:], Wn[:], AF.Exp), A)
            s.wait("vector", A, A.n)
            ilcv = ilc_f[:].unsqueeze(1).broadcast_to([128, 4, 128])
            s.op("vector", lambda e, ilcv=ilcv: e.tensor_tensor(
                DT4[:].rearrange("p (h t) -> p h t", h=4),
                DT4[:].rearrange("p (h t) -> p h t", h=4), ilcv, ALU.mult), V)
            dtV = V.n

            if b == 0 and c == 0 and H.get("dbg"):
                s.wait("gpsimd", V, dtV)
                s.op("gpsimd", lambda e: e.dma_start(out=H["dbg"]["W4"], in_=W4[:]), ST, 16)
                s.op("gpsimd", lambda e: e.dma_start(out=H["dbg"]["D4"], in_=D4[:]), ST, 16)
                s.op("gpsimd", lambda e: e.dma_start(out=H["dbg"]["DT4"], in_=DT4[:]), ST, 16)
                s.wait("vector", ST, ST.n)
            pview = pb0[:, 0:256].rearrange("p (kh t) -> p kh t", kh=2
                        ).unsqueeze(2).broadcast_to([128, 2, 2, 128])
            kqview = pb0[:, 256:512].rearrange("p (kh t) -> p kh t", kh=2
                        ).unsqueeze(2).broadcast_to([128, 2, 2, 128])
            s.wait("vector", T, pT)
            s.wait("vector", V, dV)
            s.op("vector", lambda e, pview=pview: e.tensor_tensor(
                tmpM[:].rearrange("p (a b t) -> p a b t", a=2, b=2),
                pview, D4[:].rearrange("p (a b t) -> p a b t", a=2, b=2), ALU.mult), V)
            s.wait("vector", V, V.n)
            negbv = negbcol.unsqueeze(2).broadcast_to([128, 4, 128])
            s.op("vector", lambda e, negbv=negbv: e.tensor_tensor(
                Ncur[0][:].rearrange("p (h t) -> p h t", h=4),
                tmpM[:].rearrange("p (h t) -> p h t", h=4), negbv, ALU.mult), V)
            nV = V.n
            s.wait("tensor", V, nV)
            s.wait("tensor", V, prevPb7FreeV)
            for h in range(4):
                s.op("tensor", lambda e, h=h: e.transpose(
                    pb7[:, 128 * h:128 * (h + 1)], Ncur[0][:, 128 * h:128 * (h + 1)],
                    sid_b[:]), T)
            s.wait("vector", T, T.n)
            s.op("vector", lambda e: e.tensor_copy(NTst[:, 0:512], pb7[:]), V)
            ntV = V.n
            s.op("vector", lambda e, kqview=kqview: e.tensor_tensor(
                GT4[:].rearrange("p (a b t) -> p a b t", a=2, b=2),
                kqview, DT4[:].rearrange("p (a b t) -> p a b t", a=2, b=2), ALU.mult), V)
            gtV = V.n

            for h in range(4):
                kh = h // 2
                kap, qap = kk_c(kh), qk_c(kh)
                s.op("vector", lambda e, h=h, kap=kap: e.tensor_tensor(
                    Kt4[:, 128 * h:128 * (h + 1)], kap,
                    pb5[:, 128 * h:128 * (h + 1)], ALU.mult), V)
                s.op("vector", lambda e, h=h, qap=qap: e.tensor_tensor(
                    Qt4[:, 128 * h:128 * (h + 1)], qap,
                    pb6[:, 128 * h:128 * (h + 1)], ALU.mult), V)
            ktqtV = V.n
            for h in range(4):
                s.op("vector", lambda e, h=h: e.tensor_scalar(
                    IL4[:, 128 * h:128 * (h + 1)], sid_b[:],
                    pb6[:, 128 * h + 127:128 * h + 128], None, ALU.mult), V)
            ilV = V.n
            prevPBfreeV = V.n

            if b == 0 and c == 0 and H.get("dbg"):
                s.wait("gpsimd", V, max(nV, ntV, gtV))
                s.op("gpsimd", lambda e: e.dma_start(out=H["dbg"]["N0"], in_=Ncur[0][:]), ST, 16)
                s.op("gpsimd", lambda e: e.dma_start(out=H["dbg"]["NT0"], in_=NTst[:, 0:512]), ST, 16)
                s.op("gpsimd", lambda e: e.dma_start(out=H["dbg"]["GT4"], in_=GT4[:]), ST, 16)
                s.wait("vector", ST, ST.n)
            cur = 0
            s.wait("tensor", V, max(nV, ntV))
            for lev in range(1, 7):
                for h in range(4):
                    s.op("tensor", lambda e, lev=lev, h=h, cur=cur: e.matmul(
                        pb3[:, 128 * h:128 * (h + 1)],
                        NTst[:, 512 * (lev - 1) + 128 * h: 512 * (lev - 1) + 128 * (h + 1)],
                        Ncur[cur][:, 128 * h:128 * (h + 1)],
                        start=True, stop=True), T)
                sqT = T.n
                s.wait("vector", T, sqT)
                s.op("vector", lambda e, cur=cur: e.tensor_copy(Ncur[1 - cur][:], pb3[:]), V)
                nlevV = V.n
                s.wait("tensor", V, nlevV)
                for h in range(4):
                    s.op("tensor", lambda e, h=h, cur=cur: e.transpose(
                        pb7[:, 128 * h:128 * (h + 1)],
                        Ncur[1 - cur][:, 128 * h:128 * (h + 1)], sid_b[:]), T)
                s.wait("vector", T, T.n)
                s.op("vector", lambda e, lev=lev: e.tensor_copy(
                    NTst[:, 512 * lev: 512 * (lev + 1)], pb7[:]), V)
                s.wait("tensor", V, V.n)
                cur = 1 - cur

            if b == 0 and c == 0 and H.get("dbg"):
                s.wait("gpsimd", V, V.n)
                s.op("gpsimd", lambda e: e.dma_start(out=H["dbg"]["NT1"], in_=NTst[:, 512:1024]), ST, 16)
                s.op("gpsimd", lambda e: e.dma_start(out=H["dbg"]["NT2"], in_=NTst[:, 1024:1536]), ST, 16)
                s.op("gpsimd", lambda e: e.dma_start(out=H["dbg"]["NT6"], in_=NTst[:, 3072:3584]), ST, 16)
                s.wait("vector", ST, ST.n)
                s.wait("tensor", ST, ST.n)
            s.wait("tensor", A, silA)
            s.wait("tensor", V, prevVtmUseV)
            for h in range(4):
                vap = vv_c(h)
                s.op("tensor", lambda e, h=h, vap=vap: e.transpose(
                    pb7[:, 128 * h:128 * (h + 1)], vap, sid_b[:]), T)
            vtT = T.n
            s.wait("vector", T, vtT)
            s.op("vector", lambda e: e.tensor_copy(Vtm[:], pb7[:]), V)
            vtV = V.n
            if b == 0 and c == 0 and H.get("dbg"):
                s.wait("gpsimd", V, vtV)
                s.op("gpsimd", lambda e: e.dma_start(out=H["dbg"]["Vtm"], in_=Vtm[:]), ST, 16)
                s.wait("vector", ST, ST.n)
            s.wait("tensor", V, vtV)
            for kh in range(2):
                kap = kk_c(kh)
                s.op("tensor", lambda e, kh=kh, kap=kap: e.transpose(
                    pb7[:, 128 * kh:128 * (kh + 1)], kap, sid_b[:]), T)
            ktT = T.n
            s.wait("vector", T, ktT)
            s.op("vector", lambda e: e.tensor_copy(Ktm[:], pb7[:, 0:256]), V)
            ktV = V.n
            betav = betacol.unsqueeze(2).broadcast_to([128, 4, 128])
            s.wait("vector", V, vtV)
            s.op("vector", lambda e, betav=betav: e.tensor_tensor(
                Vsc[:].rearrange("p (h d) -> p h d", h=4),
                Vtm[:].rearrange("p (h d) -> p h d", h=4), betav, ALU.mult), V)
            vscV = V.n
            ktmv = Ktm[:].rearrange("p (kh d) -> p kh d", kh=2
                       ).unsqueeze(2).broadcast_to([128, 2, 2, 128])
            dtl = DT4[:].rearrange("p (h t) -> p h t", h=4)[:, :, 127:128
                       ].broadcast_to([128, 4, 128])
            s.wait("vector", V, ktV)
            s.op("vector", lambda e, ktmv=ktmv, dtl=dtl: e.tensor_tensor(
                Kh4[:].rearrange("p (a b d) -> p a b d", a=2, b=2), ktmv,
                dtl.rearrange("p (a b) d -> p a b d", a=2), ALU.mult), V)
            khV = V.n
            prevVtmUseV = V.n
            prevPb7FreeV = V.n

            s.wait("tensor", V, ktqtV)
            for h in range(4):
                s.op("tensor", lambda e, h=h: e.matmul(
                    pb0[:, 128 * h:128 * (h + 1)], Kt4[:, 128 * h:128 * (h + 1)],
                    S0b[:, 128 * h:128 * (h + 1)], start=True, stop=True), T)
            ksT = T.n
            s.wait("vector", T, ksT)
            s.wait("vector", V, vscV)
            s.op("vector", lambda e: e.tensor_sub(Rb[:], Vsc[:], pb0[:]), V)
            prevRV = V.n

            xcur = 0
            s.wait("vector", V, prevRV)
            s.op("vector", lambda e: e.tensor_copy(Xb[0][:], Rb[:]), V)
            s.wait("vector", V, V.n)
            for i in range(6, -1, -1):
                s.wait("tensor", V, V.n)
                for h in range(4):
                    s.op("tensor", lambda e, i=i, h=h, xcur=xcur: e.matmul(
                        pb3[:, 128 * h:128 * (h + 1)],
                        NTst[:, 512 * i + 128 * h: 512 * i + 128 * (h + 1)],
                        Xb[xcur][:, 128 * h:128 * (h + 1)],
                        start=True, stop=True), T)
                s.wait("vector", T, T.n)
                s.op("vector", lambda e, xcur=xcur: e.tensor_add(
                    Xb[1 - xcur][:], pb3[:], Xb[xcur][:]), V)
                s.wait("vector", V, V.n)
                xcur = 1 - xcur
            deltaV = V.n
            delta = Xb[xcur]

            if b == 0 and c == 0 and H.get("dbg"):
                s.wait("gpsimd", V, deltaV)
                s.op("gpsimd", lambda e, delta=delta: e.dma_start(out=H["dbg"]["X"], in_=delta[:]), ST, 16)
                s.op("gpsimd", lambda e: e.dma_start(out=H["dbg"]["Rb"], in_=Rb[:]), ST, 16)
                s.op("gpsimd", lambda e: e.dma_start(out=H["dbg"]["Kt4"], in_=Kt4[:]), ST, 16)
                s.op("gpsimd", lambda e: e.dma_start(out=H["dbg"]["Qt4"], in_=Qt4[:]), ST, 16)
                s.wait("vector", ST, ST.n)
            s.wait("tensor", V, max(deltaV, gtV))
            for h in range(4):
                s.op("tensor", lambda e, h=h, delta=delta: e.matmul(
                    pb5[:, 128 * h:128 * (h + 1)], delta[:, 128 * h:128 * (h + 1)],
                    GT4[:, 128 * h:128 * (h + 1)], start=True, stop=False), T)
                s.op("tensor", lambda e, h=h: e.matmul(
                    pb5[:, 128 * h:128 * (h + 1)], S0b[:, 128 * h:128 * (h + 1)],
                    Qt4[:, 128 * h:128 * (h + 1)], start=False, stop=True), T)
            oT_T = T.n
            s.wait("vector", T, oT_T)
            s.op("vector", lambda e, c=c: e.tensor_copy(
                oT[:].rearrange("p (h ct) -> p h ct", h=4)[:, :, 128 * c:128 * (c + 1)],
                pb5[:].rearrange("p (h t) -> p h t", h=4)), V)
            oEvV = V.n

            s.wait("tensor", V, max(khV, ilV, deltaV))
            for h in range(4):
                s.op("tensor", lambda e, h=h, delta=delta: e.matmul(
                    pb6[:, 128 * h:128 * (h + 1)], Kh4[:, 128 * h:128 * (h + 1)],
                    delta[:, 128 * h:128 * (h + 1)], start=True, stop=False), T)
                s.op("tensor", lambda e, h=h: e.matmul(
                    pb6[:, 128 * h:128 * (h + 1)], IL4[:, 128 * h:128 * (h + 1)],
                    S0b[:, 128 * h:128 * (h + 1)], start=False, stop=True), T)
            sT = T.n
            s.wait("vector", T, sT)
            s.op("vector", lambda e: e.tensor_copy(S0f[:], pb6[:]), V)
            s.wait("scalar", T, sT)
            s.wait("scalar", V, V.n)
            s.op("scalar", lambda e: e.copy(S0b[:], pb6[:]), A)
            s.wait("tensor", A, A.n)
            prevPBfreeV = V.n

        if b == 0 and H.get("dbg"):
            s.wait("gpsimd", V, V.n)
            s.op("gpsimd", lambda e: e.dma_start(out=H["dbg"]["oT"], in_=oT[:]), ST, 16)
            s.wait("vector", ST, ST.n)
        s.wait("vector", ST, prevST)
        s.wait("vector", A, A.n)
        for h in range(4):
            s.wait("vector", A, A.n)
            s.op("vector", lambda e, h=h: e.tensor_mul(
                gt1[:], oT[:, 512 * h: 512 * (h + 1)], szT[:, 512 * h:512 * (h + 1)]), V)
            s.wait("scalar", V, V.n)
            s.op("scalar", lambda e: e.activation(gsq[:], gt1[:], AF.Square), A)
            s.wait("tensor", A, A.n)
            s.wait("tensor", V, V.n if h > 0 else prevRV)
            s.op("tensor", lambda e: e.matmul(pb0[0:1, :], onescol_b[:], gsq[:],
                                              start=True, stop=True), T)
            s.wait("vector", T, T.n)
            s.op("vector", lambda e, h=h: e.tensor_copy(
                grow[0:1, 512 * h:512 * (h + 1)], pb0[0:1, :]), V)
        s.wait("vector", V, V.n)
        s.op("vector", lambda e: e.tensor_scalar(grow[0:1, :], grow[0:1, :],
                                                 1.0 / DV, EPS_L0, ALU.mult, ALU.add), V)
        s.wait("scalar", V, V.n)
        s.op("scalar", lambda e: e.activation(grow[0:1, :], grow[0:1, :], AF.Sqrt), A)
        s.wait("vector", A, A.n)
        s.op("vector", lambda e: e.reciprocal(grow[0:1, :], grow[0:1, :]), V)
        s.wait("vector", V, V.n)
        s.op("vector", lambda e: e.tensor_copy(growb[0:1, :], grow[0:1, :]), V)
        growV = V.n
        for h in range(4):
            s.wait("tensor", V, V.n if h > 0 else growV)
            s.op("tensor", lambda e, h=h: e.matmul(
                pb0[:], ones1b[:], growb[0:1, 512 * h:512 * (h + 1)],
                start=True, stop=True), T)
            s.wait("vector", T, T.n)
            s.op("vector", lambda e, h=h: e.tensor_mul(
                gt1[:], oT[:, 512 * h:512 * (h + 1)], szT[:, 512 * h:512 * (h + 1)]), V)
            s.wait("vector", V, V.n)
            s.op("vector", lambda e: e.tensor_mul(gt1[:], gt1[:], pb0[:]), V)
            s.wait("vector", V, V.n)
            s.op("vector", lambda e, h=h: e.tensor_scalar(
                yslab[:, 512 * h:512 * (h + 1)], gt1[:], normw[:], None, ALU.mult), V)
        yV = V.n
        s.wait("gpsimd", V, yV)
        for h in range(4):
            s.op("gpsimd", lambda e, h=h, b=b: e.dma_start(
                out=H["yT_sh"][512 * (b // 4) + 128 * h: 512 * (b // 4) + 128 * (h + 1),
                               512 * (b % 4): 512 * (b % 4 + 1)],
                in_=yslab[:, 512 * h:512 * (h + 1)]), ST, 16)
        prevST = ST.n
        prevProjT = T.n
        prevHaloG = G.n
        prevAEnd = A.n
        prevRV = max(prevRV, yV)

    s.wait("gpsimd", ST, ST.n)
    if H.get("y_a2a") is not None:
        a2a_out, ccch = H["y_a2a"]
        s.op("gpsimd", lambda e, a2a_out=a2a_out: e.collective_compute(
            "AllToAll", ALU.bypass, replica_groups=RG8,
            ins=[H["yT_sh"][:, :].opt()], outs=[a2a_out[:, :].opt()]), ccch, 1)



def make_consts(nc):
    ident_b = nc.inline_tensor(np.eye(128, dtype=ml_dtypes.bfloat16), "c_identb")
    ident_f = nc.inline_tensor(np.eye(128, dtype=np.float32), "c_identf")
    ut_f = nc.inline_tensor(np.triu(np.ones((128, 128), np.float32)), "c_utf")
    isl_f = nc.inline_tensor(np.tril(np.ones((128, 128), np.float32), -1), "c_islf")
    ilc_f = nc.inline_tensor(np.triu(np.ones((128, 128), np.float32), 0), "c_ilcf")
    sel4 = np.zeros((4, 512), np.float32)
    for h in range(4):
        sel4[h, 128 * h:128 * (h + 1)] = 1.0
    sel4f = nc.inline_tensor(sel4, "c_sel4f")
    sel12 = np.zeros((12, 1536), np.float32)
    for q in range(3):
        for h in range(4):
            sel12[4 * q + h, 512 * q + 128 * h: 512 * q + 128 * (h + 1)] = 1.0
    sel12b = nc.inline_tensor(sel12.astype(ml_dtypes.bfloat16), "c_sel12b")
    cm = np.zeros((128, 4 * 512), np.float32)
    for tt in range(4):
        for pp2 in range(128):
            cm[pp2, 512 * tt: 512 * tt + 128 * tt + pp2 + 1] = 1.0
    cmask4 = nc.inline_tensor(cm.astype(ml_dtypes.bfloat16), "c_cmask4")
    s2 = np.zeros((2, 256), np.float32)
    s2[0, 0:128] = 1.0
    s2[1, 128:256] = 1.0
    sel2 = nc.inline_tensor(s2.astype(ml_dtypes.bfloat16), "c_sel2")
    return dict(identb=ident_b.ap(), identf=ident_f.ap(), utones_f=ut_f.ap(),
                isl_f=isl_f.ap(), ilc_f=ilc_f.ap(), sel4f=sel4f.ap(),
                sel12b=sel12b.ap(), cmask4=cmask4.ap(), sel2=sel2.ap())


def build_ph0_test():
    nc = bass.Bass("TRN2", target_bir_lowering=False)
    es = ExitStack()
    p = Prog(nc, es)
    H = make_consts(nc)
    x_rows = nc.dram_tensor("x_rows", [TPC, D], F32, kind="ExternalInput").ap()
    ln0T_sh = nc.dram_tensor("ln0T_sh", [D, TPC], BF16, kind="ExternalOutput").ap()
    with es:
        block = es.enter_context(nc.Block())
        emit_ph0(p, block, x_rows, ln0T_sh, H["identb"], ag=None)
    return nc


def build_ph1_test(dbg=True):
    nc = bass.Bass("TRN2", target_bir_lowering=False)
    es = ExitStack()
    p = Prog(nc, es)
    H = make_consts(nc)
    H["ln0T_ag"] = nc.dram_tensor("ln0T_ag", [NCORE * D, TPC], BF16, kind="ExternalInput").ap()
    H["wqkv_s"] = nc.dram_tensor("wqkv_s", [D, 1024], F32, kind="ExternalInput").ap()
    H["wz_s"] = nc.dram_tensor("wz_s", [D, 512], F32, kind="ExternalInput").ap()
    H["wab_s"] = nc.dram_tensor("wab_s", [D, 8], F32, kind="ExternalInput").ap()
    H["conv_s"] = nc.dram_tensor("conv_s", [1024, 4], F32, kind="ExternalInput").ap()
    H["alog_s"] = nc.dram_tensor("alog_s", [1, 4], F32, kind="ExternalInput").ap()
    H["dtb_s"] = nc.dram_tensor("dtb_s", [1, 4], F32, kind="ExternalInput").ap()
    H["normw_s"] = nc.dram_tensor("normw_s", [128, 1], F32, kind="ExternalInput").ap()
    H["yT_sh"] = nc.dram_tensor("yT_sh", [4096, TPC], BF16, kind="ExternalOutput").ap()
    H["y_a2a"] = None
    H["dbg"] = None if not dbg else {
        "qkT": nc.dram_tensor("dbg_qkT", [128, 2048], BF16, kind="ExternalOutput").ap(),
        "vT": nc.dram_tensor("dbg_vT", [128, 2048], BF16, kind="ExternalOutput").ap(),
        "szT": nc.dram_tensor("dbg_szT", [128, 2048], BF16, kind="ExternalOutput").ap(),
        "pk": nc.dram_tensor("dbg_pk", [128, 80], F32, kind="ExternalOutput").ap(),
        "oT": nc.dram_tensor("dbg_oT", [128, 2048], F32, kind="ExternalOutput").ap(),
        "W4": nc.dram_tensor("dbg_W4", [128, 512], F32, kind="ExternalOutput").ap(),
        "D4": nc.dram_tensor("dbg_D4", [128, 512], BF16, kind="ExternalOutput").ap(),
        "DT4": nc.dram_tensor("dbg_DT4", [128, 512], BF16, kind="ExternalOutput").ap(),
        "N0": nc.dram_tensor("dbg_N0", [128, 512], BF16, kind="ExternalOutput").ap(),
        "NT0": nc.dram_tensor("dbg_NT0", [128, 512], BF16, kind="ExternalOutput").ap(),
        "GT4": nc.dram_tensor("dbg_GT4", [128, 512], BF16, kind="ExternalOutput").ap(),
        "X": nc.dram_tensor("dbg_X", [128, 512], BF16, kind="ExternalOutput").ap(),
        "Rb": nc.dram_tensor("dbg_Rb", [128, 512], BF16, kind="ExternalOutput").ap(),
        "Kt4": nc.dram_tensor("dbg_Kt4", [128, 512], BF16, kind="ExternalOutput").ap(),
        "Qt4": nc.dram_tensor("dbg_Qt4", [128, 512], BF16, kind="ExternalOutput").ap(),
        "pb5": nc.dram_tensor("dbg_pb5", [128, 512], F32, kind="ExternalOutput").ap(),
        "rows12": nc.dram_tensor("dbg_rows12", [12, 128], BF16, kind="ExternalOutput").ap(),
        "Vtm": nc.dram_tensor("dbg_Vtm", [128, 512], BF16, kind="ExternalOutput").ap(),
        "NT1": nc.dram_tensor("dbg_NT1", [128, 512], BF16, kind="ExternalOutput").ap(),
        "NT2": nc.dram_tensor("dbg_NT2", [128, 512], BF16, kind="ExternalOutput").ap(),
        "NT6": nc.dram_tensor("dbg_NT6", [128, 512], BF16, kind="ExternalOutput").ap(),
    }
    with es:
        sched = Sched(p, "p1")
        block = es.enter_context(nc.Block())
        emit_ph1(p, sched, H)
        sched.emit(block)
    return nc



def emit_ph1b(p, s, H):
    p.push_scope()
    rLD = s.ch("R1LD")
    rLW = s.ch("R1LW")
    rV = s.ch("R1V")
    rT = s.ch("R1T")
    rST = s.ch("R1ST")
    CCR1 = s.ch("CCR1")
    wldb = p.sbuf("r1_wld", [128, 2048], F32)
    wob1 = p.sbuf("r1_wob", [128, 4 * 2048], BF16)
    ytb = [p.sbuf(f"r1_yt{i}", [128, 4 * 512], BF16) for i in range(2)]
    attp = [p.sbuf(f"r1_at{i}", [128, 2048], F32) for i in range(2)]
    pr = [p.psum(f"r1_p{i}", [128, 512], F32) for i in range(2)]

    for h in range(4):
        s.wait("sync", rV, rV.n)
        s.op("sync", lambda e, h=h: e.dma_start(
            out=wldb[:], in_=H["wout_s"][128 * h:128 * (h + 1), :]), rLW, 16)
        s.wait("vector", rLW, rLW.n)
        s.op("vector", lambda e, h=h: e.tensor_copy(
            wob1[:, 2048 * h:2048 * (h + 1)], wldb[:]), rV)
    wcast = rV.n

    evT = {}
    stDone = {0: 0, 1: 0}
    for blk in range(32):
        j, m = blk // 4, blk % 4
        pp = blk % 2
        s.wait("sync", rT, evT.get(blk - 2, 0))
        for h in range(4):
            s.op("sync", lambda e, j=j, m=m, h=h, pp=pp: e.dma_start(
                out=ytb[pp][:, 512 * h:512 * (h + 1)],
                in_=H["yT_sh"][512 * j + 128 * h: 512 * j + 128 * (h + 1),
                               512 * m:512 * (m + 1)]), rLD, 16)
        s.wait("tensor", rLD, rLD.n)
        s.wait("tensor", rV, wcast)
        for tt in range(4):
            tb = tt % 2
            s.wait("vector", rST, stDone[tb])
            for db in range(4):
                bank = pr[db % 2]
                s.wait("tensor", rV, rV.n)
                for h in range(4):
                    s.op("tensor", lambda e, h=h, tt=tt, db=db, pp=pp, bank=bank: e.matmul(
                        bank[:], ytb[pp][:, 512 * h + 128 * tt: 512 * h + 128 * (tt + 1)],
                        wob1[:, 2048 * h + 512 * db: 2048 * h + 512 * (db + 1)],
                        start=(h == 0), stop=(h == 3)), rT)
                s.wait("vector", rT, rT.n)
                s.op("vector", lambda e, db=db, tb=tb, bank=bank: e.tensor_copy(
                    attp[tb][:, 512 * db:512 * (db + 1)], bank[:]), rV)
            s.wait("gpsimd", rV, rV.n)
            s.op("gpsimd", lambda e, j=j, m=m, tt=tt, tb=tb: e.dma_start(
                out=H["rs_y"][2048 * j + 512 * m + 128 * tt:
                              2048 * j + 512 * m + 128 * (tt + 1), :],
                in_=attp[tb][:]), rST, 16)
            stDone[tb] = rST.n
        evT[blk] = rT.n
    s.wait("gpsimd", rST, rST.n)
    s.op("gpsimd", lambda e: e.collective_compute(
        "ReduceScatter", ALU.add, replica_groups=RG8,
        ins=[H["rs_y"][:, :].opt()], outs=[H["att_dram"][:, :].opt()]), CCR1, 1)
    p.pop_scope()
    return CCR1



def emit_y2sel(p, s, H, pre=()):
    for (ch, n) in pre:
        for eng in ENGS:
            s.wait(eng, ch, n)
    p.push_scope()
    qLD = s.ch("S2LD")
    qLW = s.ch("S2LW")
    qV = s.ch("S2V")
    qT = s.ch("S2T")
    qST = s.ch("S2ST")
    sldf = p.sbuf("s2_sldf", [128, 256], F32)
    selb = p.sbuf("s2_selb", [128, 16 * 256], BF16)
    y2blk = [p.sbuf(f"s2_yb{i}", [128, D], BF16) for i in range(2)]
    orow = p.sbuf("s2_orow", [128, 2 * D], BF16)
    pq = [p.psum(f"s2_p{i}", [128, 512], F32) for i in range(8)]
    for q in range(16):
        s.wait("sync", qV, qV.n)
        s.op("sync", lambda e, q=q: e.dma_start(
            out=sldf[:], in_=H["ysel_s"][128 * q:128 * (q + 1), :]), qLW, 16)
        s.wait("vector", qLW, qLW.n)
        s.op("vector", lambda e, q=q: e.tensor_copy(
            selb[:, 256 * q:256 * (q + 1)], sldf[:]), qV)
    scast = qV.n
    evR = {}
    stR = {}
    for r in range(8):
        s.wait("tensor", qV, scast)
        s.wait("tensor", qV, evR.get(r - 1, 0))
        for q in range(16):
            pp = q % 2
            if q >= 2:
                s.wait("sync", qT, stR.get((r, q - 2), 0))
            elif r > 0:
                s.wait("sync", qT, stR.get((r - 1, 14 + q), 0))
            s.op("sync", lambda e, r=r, q=q, pp=pp: e.dma_start(
                out=y2blk[pp][:],
                in_=H["y2_ag"][2048 * r + 128 * q: 2048 * r + 128 * (q + 1), :]), qLD, 16)
            s.wait("tensor", qLD, qLD.n)
            for half in range(2):
                for ct in range(4):
                    s.op("tensor", lambda e, q=q, half=half, ct=ct, pp=pp: e.matmul(
                        pq[4 * half + ct][:],
                        selb[:, 256 * q + 128 * half: 256 * q + 128 * (half + 1)],
                        y2blk[pp][:, 512 * ct:512 * (ct + 1)],
                        start=(q == 0), stop=(q == 15)), qT)
            stR[(r, q)] = qT.n
        s.wait("vector", qT, qT.n)
        s.wait("vector", qST, qST.n)
        for half in range(2):
            for ct in range(4):
                s.op("vector", lambda e, half=half, ct=ct: e.tensor_copy(
                    orow[:, 2048 * half + 512 * ct: 2048 * half + 512 * (ct + 1)],
                    pq[4 * half + ct][:]), qV)
        evR[r] = qV.n
        s.wait("gpsimd", qV, qV.n)
        for half in range(2):
            s.op("gpsimd", lambda e, r=r, half=half: e.dma_start(
                out=H["y2_loc"][256 * r + 128 * half: 256 * r + 128 * (half + 1), :],
                in_=orow[:, 2048 * half:2048 * (half + 1)]), qST, 16)
    s.wait("gpsimd", qST, qST.n)
    p.pop_scope()
    return qST



def emit_prologue(p, s, H, f32_ags=("wout", "wo", "core", "down", "up", "sh1", "sh2")):
    nc = p.nc
    p.push_scope()
    PCC = s.ch("PCC")
    PLD = s.ch("PLD")
    PV = s.ch("PV")
    PT = s.ch("PT")
    PE_ = s.ch("PEv")
    PST = s.ch("PST")

    PGD = s.ch("PGD")
    for nm in f32_ags:
        s.op("gpsimd", lambda e, nm=nm: e.dma_start(
            out=H[nm + "_bnc"], in_=H[nm + "_s"]), PGD, 16)
    s.wait("gpsimd", PGD, PGD.n)
    for nm in f32_ags:
        s.op("gpsimd", lambda e, nm=nm: e.collective_compute(
            "AllGather", ALU.bypass, replica_groups=RG8,
            ins=[H[nm + "_bnc"][:, :].opt()], outs=[H[nm + "_ag"][:, :].opt()]), PCC, 1)

    wslab = p.sbuf("pr_wslab", [128, 8192], F32)
    wslabb = p.sbuf("pr_wslabb", [128, 8192], BF16)
    tslab = p.sbuf("pr_tslab", [128, 2 * 8192], BF16)
    sidb = p.sbuf("pr_sidb", [128, 128], BF16)
    h128 = p.sbuf("pr_h128", [128, 128], BF16)
    h64b = p.sbuf("pr_h64b", [128, 128], BF16)
    ptp = p.psum("pr_ptp", [128, 512], BF16)

    PLC = s.ch("PLC")
    s.op("sync", lambda e: e.dma_start(out=sidb[:], in_=H["identb"]), PLC, 16)
    s.op("sync", lambda e: e.dma_start(out=h128[:], in_=H["h128pm"]), PLC, 16)
    s.op("sync", lambda e: e.dma_start(out=h64b[:], in_=H["h64blk"]), PLC, 16)
    cstP = PLC.n

    for wname, shname in (("W1_s", "w1t_sh"), ("W3_s", "w3t_sh")):
        for hb in range(64):
            s.wait("sync", PV, PV.n)
            s.op("sync", lambda e, hb=hb, wname=wname: e.dma_start(
                out=wslab[:, 0:256], in_=H[wname][128 * hb:(hb + 1) * 128, :]), PLD, 16)
            s.wait("vector", PLD, PLD.n)
            s.wait("vector", PT, PT.n)
            s.op("vector", lambda e: e.tensor_copy(wslabb[:, 0:256], wslab[:, 0:256]), PV)
            s.wait("tensor", PV, PV.n)
            s.wait("tensor", PLC, cstP)
            s.wait("tensor", PE_, PE_.n)
            for dh in range(2):
                s.op("tensor", lambda e, dh=dh: e.transpose(
                    ptp[:, 128 * dh:128 * (dh + 1)], wslabb[:, 128 * dh:128 * (dh + 1)],
                    sidb[:]), PT)
            s.wait("scalar", PT, PT.n)
            for dh in range(2):
                s.op("scalar", lambda e, dh=dh, hb=hb: e.copy(
                    tslab[:, 8192 * dh + 128 * hb: 8192 * dh + 128 * (hb + 1)],
                    ptp[:, 128 * dh:128 * (dh + 1)]), PE_)
        s.wait("gpsimd", PE_, PE_.n)
        for dh in range(2):
            s.op("gpsimd", lambda e, dh=dh, shname=shname: e.dma_start(
                out=H[shname][128 * dh:128 * (dh + 1), :],
                in_=tslab[:, 8192 * dh:8192 * (dh + 1)]), PST, 16)
        s.wait("gpsimd", PST, PST.n)
        s.op("gpsimd", lambda e, shname=shname: e.collective_compute(
            "AllGather", ALU.bypass, replica_groups=RG8,
            ins=[H[shname][:, :].opt()], outs=[H[shname[:-3] + "_ag"][:, :].opt()]), PCC, 1)
        s.wait("scalar", PST, PST.n)

    for dh in range(2):
        s.wait("sync", PV, PV.n)
        s.op("sync", lambda e, dh=dh: e.dma_start(
            out=wslab[:], in_=H["W2_s"][128 * dh:128 * (dh + 1), :]), PLD, 16)
        s.wait("vector", PLD, PLD.n)
        s.wait("vector", PT, PT.n)
        s.op("vector", lambda e: e.tensor_copy(wslabb[:], wslab[:]), PV)
        s.wait("tensor", PV, PV.n)
        for hb in range(64):
            s.wait("tensor", PE_, PE_.n)
            s.op("tensor", lambda e, hb=hb: e.transpose(
                ptp[:, 128 * (hb % 2):128 * (hb % 2 + 1)],
                wslabb[:, 128 * hb:128 * (hb + 1)], sidb[:]), PT)
            s.wait("scalar", PT, PT.n)
            s.op("scalar", lambda e, hb=hb, dh=dh: e.copy(
                tslab[:, 8192 * 0 + 0:0 + 128] if False else
                tslab[:, 128 * (hb % 4):128 * (hb % 4 + 1)],
                ptp[:, 128 * (hb % 2):128 * (hb % 2 + 1)]), PE_)
            s.wait("gpsimd", PE_, PE_.n)
            s.op("gpsimd", lambda e, hb=hb, dh=dh: e.dma_start(
                out=H["w2t_dram"][128 * hb:128 * (hb + 1), 128 * dh:128 * (dh + 1)],
                in_=tslab[:, 128 * (hb % 4):128 * (hb % 4 + 1)]), PST, 16)
            if hb % 4 == 3:
                s.wait("scalar", PST, PST.n)
    st1in = p.sbuf("pr_st1in", [128, 256], BF16)
    st1out = p.sbuf("pr_st1out", [128, 256], BF16)
    ps1 = p.psum("pr_ps1", [128, 256], F32)
    s.wait("sync", PST, PST.n)
    for pr in range(64):
        s.wait("sync", PV, PV.n if pr > 0 else PV.n)
        s.op("sync", lambda e, pr=pr: e.dma_start(
            out=st1in[:], in_=H["w2t_dram"][128 * pr:128 * (pr + 1), :]), PLD, 16)
        s.wait("tensor", PLD, PLD.n)
        s.wait("tensor", PV, PV.n)
        s.op("tensor", lambda e: e.matmul(ps1[:], h64b[:], st1in[:],
                                          start=True, stop=True), PT)
        s.wait("vector", PT, PT.n)
        s.op("vector", lambda e: e.tensor_copy(st1out[:], ps1[:]), PV)
        s.wait("gpsimd", PV, PV.n)
        s.op("gpsimd", lambda e, pr=pr: e.dma_start(
            out=H["s1_dram"][128 * pr:128 * (pr + 1), :], in_=st1out[:]), PST, 16)
        s.wait("sync", PST, PST.n)
    s1v = H["s1_dram"].rearrange("(j i) d -> i j d", i=64)
    w2fv = H["w2f_sh"].rearrange("(j i) d -> i j d", i=64)
    for i in range(64):
        s.wait("sync", PV, PV.n)
        s.op("sync", lambda e, i=i: e.dma_start(out=st1in[:], in_=s1v[i]), PLD, 16)
        s.wait("tensor", PLD, PLD.n)
        s.wait("tensor", PV, PV.n)
        s.op("tensor", lambda e: e.matmul(ps1[:], h128[:], st1in[:],
                                          start=True, stop=True), PT)
        s.wait("scalar", PT, PT.n)
        s.op("scalar", lambda e: e.activation(st1out[:], ps1[:], AF.Copy,
                                              scale=float(1.0 / np.sqrt(8192.0))), PV)
        s.wait("gpsimd", PV, PV.n)
        s.op("gpsimd", lambda e, i=i: e.dma_start(out=w2fv[i], in_=st1out[:]), PST, 16)
        s.wait("sync", PST, PST.n)
    s.wait("gpsimd", PST, PST.n)
    s.op("gpsimd", lambda e: e.collective_compute(
        "AllGather", ALU.bypass, replica_groups=RG8,
        ins=[H["w2f_sh"][:, :].opt()], outs=[H["w2f_ag"][:, :].opt()]), PCC, 1)
    p.pop_scope()
    return PCC



def emit_ln_rows(s, V, A, src, dst, stat, sq, epsc):
    s.op("vector", lambda e: e.reduce_sum(stat[:, 0:1], src, axis=AX), V)
    s.wait("vector", V, V.n)
    s.op("vector", lambda e: e.tensor_scalar(stat[:, 2:3], stat[:, 0:1], 1.0 / D, None, ALU.mult), V)
    s.wait("scalar", V, V.n)
    s.op("scalar", lambda e: e.activation(sq[:], src, AF.Square, accum_out=stat[:, 1:2]), A)
    s.wait("vector", A, A.n)
    s.op("vector", lambda e: e.tensor_scalar(stat[:, 3:4], stat[:, 1:2], 1.0 / D, None, ALU.mult), V)
    s.op("vector", lambda e: e.tensor_mul(stat[:, 5:6], stat[:, 2:3], stat[:, 2:3]), V)
    s.wait("vector", V, V.n)
    s.op("vector", lambda e: e.tensor_sub(stat[:, 4:5], stat[:, 3:4], stat[:, 5:6]), V)
    s.wait("scalar", V, V.n)
    s.op("scalar", lambda e: e.activation(stat[:, 6:7], stat[:, 4:5], AF.Sqrt, bias=epsc), A)
    s.wait("vector", A, A.n)
    s.op("vector", lambda e: e.reciprocal(stat[:, 7:8], stat[:, 6:7]), V)
    s.wait("vector", V, V.n)
    s.op("vector", lambda e: e.tensor_scalar(dst, src, stat[:, 2:3], stat[:, 7:8],
                                             ALU.subtract, ALU.mult), V)
    return V.n


def emit_ph2(p, s, H):
    for (ch, n) in H.get("pre_waits", []):
        for eng in ENGS:
            s.wait(eng, ch, n)
    LDy = None
    LDX = s.ch("2LDX")
    LDW = s.ch("2LDW")
    V = s.ch("2V")
    A = s.ch("2A")
    T = s.ch("2T")
    ST = s.ch("2ST")
    NT = TPC // 128

    if H.get("skip_pass1"):
        return _emit_ph2_tail(p, s, H, LDX, LDW, V, A, T, ST, NT)

    p.push_scope()
    LDy = [s.ch("2LDa"), s.ch("2LDb")]
    woutb = p.sbuf("p2_woutb", [128, 32 * 2048], BF16)
    wldf = p.sbuf("p2_wldf", [128, 2048], F32)
    ytile = [p.sbuf(f"p2_yt{i}", [128, 32 * 128], BF16) for i in range(2)]
    atb = [p.sbuf(f"p2_atb{i}", [128, D], F32) for i in range(2)]
    pb = [p.psum(f"p2_pb{i}", [128, 512], F32) for i in range(6)]
    for k in range(32):
        s.wait("sync", V, V.n)
        s.op("sync", lambda e, k=k: e.dma_start(
            out=wldf[:], in_=H["wout_ag"][128 * k:128 * (k + 1), :]), LDW, 16)
        s.wait("vector", LDW, LDW.n)
        s.op("vector", lambda e, k=k: e.tensor_copy(
            woutb[:, 2048 * k:2048 * (k + 1)], wldf[:]), V)
    wcastV = V.n
    attT = {}
    evV1 = {}
    for i in range(NT):
        pp = i % 2
        if i >= 2:
            s.wait("sync", T, attT[i - 2])
        for k in range(32):
            s.op("sync", lambda e, k=k, i=i, pp=pp: e.dma_start(
                out=ytile[pp][:, 128 * k:128 * (k + 1)],
                in_=H["yT_loc"][128 * k:128 * (k + 1), 128 * i:128 * (i + 1)]), LDy[pp], 16)
        s.wait("tensor", LDy[pp], LDy[pp].n)
        s.wait("tensor", V, wcastV if i == 0 else evV1.get(i - 1, 0))
        for db in range(4):
            for k in range(32):
                s.op("tensor", lambda e, db=db, k=k, pp=pp: e.matmul(
                    pb[db][:], ytile[pp][:, 128 * k:128 * (k + 1)],
                    woutb[:, 2048 * k + 512 * db: 2048 * k + 512 * (db + 1)],
                    start=(k == 0), stop=(k == 31)), T)
        attT[i] = T.n
        s.wait("vector", T, T.n)
        s.wait("vector", ST, ST.n)
        for db in range(4):
            s.op("vector", lambda e, db=db, pp=pp: e.tensor_copy(
                atb[pp][:, 512 * db:512 * (db + 1)], pb[db][:]), V)
        evV1[i] = V.n
        s.wait("gpsimd", V, V.n)
        s.op("gpsimd", lambda e, i=i, pp=pp: e.dma_start(
            out=H["att_dram"][128 * i:128 * (i + 1), :], in_=atb[pp][:]), ST, 16)
    s.barrier("B12", pre_dma_waits=[("gpsimd", ST, ST.n)])
    p.pop_scope()
    return _emit_ph2_tail(p, s, H, LDX, LDW, V, A, T, ST, NT)


def _emit_ph2_tail(p, s, H, LDX, LDW, V, A, T, ST, NT):
    p.push_scope()
    at1 = p.sbuf("p2_at1", [128, D], F32)
    at2 = p.sbuf("p2_at2", [128, D], F32)
    xrow = p.sbuf("p2_xrow", [128, D], F32)
    x1b = p.sbuf("p2_x1b", [128, D], F32)
    stat = p.sbuf("p2_stat", [128, 8], F32)
    sq = p.sbuf("p2_sq", [128, D], F32)
    epsc = p.sbuf("p2_eps", [128, 1], F32)
    lnb = p.sbuf("p2_lnb", [128, D], BF16)
    sid = p.sbuf("p2_sid", [128, 128], BF16)
    slab = p.sbuf("p2_slab", [128, 16 * 128], BF16)
    ptr = [p.psum(f"p2_ptr{i}", [128, 128], BF16) for i in range(2)]
    s.op("sync", lambda e: e.dma_start(out=sid[:], in_=H["identb"]), LDW, 16)
    s.op("vector", lambda e: e.memset(epsc[:], EPS_LN), V)
    ld2 = s.ch("2LD2")
    for i in range(NT):
        s.wait("sync", V, V.n)
        s.wait("sync", A, A.n)
        s.op("sync", lambda e, i=i: e.dma_start(
            out=at1[:], in_=H["att_dram"][128 * i:128 * (i + 1), :]), ld2, 16)
        s.op("sync", lambda e, i=i: e.dma_start(
            out=xrow[:], in_=H["x_rows"][128 * i:128 * (i + 1), :]), ld2, 16)
        s.wait("vector", ld2, ld2.n)
        s.wait("scalar", ld2, ld2.n)
        emit_ln_rows(s, V, A, at1[:], at2[:], stat, sq, epsc[:])
        if i == 0 and "p2stat" in H:
            s.wait("gpsimd", V, V.n)
            s.op("gpsimd", lambda e: e.dma_start(out=H["p2stat"], in_=stat[:]), ST, 16)
            s.op("gpsimd", lambda e: e.dma_start(out=H["p2at2"], in_=at2[:]), ST, 16)
            s.op("gpsimd", lambda e: e.dma_start(out=H["p2at1"], in_=at1[:]), ST, 16)
            s.op("gpsimd", lambda e: e.dma_start(out=H["p2sq"], in_=sq[:]), ST, 16)
            s.wait("vector", ST, ST.n)
            s.wait("scalar", ST, ST.n)
        s.wait("vector", V, V.n)
        s.op("vector", lambda e: e.tensor_add(x1b[:], at2[:], xrow[:]), V)
        s.wait("vector", V, V.n)
        s.wait("gpsimd", V, V.n)
        s.wait("gpsimd", ST, ST.n)
        s.op("gpsimd", lambda e, i=i: e.dma_start(
            out=H["x1_dram"][128 * i:128 * (i + 1), :], in_=x1b[:]), ST, 16)
        emit_ln_rows(s, V, A, x1b[:], lnb[:], stat, sq, epsc[:])
        lnV = V.n
        s.wait("vector", ST, ST.n)
        s.wait("tensor", V, lnV)
        s.wait("tensor", LDW, LDW.n)
        s.wait("tensor", ST, ST.n)
        for k in range(16):
            if k >= 2:
                s.wait("tensor", V, slV)
            s.op("tensor", lambda e, k=k: e.transpose(
                ptr[k % 2][:], lnb[:, 128 * k:128 * (k + 1)], sid[:]), T)
            s.wait("vector", T, T.n)
            s.op("vector", lambda e, k=k: e.tensor_copy(
                slab[:, 128 * k:128 * (k + 1)], ptr[k % 2][:]), V)
            slV = V.n
        s.wait("gpsimd", V, slV)
        s.op("gpsimd", lambda e, i=i: e.dma_start(
            out=H["ln1T_dram"].rearrange("(k p) t -> p k t", p=128)[:, :, 128 * i:128 * (i + 1)],
            in_=slab[:].rearrange("p (k t) -> p k t", k=16)), ST, 16)
    s.barrier("B23", pre_dma_waits=[("gpsimd", ST, ST.n)])
    p.pop_scope()

    p.push_scope()
    lnTb = p.sbuf("p2_lnTb", [128, 16 * 512], BF16)
    hidT = p.sbuf("p2_hidT", [128, 64 * 512], BF16)
    w1tile = [p.sbuf(f"p2_w1t{i}", [128, 16 * 128], BF16) for i in range(2)]
    w3tile = [p.sbuf(f"p2_w3t{i}", [128, 16 * 128], BF16) for i in range(2)]
    w2tile = [p.sbuf(f"p2_w2t{i}", [128, 512], BF16) for i in range(2)]
    sg = [p.sbuf(f"p2_sg{i}", [128, 512], BF16) for i in range(2)]
    fout = p.sbuf("p2_fout", [128, 4 * D], F32)
    fat1 = p.sbuf("p2_at1b", [128, D], F32)
    fxrow = p.sbuf("p2_xrowb", [128, D], F32)
    x2b = p.sbuf("p2_x2b", [128, D], F32)
    fstat = p.sbuf("p2_statb", [128, 8], F32)
    fsq = p.sbuf("p2_sqb", [128, D], F32)
    fepsc = p.sbuf("p2_epsb", [128, 1], F32)
    flnb = p.sbuf("p2_lnbb", [128, D], BF16)
    fsid = p.sbuf("p2_sidb", [128, 128], BF16)
    fslab = p.sbuf("p2_slabb", [128, 16 * 128], BF16)
    fpb = [p.psum(f"p2_qb{i}", [128, 512], F32) for i in range(6)]
    fptr = [p.psum(f"p2_qtr{i}", [128, 128], BF16) for i in range(2)]
    s.op("sync", lambda e: e.dma_start(out=fsid[:], in_=H["identb"]), LDW, 16)
    s.op("vector", lambda e: e.memset(fepsc[:], EPS_LN), V)
    ld3 = s.ch("2LD3")
    hmmT = 0
    w2mmT = 0
    l2evV = 0
    H["w1t_view"] = H["w1t_ag"].rearrange("(k p) n -> p k n", p=128)
    H["w3t_view"] = H["w3t_ag"].rearrange("(k p) n -> p k n", p=128)
    awA = {}
    evA = {}
    awB = {}
    hidline = H["ln1T_dram"]
    for tb in range(4):
        t0 = 512 * tb
        s.wait("sync", T, hmmT)
        s.op("sync", lambda e, t0=t0: e.dma_start(
            out=lnTb[:].rearrange("p (k t) -> p k t", k=16),
            in_=hidline.rearrange("(k p) t -> p k t", p=128)[:, :, t0:t0 + 512]), ld3, 16)
        lnTb_done = ld3.n
        l2evV_tb = l2evV
        for ht in range(64):
            pp = ht % 2
            q0 = 2 * pp
            g = 64 * tb + ht
            s.wait("sync", T, awA.get(g - 2, 0))
            s.op("sync", lambda e, ht=ht, pp=pp: e.dma_start(
                out=w1tile[pp][:].rearrange("p (k n) -> p k n", k=16),
                in_=H["w1t_view"][:, :, 128 * ht:128 * (ht + 1)]), LDW, 16)
            s.op("sync", lambda e, ht=ht, pp=pp: e.dma_start(
                out=w3tile[pp][:].rearrange("p (k n) -> p k n", k=16),
                in_=H["w3t_view"][:, :, 128 * ht:128 * (ht + 1)]), LDW, 16)
            s.wait("tensor", LDW, LDW.n)
            s.wait("tensor", ld3, lnTb_done)
            s.wait("tensor", V, max(evA.get(g - 2, 0), l2evV_tb))
            for k in range(16):
                s.op("tensor", lambda e, k=k, pp=pp, q0=q0: e.matmul(
                    fpb[q0][:], w1tile[pp][:, 128 * k:128 * (k + 1)],
                    lnTb[:, 512 * k:512 * (k + 1)], start=(k == 0), stop=(k == 15)), T)
            for k in range(16):
                s.op("tensor", lambda e, k=k, pp=pp, q0=q0: e.matmul(
                    fpb[q0 + 1][:], w3tile[pp][:, 128 * k:128 * (k + 1)],
                    lnTb[:, 512 * k:512 * (k + 1)], start=(k == 0), stop=(k == 15)), T)
            hmmT = T.n
            awA[g] = T.n
            s.wait("scalar", T, hmmT)
            s.op("scalar", lambda e, pp=pp, q0=q0: e.activation(
                sg[pp][:], fpb[q0][:], AF.Silu), A)
            s.wait("vector", A, A.n)
            s.op("vector", lambda e, ht=ht, pp=pp, q0=q0: e.tensor_tensor(
                hidT[:, 512 * ht:512 * (ht + 1)], sg[pp][:], fpb[q0 + 1][:], ALU.mult), V)
            evA[g] = V.n
        hidV = V.n
        for db in range(4):
            for h in range(64):
                pp = h % 2
                gb = 256 * tb + 64 * db + h
                s.wait("sync", T, awB.get(gb - 2, 0))
                s.op("sync", lambda e, h=h, db=db, pp=pp: e.dma_start(
                    out=w2tile[pp][:].rearrange("p (r d) -> p r d", r=2),
                    in_=H["w2f_view"][128 * h:128 * (h + 1), 2 * db:2 * db + 2, :]), LDW, 16)
                s.wait("tensor", LDW, LDW.n)
                if h == 0:
                    s.wait("tensor", V, hidV)
                    s.wait("tensor", V, l2evV)
                for ts in range(4):
                    s.op("tensor", lambda e, h=h, ts=ts, pp=pp: e.matmul(
                        fpb[2 + ts][:],
                        hidT[:].rearrange("p (h t) -> p h t", h=64)[:, h, 128 * ts:128 * (ts + 1)],
                        w2tile[pp][:], start=(h == 0), stop=(h == 63)), T)
                w2mmT = T.n
                awB[gb] = T.n
            s.wait("vector", T, w2mmT)
            for ts in range(4):
                s.op("vector", lambda e, ts=ts, db=db: e.tensor_copy(
                    fout[:].rearrange("p (ts d) -> p ts d", ts=4)[:, ts, 512 * db:512 * (db + 1)],
                    fpb[2 + ts][:]), V)
            l2evV = V.n
        for ts in range(4):
            i = 4 * tb + ts
            s.wait("sync", V, V.n)
            s.op("sync", lambda e, i=i: e.dma_start(
                out=fxrow[:], in_=H["x1_dram"][128 * i:128 * (i + 1), :]), ld3, 16)
            fo = fout[:].rearrange("p (ts d) -> p ts d", ts=4)[:, ts, :]
            s.wait("vector", V, l2evV)
            s.wait("vector", ld3, ld3.n)
            emit_ln_rows(s, V, A, fo, fat1[:], fstat, fsq, fepsc[:])
            s.wait("vector", V, V.n)
            s.op("vector", lambda e: e.tensor_add(x2b[:], fat1[:], fxrow[:]), V)
            s.wait("vector", V, V.n)
            s.wait("gpsimd", V, V.n)
            s.wait("gpsimd", ST, ST.n)
            s.op("gpsimd", lambda e, i=i: e.dma_start(
                out=H["x2_dram"][128 * i:128 * (i + 1), :], in_=x2b[:]), ST, 16)
            emit_ln_rows(s, V, A, x2b[:], flnb[:], fstat, fsq, fepsc[:])
            ln2V = V.n
            s.wait("vector", ST, ST.n)
            s.wait("tensor", V, ln2V)
            s.wait("tensor", LDW, LDW.n)
            s.wait("tensor", ST, ST.n)
            for k in range(16):
                if k >= 2:
                    s.wait("tensor", V, slV2)
                s.op("tensor", lambda e, k=k: e.transpose(
                    fptr[k % 2][:], flnb[:, 128 * k:128 * (k + 1)], fsid[:]), T)
                s.wait("vector", T, T.n)
                s.op("vector", lambda e, k=k: e.tensor_copy(
                    fslab[:, 128 * k:128 * (k + 1)], fptr[k % 2][:]), V)
                slV2 = V.n
            s.wait("gpsimd", V, slV2)
            s.op("gpsimd", lambda e, i=i: e.dma_start(
                out=H["ln2T_sh"].rearrange("(k p) t -> p k t", p=128)[:, :, 128 * i:128 * (i + 1)],
                in_=fslab[:].rearrange("p (k t) -> p k t", k=16)), ST, 16)
    s.wait("gpsimd", ST, ST.n)
    p.pop_scope()
    if H.get("ln2_ag") is not None:
        ag_out, ccch = H["ln2_ag"]
        s.op("gpsimd", lambda e, ag_out=ag_out: e.collective_compute(
            "AllGather", ALU.bypass, replica_groups=RG8,
            ins=[H["ln2T_sh"][:, :].opt()], outs=[ag_out[:, :].opt()]), ccch, 1)


def declare_prologue_drams(nc, need=("wout", "wo", "core", "down", "up", "sh1", "sh2",
                                     "w1", "w3", "w2")):
    H = {}
    shapes = dict(wout=(512, 2048), wo=(256, 2048), core=(256, 2048),
                  down=(256, 1024), up=(128, 2048), sh1=(128, 1024), sh2=(128, 1024))
    shapes = {nm: v for nm, v in shapes.items() if nm in need}
    for nm, (r, c) in shapes.items():
        H[nm + "_s"] = nc.dram_tensor(nm + "_s", [r, c], F32, kind="ExternalInput").ap()
        H[nm + "_bnc"] = nc.dram_tensor(nm + "_bnc", [r, c], F32, kind="Internal").ap()
        H[nm + "_ag"] = nc.dram_tensor(nm + "_ag", [8 * r, c], F32, kind="Internal",
                                       addr_space="Shared").ap()
    H["W1_s"] = nc.dram_tensor("W1_s", [8192, 256], F32, kind="ExternalInput").ap()
    H["W3_s"] = nc.dram_tensor("W3_s", [8192, 256], F32, kind="ExternalInput").ap()
    H["W2_s"] = nc.dram_tensor("W2_s", [256, 8192], F32, kind="ExternalInput").ap()
    H["w1t_sh"] = nc.dram_tensor("w1t_sh", [256, 8192], BF16, kind="Internal").ap()
    H["w3t_sh"] = nc.dram_tensor("w3t_sh", [256, 8192], BF16, kind="Internal").ap()
    H["w1t_ag"] = nc.dram_tensor("w1t_ag", [2048, 8192], BF16, kind="Internal",
                                 addr_space="Shared").ap()
    H["w3t_ag"] = nc.dram_tensor("w3t_ag", [2048, 8192], BF16, kind="Internal",
                                 addr_space="Shared").ap()
    H["w2t_dram"] = nc.dram_tensor("w2t_dram", [8192, 256], BF16, kind="Internal").ap()
    H["s1_dram"] = nc.dram_tensor("s1_dram", [8192, 256], BF16, kind="Internal").ap()
    H["w2f_sh"] = nc.dram_tensor("w2f_sh", [8192, 256], BF16, kind="Internal").ap()
    H["w2f_ag"] = nc.dram_tensor("w2f_ag", [65536, 256], BF16, kind="Internal",
                                 addr_space="Shared").ap()
    H["h128pm"] = nc.inline_tensor(hadamard_pm(128).astype(ml_dtypes.bfloat16), "c_h128").ap()
    h64 = hadamard_pm(64)
    blk = np.zeros((128, 128), np.float32)
    blk[:64, :64] = h64
    blk[64:, 64:] = h64
    H["h64blk"] = nc.inline_tensor(blk.astype(ml_dtypes.bfloat16), "c_h64b").ap()
    return H


def slice_prologue_inputs(inp, c):
    return {
        "wout_s": np.ascontiguousarray(inp["L0_attn_Wout"][512 * c:512 * (c + 1), :]),
        "wo_s": np.ascontiguousarray(inp["L1_attn_Wo"][256 * c:256 * (c + 1), :]),
        "core_s": np.ascontiguousarray(inp["L1_ffn_core_w"][256 * c:256 * (c + 1), :]),
        "down_s": np.ascontiguousarray(inp["L1_ffn_down_w"][256 * c:256 * (c + 1), :]),
        "up_s": np.ascontiguousarray(inp["L1_ffn_up_w"][128 * c:128 * (c + 1), :]),
        "sh1_s": np.ascontiguousarray(inp["L1_ffn_shared_w1"][128 * c:128 * (c + 1), :]),
        "sh2_s": np.ascontiguousarray(inp["L1_ffn_shared_w2"][128 * c:128 * (c + 1), :]),
        "W1_s": np.ascontiguousarray(inp["L0_ffn_W1"][:, 256 * c:256 * (c + 1)]),
        "W3_s": np.ascontiguousarray(inp["L0_ffn_W3"][:, 256 * c:256 * (c + 1)]),
        "W2_s": np.ascontiguousarray(inp["L0_ffn_W2"][256 * c:256 * (c + 1), :]),
    }


def build_ph2_test(no_prologue=False, skip_pass3=False, lean=False):
    nc = bass.Bass("TRN2", target_bir_lowering=False)
    es = ExitStack()
    p = Prog(nc, es)
    H = make_consts(nc)
    ik = "Internal" if lean else "ExternalOutput"
    if no_prologue:
        H["wout_ag"] = nc.dram_tensor("wout_ag", [4096, 2048], F32, kind="ExternalInput").ap()
        H["w1t_ag"] = nc.dram_tensor("w1t_ag", [2048, 8192], BF16, kind="ExternalInput").ap()
        H["w3t_ag"] = nc.dram_tensor("w3t_ag", [2048, 8192], BF16, kind="ExternalInput").ap()
        H["w2f_ag"] = nc.dram_tensor("w2f_ag", [65536, 256], BF16, kind="ExternalInput").ap()
    else:
        H.update(declare_prologue_drams(nc))
    H["yT_loc"] = nc.dram_tensor("yT_loc", [4096, TPC], BF16, kind="ExternalInput").ap()
    H["x_rows"] = nc.dram_tensor("x_rows", [TPC, D], F32, kind="ExternalInput").ap()
    H["x1_dram"] = nc.dram_tensor("x1_dram", [TPC, D], F32, kind=ik).ap()
    H["att_dram"] = nc.dram_tensor("att_dram", [TPC, D], F32, kind=ik).ap()
    H["ln1T_dram"] = nc.dram_tensor("ln1T_dram", [D, TPC], BF16, kind=ik).ap()
    H["x2_dram"] = nc.dram_tensor("x2_dram", [TPC, D], F32, kind="ExternalOutput").ap()
    H["ln2T_sh"] = nc.dram_tensor("ln2T_sh", [D, TPC], BF16, kind="ExternalOutput").ap()
    if not lean:
        H["p2stat"] = nc.dram_tensor("p2stat", [128, 8], F32, kind="ExternalOutput").ap()
        H["p2at2"] = nc.dram_tensor("p2at2", [128, D], F32, kind="ExternalOutput").ap()
        H["p2at1"] = nc.dram_tensor("p2at1", [128, D], F32, kind="ExternalOutput").ap()
        H["p2sq"] = nc.dram_tensor("p2sq", [128, D], F32, kind="ExternalOutput").ap()
    H["ln2_ag"] = None
    with es:
        sched = Sched(p, "pp")
        block = es.enter_context(nc.Block())
        if not no_prologue:
            PCC = emit_prologue(p, sched, H)
            H["pre_waits"] = [(PCC, 10)]
        H["w2f_view"] = H["w2f_ag"].rearrange("(r h) d -> h r d", r=8)
        H["skip_pass3"] = skip_pass3
        emit_ph2(p, sched, H)
        sched.emit(block)
    return nc


def build_pass2_test(fake1=0):
    nc = bass.Bass("TRN2", target_bir_lowering=False)
    es = ExitStack()
    p = Prog(nc, es)
    H = make_consts(nc)
    H["wout_ag"] = nc.dram_tensor("wout_ag", [4096, 2048], F32, kind="ExternalInput").ap()
    H["att_dram"] = nc.dram_tensor("att_in", [TPC, D], F32, kind="ExternalInput").ap()
    H["att_dram2"] = nc.dram_tensor("att_d2", [TPC, D], F32, kind="ExternalOutput").ap()
    H["yT_loc"] = nc.dram_tensor("yT_loc", [4096, TPC], BF16, kind="ExternalInput").ap()
    H["x_rows"] = nc.dram_tensor("x_rows", [TPC, D], F32, kind="ExternalInput").ap()
    H["x1_dram"] = nc.dram_tensor("x1_dram", [TPC, D], F32, kind="ExternalOutput").ap()
    H["ln1T_dram"] = nc.dram_tensor("ln1T_dram", [D, TPC], BF16, kind="ExternalOutput").ap()
    H["p2stat"] = nc.dram_tensor("p2stat", [128, 8], F32, kind="ExternalOutput").ap()
    H["p2at2"] = nc.dram_tensor("p2at2", [128, D], F32, kind="ExternalOutput").ap()
    H["p2at1"] = nc.dram_tensor("p2at1", [128, D], F32, kind="ExternalOutput").ap()
    H["p2sq"] = nc.dram_tensor("p2sq", [128, D], F32, kind="ExternalOutput").ap()
    with es:
        s = Sched(p, "q2")
        block = es.enter_context(nc.Block())
        LDW = s.ch("2LDW")
        V = s.ch("2V")
        A = s.ch("2A")
        T = s.ch("2T")
        ST = s.ch("2ST")
        if fake1 >= 1:
            p.push_scope()
            woutb = p.sbuf("p2_woutb", [128, 32 * 2048], BF16)
            wldf = p.sbuf("p2_wldf", [128, 2048], F32)
            for k in range(32):
                s.wait("sync", V, V.n)
                s.op("sync", lambda e, k=k: e.dma_start(
                    out=wldf[:], in_=H["wout_ag"][128 * k:128 * (k + 1), :]), LDW, 16)
                s.wait("vector", LDW, LDW.n)
                s.op("vector", lambda e, k=k: e.tensor_copy(
                    woutb[:, 2048 * k:2048 * (k + 1)], wldf[:]), V)
            if fake1 >= 2:
                LDy = [s.ch("2LDa"), s.ch("2LDb")]
                ytile = [p.sbuf(f"p2_yt{i}", [128, 32 * 128], BF16) for i in range(2)]
                atb = [p.sbuf(f"p2_atb{i}", [128, D], F32) for i in range(2)]
                pb = [p.psum(f"p2_pb{i}", [128, 512], F32) for i in range(6)]
                wcastV = V.n
                attT = {}
                evV1 = {}
                NTf = TPC // 128
                for i in range(NTf):
                    pp = i % 2
                    if i >= 2:
                        s.wait("sync", T, attT[i - 2])
                    for k in range(32):
                        s.op("sync", lambda e, k=k, i=i, pp=pp: e.dma_start(
                            out=ytile[pp][:, 128 * k:128 * (k + 1)],
                            in_=H["yT_loc"][128 * k:128 * (k + 1), 128 * i:128 * (i + 1)]), LDy[pp], 16)
                    s.wait("tensor", LDy[pp], LDy[pp].n)
                    s.wait("tensor", V, wcastV if i == 0 else evV1.get(i - 1, 0))
                    for db in range(4):
                        for k in range(32):
                            s.op("tensor", lambda e, db=db, k=k, pp=pp: e.matmul(
                                pb[db][:], ytile[pp][:, 128 * k:128 * (k + 1)],
                                woutb[:, 2048 * k + 512 * db: 2048 * k + 512 * (db + 1)],
                                start=(k == 0), stop=(k == 31)), T)
                    attT[i] = T.n
                    s.wait("vector", T, T.n)
                    s.wait("vector", ST, ST.n)
                    for db in range(4):
                        s.op("vector", lambda e, db=db, pp=pp: e.tensor_copy(
                            atb[pp][:, 512 * db:512 * (db + 1)], pb[db][:]), V)
                    evV1[i] = V.n
                    s.wait("gpsimd", V, V.n)
                    s.op("gpsimd", lambda e, i=i, pp=pp: e.dma_start(
                        out=H["att_dram2"][128 * i:128 * (i + 1), :], in_=atb[pp][:]), ST, 16)
            s.wait("vector", ST, ST.n)
            s.wait("tensor", ST, ST.n)
            s.wait("scalar", ST, ST.n)
            s.wait("sync", ST, ST.n)
            s.wait("gpsimd", ST, ST.n)
            s.wait("vector", V, V.n)
            s.wait("sync", V, V.n)
            s.wait("scalar", V, V.n)
            s.wait("tensor", V, V.n)
            s.wait("gpsimd", V, V.n)
            p.pop_scope()
        p.push_scope()
        if fake1 >= 3:
            H["att_dram"] = H["att_dram2"]
        at1 = p.sbuf("p2_at1", [128, D], F32)
        at2 = p.sbuf("p2_at2", [128, D], F32)
        xrow = p.sbuf("p2_xrow", [128, D], F32)
        x1b = p.sbuf("p2_x1b", [128, D], F32)
        stat = p.sbuf("p2_stat", [128, 8], F32)
        sq = p.sbuf("p2_sq", [128, D], F32)
        epsc = p.sbuf("p2_eps", [128, 1], F32)
        lnb = p.sbuf("p2_lnb", [128, D], BF16)
        sid = p.sbuf("p2_sid", [128, 128], BF16)
        slab = p.sbuf("p2_slab", [128, 16 * 128], BF16)
        ptr = [p.psum(f"p2_ptr{i}", [128, 128], BF16) for i in range(2)]
        s.op("sync", lambda e: e.dma_start(out=sid[:], in_=H["identb"]), LDW, 16)
        s.op("vector", lambda e: e.memset(epsc[:], EPS_LN), V)
        ld2 = s.ch("2LD2")
        NT = TPC // 128
        for i in range(NT):
            s.wait("sync", V, V.n)
            s.wait("sync", A, A.n)
            s.op("sync", lambda e, i=i: e.dma_start(
                out=at1[:], in_=H["att_dram"][128 * i:128 * (i + 1), :]), ld2, 16)
            s.op("sync", lambda e, i=i: e.dma_start(
                out=xrow[:], in_=H["x_rows"][128 * i:128 * (i + 1), :]), ld2, 16)
            s.wait("vector", ld2, ld2.n)
            s.wait("scalar", ld2, ld2.n)
            emit_ln_rows(s, V, A, at1[:], at2[:], stat, sq, epsc[:])
            if i == 0:
                s.wait("gpsimd", V, V.n)
                s.op("gpsimd", lambda e: e.dma_start(out=H["p2stat"], in_=stat[:]), ST, 16)
                s.op("gpsimd", lambda e: e.dma_start(out=H["p2at2"], in_=at2[:]), ST, 16)
                s.op("gpsimd", lambda e: e.dma_start(out=H["p2at1"], in_=at1[:]), ST, 16)
                s.op("gpsimd", lambda e: e.dma_start(out=H["p2sq"], in_=sq[:]), ST, 16)
                s.wait("vector", ST, ST.n)
                s.wait("scalar", ST, ST.n)
            s.wait("vector", V, V.n)
            s.op("vector", lambda e: e.tensor_add(x1b[:], at2[:], xrow[:]), V)
            s.wait("vector", V, V.n)
            s.wait("gpsimd", V, V.n)
            s.wait("gpsimd", ST, ST.n)
            s.op("gpsimd", lambda e, i=i: e.dma_start(
                out=H["x1_dram"][128 * i:128 * (i + 1), :], in_=x1b[:]), ST, 16)
            emit_ln_rows(s, V, A, x1b[:], lnb[:], stat, sq, epsc[:])
            lnV = V.n
            s.wait("vector", ST, ST.n)
            s.wait("tensor", V, lnV)
            s.wait("tensor", LDW, 16)
            s.wait("tensor", ST, ST.n)
            for k in range(16):
                if k >= 2:
                    s.wait("tensor", V, slV)
                s.op("tensor", lambda e, k=k: e.transpose(
                    ptr[k % 2][:], lnb[:, 128 * k:128 * (k + 1)], sid[:]), T)
                s.wait("vector", T, T.n)
                s.op("vector", lambda e, k=k: e.tensor_copy(
                    slab[:, 128 * k:128 * (k + 1)], ptr[k % 2][:]), V)
                slV = V.n
            s.wait("gpsimd", V, slV)
            s.op("gpsimd", lambda e, i=i: e.dma_start(
                out=H["ln1T_dram"].rearrange("(k p) t -> p k t", p=128)[:, :, 128 * i:128 * (i + 1)],
                in_=slab[:].rearrange("p (k t) -> p k t", k=16)), ST, 16)
        s.wait("gpsimd", ST, ST.n)
        p.pop_scope()
        s.emit(block)
    return nc



def emit_ph3(p, s, H):
    for (ch, n) in H.get("pre_waits3", []):
        for eng in ENGS:
            s.wait(eng, ch, n)
    p.push_scope()
    gLD = s.ch("3LD")
    gLDW = s.ch("3LDW")
    gV = s.ch("3V")
    gA = s.ch("3A")
    gT = s.ch("3T")
    gST = s.ch("3ST")

    wqb = p.sbuf("p3_wqb", [128, 16 * 256], BF16)
    wkb = p.sbuf("p3_wkb", [128, 16 * 256], BF16)
    wvb = p.sbuf("p3_wvb", [128, 16 * 256], BF16)
    wldf3 = p.sbuf("p3_wldf", [128, 256], F32)
    lnb3 = p.sbuf("p3_lnb", [128, 16 * 512], BF16)
    qT3 = p.sbuf("p3_qT", [128, 2 * 512], BF16)
    kT3 = p.sbuf("p3_kT", [128, 2 * 512], BF16)
    vT3 = p.sbuf("p3_vT", [128, 2 * 512], BF16)
    sqT3 = p.sbuf("p3_sqT", [128, 2 * 512], BF16)
    skT3 = p.sbuf("p3_skT", [128, 2 * 512], BF16)
    el1 = p.sbuf("p3_el1", [128, 512], F32)
    el2 = p.sbuf("p3_el2", [128, 512], F32)
    Vtm3 = p.sbuf("p3_Vtm", [128, 2 * 512], BF16)
    sktm3 = p.sbuf("p3_sktm", [128, 512], BF16)
    Aexp = p.sbuf("p3_Aexp", [128, 4 * 512], BF16)
    rmax = p.sbuf("p3_rmax", [128, 8], F32)
    rsum = p.sbuf("p3_rsum", [128, 8], F32)
    mem_f = [p.sbuf(f"p3_memf{h}", [128, 128], F32) for h in range(2)]
    mem_b = [p.sbuf(f"p3_memb{h}", [128, 128], BF16) for h in range(2)]
    z_f = [p.sbuf(f"p3_zf{h}", [128, 1], F32) for h in range(2)]
    z_b = [p.sbuf(f"p3_zb{h}", [128, 1], BF16) for h in range(2)]
    beta_t = p.sbuf("p3_betat", [128, 2 * 128], F32)
    betc_t = p.sbuf("p3_betct", [128, 2 * 128], F32)
    bsig2 = p.sbuf("p3_bsig2", [2, 128], F32)
    bsig2b = p.sbuf("p3_bsig2b", [2, 128], BF16)
    sel2b = p.sbuf("p3_sel2b", [2, 256], BF16)
    ones1b3 = p.sbuf("p3_ones1b", [1, 128], BF16)
    onescol3 = p.sbuf("p3_onescol", [128, 1], BF16)
    sid3 = p.sbuf("p3_sid", [128, 128], BF16)
    cmask = p.sbuf("p3_cmask", [128, 4 * 512], BF16)
    dcol = p.sbuf("p3_dcol", [128, 16], F32)
    attn_s = p.sbuf("p3_attn", [128, 4 * 128], BF16)
    dmem = p.sbuf("p3_dmem", [128, 128], BF16)
    dmm = p.sbuf("p3_dmm", [128, 128], F32)
    pA = p.psum("p3_pA", [128, 512], F32)
    pB = p.psum("p3_pB", [128, 512], F32)
    pC = p.psum("p3_pC", [128, 512], F32)
    pD = p.psum("p3_pD", [128, 512], F32)
    pE = p.psum("p3_pE", [128, 512], F32)
    pF = p.psum("p3_pF", [128, 512], BF16)
    pG = p.psum("p3_pG", [128, 512], F32)

    s.op("sync", lambda e: e.dma_start(out=sid3[:], in_=H["identb"]), gLDW, 16)
    s.op("sync", lambda e: e.dma_start(out=cmask[:], in_=H["cmask4"]), gLDW, 16)
    s.op("sync", lambda e: e.dma_start(out=bsig2[:], in_=H["betas_s"]), gLDW, 16)
    s.op("sync", lambda e: e.dma_start(out=sel2b[:], in_=H["sel2"]), gLDW, 16)
    s.op("vector", lambda e: e.memset(ones1b3[:], 1.0), gV)
    s.op("vector", lambda e: e.memset(onescol3[:], 1.0), gV)
    for k3 in range(16):
        s.wait("sync", gV, gV.n)
        s.op("sync", lambda e, k3=k3: e.dma_start(
            out=wldf3[:], in_=H["wq_s"][128 * k3:128 * (k3 + 1), :]), gLDW, 16)
        s.wait("vector", gLDW, gLDW.n)
        s.op("vector", lambda e, k3=k3: e.tensor_copy(wqb[:, 256 * k3:256 * (k3 + 1)], wldf3[:]), gV)
        s.wait("sync", gV, gV.n)
        s.op("sync", lambda e, k3=k3: e.dma_start(
            out=wldf3[:], in_=H["wk_s"][128 * k3:128 * (k3 + 1), :]), gLDW, 16)
        s.wait("vector", gLDW, gLDW.n)
        s.op("vector", lambda e, k3=k3: e.tensor_copy(wkb[:, 256 * k3:256 * (k3 + 1)], wldf3[:]), gV)
        s.wait("sync", gV, gV.n)
        s.op("sync", lambda e, k3=k3: e.dma_start(
            out=wldf3[:], in_=H["wv_s"][128 * k3:128 * (k3 + 1), :]), gLDW, 16)
        s.wait("vector", gLDW, gLDW.n)
        s.op("vector", lambda e, k3=k3: e.tensor_copy(wvb[:, 256 * k3:256 * (k3 + 1)], wldf3[:]), gV)
    wcast3 = gV.n
    s.wait("scalar", gLDW, gLDW.n)
    s.op("scalar", lambda e: e.activation(bsig2b[:], bsig2[:], AF.Sigmoid), gA)
    s.wait("tensor", gA, gA.n)
    for h3 in range(2):
        s.op("tensor", lambda e, h3=h3: e.matmul(
            pA[:, 128 * h3:128 * (h3 + 1)], sel2b[:, 128 * h3:128 * (h3 + 1)],
            bsig2b[:], start=True, stop=True), gT)
    s.wait("vector", gT, gT.n)
    s.op("vector", lambda e: e.tensor_copy(beta_t[:], pA[:, 0:256]), gV)
    s.wait("vector", gV, gV.n)
    s.op("vector", lambda e: e.tensor_scalar(betc_t[:], beta_t[:], -1.0, 1.0,
                                             ALU.mult, ALU.add), gV)
    binitV = gV.n
    for h3 in range(2):
        s.op("vector", lambda e, h3=h3: e.memset(mem_f[h3][:], 0.0), gV)
        s.op("vector", lambda e, h3=h3: e.memset(mem_b[h3][:], 0.0), gV)
        s.op("vector", lambda e, h3=h3: e.memset(z_f[h3][:], 1.0 / 128.0), gV)
        s.op("vector", lambda e, h3=h3: e.memset(z_b[h3][:], 1.0 / 128.0), gV)
    s.wait("vector", gV, gV.n)

    projT3 = 0
    segVdone = gV.n
    prevStT = 0
    for sg3 in range(int(_os.environ.get("PH3_NSEG", 32))):
        r3, t03 = sg3 // 4, (sg3 % 4) * 512
        s.wait("sync", gT, projT3)
        for k3 in range(16):
            s.op("sync", lambda e, k3=k3, r3=r3, t03=t03: e.dma_start(
                out=lnb3[:, 512 * k3:512 * (k3 + 1)],
                in_=H["ln2T_ag"][2048 * r3 + 128 * k3: 2048 * r3 + 128 * (k3 + 1),
                                 t03:t03 + 512]), gLD, 16)
        ld3done = gLD.n
        s.wait("tensor", gLD, ld3done)
        if sg3 == 0:
            s.wait("tensor", gV, wcast3)
        s.wait("tensor", gV, segVdone)
        for h3 in range(2):
            for k3 in range(16):
                s.op("tensor", lambda e, h3=h3, k3=k3: e.matmul(
                    pG[:], wqb[:, 256 * k3 + 128 * h3: 256 * k3 + 128 * (h3 + 1)],
                    lnb3[:, 512 * k3:512 * (k3 + 1)], start=(k3 == 0), stop=(k3 == 15)), gT)
            s.wait("vector", gT, gT.n)
            s.op("vector", lambda e, h3=h3: e.tensor_copy(qT3[:, 512 * h3:512 * (h3 + 1)], pG[:]), gV)
            s.wait("tensor", gV, gV.n)
            for k3 in range(16):
                s.op("tensor", lambda e, h3=h3, k3=k3: e.matmul(
                    pG[:], wkb[:, 256 * k3 + 128 * h3: 256 * k3 + 128 * (h3 + 1)],
                    lnb3[:, 512 * k3:512 * (k3 + 1)], start=(k3 == 0), stop=(k3 == 15)), gT)
            s.wait("vector", gT, gT.n)
            s.op("vector", lambda e, h3=h3: e.tensor_copy(kT3[:, 512 * h3:512 * (h3 + 1)], pG[:]), gV)
            s.wait("tensor", gV, gV.n)
            for k3 in range(16):
                s.op("tensor", lambda e, h3=h3, k3=k3: e.matmul(
                    pG[:], wvb[:, 256 * k3 + 128 * h3: 256 * k3 + 128 * (h3 + 1)],
                    lnb3[:, 512 * k3:512 * (k3 + 1)], start=(k3 == 0), stop=(k3 == 15)), gT)
            s.wait("vector", gT, gT.n)
            s.op("vector", lambda e, h3=h3: e.tensor_copy(vT3[:, 512 * h3:512 * (h3 + 1)], pG[:]), gV)
            s.wait("tensor", gV, gV.n)
        projT3 = gT.n
        projV3 = gV.n
        _st3 = int(_os.environ.get("PH3_STAGE", 9))
        if _st3 < 2:
            continue
        for (srcb, dstb) in ((qT3, sqT3), (kT3, skT3)):
            for h3 in range(2):
                sl3 = lambda b: b[:, 512 * h3:512 * (h3 + 1)]
                s.wait("vector", gV, gV.n)
                s.wait("vector", gA, gA.n)
                s.op("vector", lambda e, srcb=srcb, h3=h3: e.tensor_scalar(
                    el1[:], srcb[:, 512 * h3:512 * (h3 + 1)], 0.0, None, ALU.min), gV)
                s.wait("scalar", gV, gV.n)
                s.op("scalar", lambda e: e.activation(el2[:], el1[:], AF.Exp), gA)
                s.wait("vector", gA, gA.n)
                s.op("vector", lambda e, srcb=srcb, h3=h3: e.tensor_sub(
                    el1[:], srcb[:, 512 * h3:512 * (h3 + 1)], el1[:]), gV)
                s.wait("vector", gV, gV.n)
                s.op("vector", lambda e, dstb=dstb, h3=h3: e.tensor_add(
                    dstb[:, 512 * h3:512 * (h3 + 1)], el2[:], el1[:]), gV)
        eluV = gV.n
        if _st3 < 3:
            continue
        for h3 in range(2):
            qh = lambda: qT3[:, 512 * h3:512 * (h3 + 1)]
            kh = lambda: kT3[:, 512 * h3:512 * (h3 + 1)]
            vh = lambda: vT3[:, 512 * h3:512 * (h3 + 1)]
            sqh = lambda: sqT3[:, 512 * h3:512 * (h3 + 1)]
            skh = lambda: skT3[:, 512 * h3:512 * (h3 + 1)]
            s.wait("tensor", gV, gV.n)
            for c3 in range(4):
                s.op("tensor", lambda e, h3=h3, c3=c3: e.transpose(
                    pF[:, 128 * c3:128 * (c3 + 1)],
                    vT3[:, 512 * h3 + 128 * c3: 512 * h3 + 128 * (c3 + 1)], sid3[:]), gT)
            s.wait("vector", gT, gT.n)
            s.op("vector", lambda e, h3=h3: e.tensor_copy(Vtm3[:, 512 * h3:512 * (h3 + 1)], pF[:]), gV)
            vtmV = gV.n
            s.wait("tensor", gV, vtmV)
            for tt3 in range(4):
                s.op("tensor", lambda e, h3=h3, tt3=tt3: e.matmul(
                    pA[:], qT3[:, 512 * h3 + 128 * tt3: 512 * h3 + 128 * (tt3 + 1)],
                    kT3[:, 512 * h3:512 * (h3 + 1)], start=True, stop=True), gT)
                s.wait("vector", gT, gT.n)
                s.op("vector", lambda e, tt3=tt3: e.reduce_max(
                    rmax[:, tt3:tt3 + 1], pA[:], axis=AX), gV)
                s.wait("vector", gV, gV.n)
                s.op("vector", lambda e, tt3=tt3: e.tensor_scalar(
                    rmax[:, 4 + tt3:5 + tt3], rmax[:, tt3:tt3 + 1],
                    -(128.0 ** -0.5), None, ALU.mult), gV)
                s.wait("scalar", gV, gV.n)
                s.op("scalar", lambda e, tt3=tt3: e.activation(
                    el1[:], pA[:], AF.Exp, bias=rmax[:, 4 + tt3:5 + tt3],
                    scale=float(128.0 ** -0.5)), gA)
                s.wait("vector", gA, gA.n)
                s.op("vector", lambda e, tt3=tt3: e.tensor_tensor(
                    Aexp[:, 512 * tt3:512 * (tt3 + 1)], el1[:],
                    cmask[:, 512 * tt3:512 * (tt3 + 1)], ALU.mult), gV)
                s.wait("vector", gV, gV.n)
                s.op("vector", lambda e, tt3=tt3: e.reduce_sum(
                    rsum[:, tt3:tt3 + 1], Aexp[:, 512 * tt3:512 * (tt3 + 1)], axis=AX), gV)
                s.wait("vector", gV, gV.n)
                s.op("vector", lambda e, tt3=tt3: e.reciprocal(
                    rsum[:, 4 + tt3:5 + tt3], rsum[:, tt3:tt3 + 1]), gV)
                s.wait("tensor", gV, gV.n)
            expV = gV.n
            if _st3 < 4:
                continue
            for tt3 in range(4):
                for kt3 in range(4):
                    s.wait("tensor", gV, gV.n)
                    s.op("tensor", lambda e, tt3=tt3, kt3=kt3: e.transpose(
                        pF[:, 128 * (kt3 % 2):128 * (kt3 % 2 + 1)],
                        Aexp[:, 512 * tt3 + 128 * kt3: 512 * tt3 + 128 * (kt3 + 1)],
                        sid3[:]), gT)
                    s.wait("vector", gT, gT.n)
                    s.op("vector", lambda e, kt3=kt3: e.tensor_copy(
                        dmem[:], pF[:, 128 * (kt3 % 2):128 * (kt3 % 2 + 1)]), gV)
                    s.wait("tensor", gV, gV.n)
                    s.op("tensor", lambda e, h3=h3, tt3=tt3, kt3=kt3: e.matmul(
                        pB[:, 128 * tt3:128 * (tt3 + 1)], dmem[:],
                        Vtm3[:, 512 * h3 + 128 * kt3: 512 * h3 + 128 * (kt3 + 1)],
                        start=(kt3 == 0), stop=(kt3 == 3)), gT)
            adotT = gT.n
            if _st3 < 5:
                continue
            s.wait("tensor", gV, eluV)
            for tt3 in range(4):
                s.op("tensor", lambda e, h3=h3, tt3=tt3: e.matmul(
                    pC[:, 128 * tt3:128 * (tt3 + 1)],
                    sqT3[:, 512 * h3 + 128 * tt3: 512 * h3 + 128 * (tt3 + 1)],
                    mem_b[h3][:], start=True, stop=True), gT)
            for tt3 in range(4):
                s.op("tensor", lambda e, h3=h3, tt3=tt3: e.matmul(
                    pD[:, tt3:tt3 + 1],
                    sqT3[:, 512 * h3 + 128 * tt3: 512 * h3 + 128 * (tt3 + 1)],
                    z_b[h3][:], start=True, stop=True), gT)
            amemT = gT.n
            s.wait("vector", gT, amemT)
            s.wait("vector", gST, gST.n)
            s.op("vector", lambda e: e.tensor_copy(dcol[:, 0:4], pD[:, 0:4]), gV)
            s.wait("vector", gV, gV.n)
            s.op("vector", lambda e: e.reciprocal(dcol[:, 4:8], dcol[:, 0:4]), gV)
            s.wait("vector", gV, gV.n)
            for tt3 in range(4):
                s.op("vector", lambda e, tt3=tt3: e.tensor_scalar(
                    el1[:, 0:128], pB[:, 128 * tt3:128 * (tt3 + 1)],
                    rsum[:, 4 + tt3:5 + tt3], None, ALU.mult), gV)
                s.wait("vector", gV, gV.n)
                s.op("vector", lambda e, h3=h3, tt3=tt3: e.tensor_tensor(
                    el1[:, 128:256], el1[:, 0:128],
                    betc_t[:, 128 * h3:128 * (h3 + 1)], ALU.mult), gV)
                s.op("vector", lambda e, tt3=tt3: e.tensor_scalar(
                    el1[:, 256:384], pC[:, 128 * tt3:128 * (tt3 + 1)],
                    dcol[:, 4 + tt3:5 + tt3], None, ALU.mult), gV)
                s.wait("vector", gV, gV.n)
                s.op("vector", lambda e, h3=h3, tt3=tt3: e.tensor_tensor(
                    el2[:, 128:256], el1[:, 256:384],
                    beta_t[:, 128 * h3:128 * (h3 + 1)], ALU.mult), gV)
                s.wait("vector", gV, gV.n)
                s.op("vector", lambda e, tt3=tt3: e.tensor_add(
                    attn_s[:, 128 * tt3:128 * (tt3 + 1)], el1[:, 128:256], el2[:, 128:256]), gV)
                s.wait("vector", gV, gV.n)
            attV = gV.n
            s.wait("gpsimd", gV, attV)
            s.wait("gpsimd", gST, prevStT)
            for tt3 in range(4):
                base = 256 * (sg3 // 4) + 64 * (sg3 % 4) + 32 * h3 + 8 * tt3
                dview = H["y2_sh"][base:base + 8, :].rearrange("j (m d) -> (j m) d", m=16)
                s.op("gpsimd", lambda e, dview=dview, tt3=tt3: e.dma_start(
                    out=dview, in_=attn_s[:, 128 * tt3:128 * (tt3 + 1)]), gST, 16)
            if _st3 < 6:
                continue
            s.wait("tensor", gV, attV)
            for kt3 in range(4):
                s.op("tensor", lambda e, h3=h3, kt3=kt3: e.matmul(
                    pD[:, 128 + 128 * kt3 if False else 128 * kt3:128 * (kt3 + 1)] if False else pE[:, 128 * kt3:128 * (kt3 + 1)],
                    skT3[:, 512 * h3 + 128 * kt3: 512 * h3 + 128 * (kt3 + 1)],
                    mem_b[h3][:], start=True, stop=True), gT)
            for kt3 in range(4):
                s.op("tensor", lambda e, h3=h3, kt3=kt3: e.matmul(
                    pD[:, 8 + kt3: 9 + kt3],
                    skT3[:, 512 * h3 + 128 * kt3: 512 * h3 + 128 * (kt3 + 1)],
                    z_b[h3][:], start=True, stop=True), gT)
            dmT = gT.n
            s.wait("vector", gT, dmT)
            s.op("vector", lambda e: e.tensor_copy(dcol[:, 8:12], pD[:, 8:12]), gV)
            s.wait("vector", gV, gV.n)
            s.op("vector", lambda e: e.reciprocal(dcol[:, 12:16], dcol[:, 8:12]), gV)
            s.wait("vector", gV, gV.n)
            for kt3 in range(4):
                s.wait("tensor", gV, gV.n)
                s.op("tensor", lambda e, h3=h3, kt3=kt3: e.transpose(
                    pF[:, 256 + 128 * (kt3 % 2): 256 + 128 * (kt3 % 2 + 1)],
                    skT3[:, 512 * h3 + 128 * kt3: 512 * h3 + 128 * (kt3 + 1)], sid3[:]), gT)
                s.wait("vector", gT, gT.n)
                s.op("vector", lambda e, kt3=kt3: e.tensor_copy(
                    sktm3[:, 128 * (kt3 % 2):128 * (kt3 % 2 + 1)],
                    pF[:, 256 + 128 * (kt3 % 2): 256 + 128 * (kt3 % 2 + 1)]), gV)
                s.op("vector", lambda e, kt3=kt3: e.tensor_scalar(
                    dmm[:], pE[:, 128 * kt3:128 * (kt3 + 1)],
                    dcol[:, 12 + kt3:13 + kt3], None, ALU.mult), gV)
                s.wait("vector", gV, gV.n)
                s.op("vector", lambda e, h3=h3, kt3=kt3: e.tensor_sub(
                    dmem[:], Vtm3[:, 512 * h3 + 128 * kt3: 512 * h3 + 128 * (kt3 + 1)],
                    dmm[:]), gV)
                s.wait("tensor", gV, gV.n)
                s.op("tensor", lambda e, kt3=kt3: e.matmul(
                    pA[:, 0:128], sktm3[:, 128 * (kt3 % 2):128 * (kt3 % 2 + 1)],
                    dmem[:], start=(kt3 == 0), stop=False), gT)
                s.op("tensor", lambda e, kt3=kt3: e.matmul(
                    pC[:, 0:1], sktm3[:, 128 * (kt3 % 2):128 * (kt3 % 2 + 1)],
                    onescol3[:], start=(kt3 == 0), stop=False), gT)
                s.wait("vector", gT, gT.n)
            s.op("tensor", lambda e, h3=h3: e.matmul(
                pA[:, 0:128], sid3[:], mem_b[h3][:], start=False, stop=True), gT)
            s.op("tensor", lambda e, h3=h3: e.matmul(
                pC[:, 0:1], sid3[:], z_b[h3][:], start=False, stop=True), gT)
            memT = gT.n
            s.wait("vector", gT, memT)
            s.op("vector", lambda e, h3=h3: e.tensor_copy(mem_f[h3][:], pA[:, 0:128]), gV)
            s.op("vector", lambda e, h3=h3: e.tensor_copy(z_f[h3][:], pC[:, 0:1]), gV)
            s.wait("scalar", gT, memT)
            s.wait("scalar", gV, gV.n)
            s.op("scalar", lambda e, h3=h3: e.copy(mem_b[h3][:], pA[:, 0:128]), gA)
            s.op("scalar", lambda e, h3=h3: e.copy(z_b[h3][:], pC[:, 0:1]), gA)
            s.wait("tensor", gA, gA.n)
            s.wait("vector", gV, gV.n)
        segVdone = gV.n
        prevStT = gST.n
    s.wait("gpsimd", gST, gST.n)
    p.pop_scope()
    if H.get("y2_a2a") is not None:
        a2a_out, ccch = H["y2_a2a"]
        s.op("gpsimd", lambda e, a2a_out=a2a_out: e.collective_compute(
            "AllToAll", ALU.bypass, replica_groups=RG8,
            ins=[H["y2_sh"][:, :].opt()], outs=[a2a_out[:, :].opt()]), ccch, 1)


def build_ph3_test():
    nc = bass.Bass("TRN2", target_bir_lowering=False)
    es = ExitStack()
    p = Prog(nc, es)
    H = make_consts(nc)
    H["ln2T_ag"] = nc.dram_tensor("ln2T_ag", [NCORE * D, TPC], BF16, kind="ExternalInput").ap()
    H["wq_s"] = nc.dram_tensor("wq_s", [D, 256], F32, kind="ExternalInput").ap()
    H["wk_s"] = nc.dram_tensor("wk_s", [D, 256], F32, kind="ExternalInput").ap()
    H["wv_s"] = nc.dram_tensor("wv_s", [D, 256], F32, kind="ExternalInput").ap()
    H["betas_s"] = nc.dram_tensor("betas_s", [2, 128], F32, kind="ExternalInput").ap()
    H["y2_sh"] = nc.dram_tensor("y2_sh", [TPC, D], BF16, kind="ExternalOutput").ap()
    H["y2_a2a"] = None
    with es:
        sched = Sched(p, "p3")
        block = es.enter_context(nc.Block())
        emit_ph3(p, sched, H)
        sched.emit(block)
    return nc



def emit_ph4(p, s, H):
    for (ch, n) in H.get("pre_waits4", []):
        for eng in ENGS:
            s.wait(eng, ch, n)
    mLD = s.ch("4LD")
    mLDW = s.ch("4LDW")
    mLDX = s.ch("4LDX")
    mLDR = s.ch("4LDR")
    mV = s.ch("4V")
    mA = s.ch("4A")
    mT = s.ch("4T")
    mST = s.ch("4ST")
    mGD = s.ch("4GD")
    mCC = s.ch("4CC")
    NT = TPC // 128

    p.push_scope()
    ln3T = p.sbuf("p4_ln3T", [128, 16 * TPC], BF16, side="right")
    p.push_scope()
    wob = p.sbuf("p4_wob", [128, 16 * 2048], BF16)
    wldf4 = p.sbuf("p4_wldf", [128, 2048], F32)
    y2t = p.sbuf("p4_y2t", [128, 16 * 128], BF16)
    y2T = p.sbuf("p4_y2T", [128, 16 * 128], BF16)
    a4 = p.sbuf("p4_a4", [128, D], F32)
    b4 = p.sbuf("p4_b4", [128, D], F32)
    xr4 = p.sbuf("p4_xr4", [128, D], F32)
    st4 = p.sbuf("p4_st4", [128, 8], F32)
    sq4 = p.sbuf("p4_sq4", [128, D], F32)
    ep4 = p.sbuf("p4_ep4", [128, 1], F32)
    lb4 = p.sbuf("p4_lb4", [128, D], BF16)
    si4 = p.sbuf("p4_si4", [128, 128], BF16)
    glnsl = p.sbuf("p4_glnsl", [128, 512], BF16)
    pq = [p.psum(f"p4_pq{i}", [128, 512], F32) for i in range(6)]
    pt4 = [p.psum(f"p4_pt{i}", [128, 128], BF16) for i in range(2)]

    s.op("sync", lambda e: e.dma_start(out=si4[:], in_=H["identb"]), mLDW, 16)
    s.op("vector", lambda e: e.memset(ep4[:], EPS_LN), mV)
    for k4 in range(16):
        s.wait("sync", mV, mV.n)
        s.op("sync", lambda e, k4=k4: e.dma_start(
            out=wldf4[:], in_=H["wo_ag"][128 * k4:128 * (k4 + 1), :]), mLDW, 16)
        s.wait("vector", mLDW, mLDW.n)
        s.op("vector", lambda e, k4=k4: e.tensor_copy(
            wob[:, 2048 * k4:2048 * (k4 + 1)], wldf4[:]), mV)
    for k4 in range(16):
        s.wait("sync", mV, mV.n)
        s.op("sync", lambda e, k4=k4: e.dma_start(
            out=wldf4[:], in_=H["core_ag"][128 * k4:128 * (k4 + 1), :]), mLDW, 16)
        s.wait("vector", mLDW, mLDW.n)
        s.op("vector", lambda e: e.tensor_copy(b4[:].bitcast(BF16)[:, 0:2048], wldf4[:]), mV)
        s.wait("gpsimd", mV, mV.n)
        s.op("gpsimd", lambda e, k4=k4: e.dma_start(
            out=H["coreb_dram"][128 * k4:128 * (k4 + 1), :],
            in_=b4[:].bitcast(BF16)[:, 0:2048]), mGD, 16)
        s.wait("vector", mGD, mGD.n)
    wcast4 = mV.n

    y2v = H["y2_loc"].rearrange("(A s b j) d -> A s b j d", A=8, s=4, b=2)
    woT4 = 0
    evV4 = wcast4
    lnslab4 = 0
    for i4 in range(NT):
        s.wait("sync", mT, woT4)
        Ab = 2 * (i4 % 4)
        s.op("sync", lambda e, i4=i4, Ab=Ab: e.dma_start(
            out=y2t[:], in_=y2v[Ab:Ab + 2, i4 // 4, :, :, :]), mLD, 16)
        s.wait("tensor", mLD, mLD.n)
        s.wait("tensor", mV, evV4)
        for k4 in range(16):
            if k4 >= 2:
                s.wait("tensor", mV, y2evV)
            s.op("tensor", lambda e, k4=k4: e.transpose(
                pt4[k4 % 2][:], y2t[:, 128 * k4:128 * (k4 + 1)], si4[:]), mT)
            s.wait("vector", mT, mT.n)
            s.op("vector", lambda e, k4=k4: e.tensor_copy(
                y2T[:, 128 * k4:128 * (k4 + 1)], pt4[k4 % 2][:]), mV)
            y2evV = mV.n
        s.wait("tensor", mV, y2evV)
        for db4 in range(4):
            for k4 in range(16):
                s.op("tensor", lambda e, db4=db4, k4=k4: e.matmul(
                    pq[db4][:], y2T[:, 128 * k4:128 * (k4 + 1)],
                    wob[:, 2048 * k4 + 512 * db4: 2048 * k4 + 512 * (db4 + 1)],
                    start=(k4 == 0), stop=(k4 == 15)), mT)
        woT4 = mT.n
        s.wait("vector", mT, woT4)
        for db4 in range(4):
            s.op("vector", lambda e, db4=db4: e.tensor_copy(
                a4[:, 512 * db4:512 * (db4 + 1)], pq[db4][:]), mV)
        s.wait("vector", mV, mV.n)
        s.wait("sync", mV, mV.n)
        s.op("sync", lambda e, i4=i4: e.dma_start(
            out=xr4[:], in_=H["x2_dram"][128 * i4:128 * (i4 + 1), :]), mLDX, 16)
        emit_ln_rows(s, mV, mA, a4[:], b4[:], st4, sq4, ep4[:])
        s.wait("vector", mV, mV.n)
        s.wait("vector", mLDX, mLDX.n)
        s.op("vector", lambda e: e.tensor_add(a4[:], b4[:], xr4[:]), mV)
        s.wait("vector", mV, mV.n)
        s.wait("gpsimd", mV, mV.n)
        s.wait("gpsimd", mST, mST.n)
        s.op("gpsimd", lambda e, i4=i4: e.dma_start(
            out=H["x3_dram"][128 * i4:128 * (i4 + 1), :], in_=a4[:]), mST, 16)
        emit_ln_rows(s, mV, mA, a4[:], lb4[:], st4, sq4, ep4[:])
        ln3V = mV.n
        s.wait("vector", mST, mST.n)
        s.wait("tensor", mV, ln3V)
        for k4 in range(16):
            if k4 >= 2:
                s.wait("tensor", mV, lnevV)
            s.op("tensor", lambda e, k4=k4: e.transpose(
                pt4[k4 % 2][:], lb4[:, 128 * k4:128 * (k4 + 1)], si4[:]), mT)
            s.wait("vector", mT, mT.n)
            s.op("vector", lambda e, k4=k4, i4=i4: e.tensor_copy(
                ln3T[:, TPC * k4 + 128 * i4: TPC * k4 + 128 * (i4 + 1)],
                pt4[k4 % 2][:]), mV)
            lnevV = mV.n
        evV4 = mV.n
    s.wait("scalar", mV, mV.n)
    for k4 in range(16):
        for tb4 in range(4):
            s.wait("scalar", mGD, mGD.n)
            s.op("scalar", lambda e, k4=k4, tb4=tb4: e.activation(
                glnsl[:], ln3T[:, TPC * k4 + 512 * tb4: TPC * k4 + 512 * (tb4 + 1)],
                AF.Gelu), mA)
            s.wait("gpsimd", mA, mA.n)
            s.op("gpsimd", lambda e, k4=k4, tb4=tb4: e.dma_start(
                out=H["gln_dram"][128 * k4:128 * (k4 + 1), 512 * tb4:512 * (tb4 + 1)],
                in_=glnsl[:]), mGD, 16)
    glnA = mA.n
    s.barrier("B4A", pre_dma_waits=[("gpsimd", mST, mST.n), ("gpsimd", mGD, mGD.n)])
    p.pop_scope()

    p.push_scope()
    dwb = p.sbuf("p4_dwb", [128, 16 * 1024], BF16)
    dnb = p.sbuf("p4_dnb", [128, 8], F32)
    rwb = p.sbuf("p4_rwb", [128, 16 * 8], BF16)
    btmpB = p.sbuf("p4_btmpB", [20, 128], F32)
    sifB = p.sbuf("p4_sifB", [128, 128], F32)
    rbrow = p.sbuf("p4_rbrow", [1, 8], F32)
    rbrow_b = p.sbuf("p4_rbrowb", [1, 8], BF16)
    rcost = p.sbuf("p4_rcost", [1, 8], F32)
    one1b4 = p.sbuf("p4_one1b", [1, 128], BF16)
    ghT = p.sbuf("p4_ghT", [128, 8 * TPC], BF16, side="right")
    wl4 = p.sbuf("p4_wl4", [128, 1024], F32)
    rt = p.sbuf("p4_rt", [128, 8], F32)
    rp = p.sbuf("p4_rp", [128, 8], F32)
    rs1 = p.sbuf("p4_rs1", [128, 8], F32)
    rs2 = p.sbuf("p4_rs2", [128, 16], F32)
    w9b = p.sbuf("p4_w9b", [128, 16], F32)
    pq2 = [p.psum(f"p4_q2{i}", [128, 512], F32) for i in range(4)]

    s.op("vector", lambda e: e.memset(one1b4[:], 1.0), mV)
    s.op("sync", lambda e: e.dma_start(out=rcost[:], in_=H["costrow"]), mLDW, 16)
    s.op("sync", lambda e: e.dma_start(out=rbrow[:], in_=H["router_b"]), mLDW, 16)
    s.op("sync", lambda e: e.dma_start(
        out=btmpB[0:8, :], in_=H["down_b"].rearrange("(c p) -> c p", p=128)), mLDW, 16)
    s.op("sync", lambda e: e.dma_start(out=sifB[:], in_=H["identf"]), mLDW, 16)
    for k4 in range(16):
        s.wait("sync", mV, mV.n)
        s.op("sync", lambda e, k4=k4: e.dma_start(
            out=wl4[:], in_=H["down_ag"][128 * k4:128 * (k4 + 1), :]), mLDW, 16)
        s.wait("vector", mLDW, mLDW.n)
        s.op("vector", lambda e, k4=k4: e.tensor_copy(
            dwb[:, 1024 * k4:1024 * (k4 + 1)], wl4[:]), mV)
        s.wait("sync", mV, mV.n)
        s.op("sync", lambda e, k4=k4: e.dma_start(
            out=wl4[:, 0:8], in_=H["router_w"][128 * k4:128 * (k4 + 1), :]), mLDW, 16)
        s.wait("vector", mLDW, mLDW.n)
        s.op("vector", lambda e, k4=k4: e.tensor_copy(
            rwb[:, 8 * k4:8 * (k4 + 1)], wl4[:, 0:8]), mV)
    s.wait("tensor", mLDW, mLDW.n)
    s.op("tensor", lambda e: e.transpose(pq2[3][:, 0:8], btmpB[0:8, :], sifB[0:8, 0:8]), mT)
    s.wait("vector", mT, mT.n)
    s.op("vector", lambda e: e.tensor_copy(dnb[:], pq2[3][:, 0:8]), mV)
    s.wait("vector", mLDW, mLDW.n)
    s.op("vector", lambda e: e.tensor_sub(rbrow[:], rbrow[:], rcost[:]), mV)
    s.wait("vector", mV, mV.n)
    s.op("vector", lambda e: e.tensor_copy(rbrow_b[:], rbrow[:]), mV)
    wcastB = mV.n
    ghT_T = 0
    gh_hist = {}
    ghit = 0
    for hc in range(8):
        for tb4 in range(4):
            bank = pq2[tb4 % 2]
            s.wait("tensor", mV, wcastB)
            s.wait("tensor", mA, gh_hist.get(ghit - 2, 0))
            for k4 in range(16):
                s.op("tensor", lambda e, hc=hc, tb4=tb4, k4=k4, bank=bank: e.matmul(
                    bank[:], dwb[:, 1024 * k4 + 128 * hc: 1024 * k4 + 128 * (hc + 1)],
                    ln3T[:, TPC * k4 + 512 * tb4: TPC * k4 + 512 * (tb4 + 1)],
                    start=(k4 == 0), stop=(k4 == 15)), mT)
            s.wait("scalar", mT, mT.n)
            s.op("scalar", lambda e, hc=hc, tb4=tb4, bank=bank: e.activation(
                ghT[:, TPC * hc + 512 * tb4: TPC * hc + 512 * (tb4 + 1)],
                bank[:], AF.Gelu, bias=dnb[:, hc:hc + 1]), mA)
            gh_hist[ghit] = mA.n
            ghit += 1
    ghA = mA.n
    s.wait("gpsimd", mA, ghA)
    for hc in range(8):
        s.op("gpsimd", lambda e, hc=hc: e.dma_start(
            out=H["gh_bnc"][128 * hc:128 * (hc + 1), :],
            in_=ghT[:, TPC * hc:TPC * (hc + 1)]), mGD, 16)
    s.wait("gpsimd", mGD, mGD.n)
    s.op("gpsimd", lambda e: e.collective_compute(
        "AllGather", ALU.bypass, replica_groups=RG8,
        ins=[H["gh_bnc"][:, :].opt()], outs=[H["gh_ag"][:, :].opt()]), mCC, 1)
    w9V = 0
    for i4 in range(NT):
        bank = pq2[2]
        s.wait("tensor", mV, w9V)
        for k4 in range(16):
            s.op("tensor", lambda e, i4=i4, k4=k4, bank=bank: e.matmul(
                bank[:, 0:8], ln3T[:, TPC * k4 + 128 * i4: TPC * k4 + 128 * (i4 + 1)],
                rwb[:, 8 * k4:8 * (k4 + 1)], start=(k4 == 0), stop=False), mT)
        s.op("tensor", lambda e, bank=bank: e.matmul(
            bank[:, 0:8], one1b4[:], rbrow_b[:], start=False, stop=True), mT)
        rT = mT.n
        s.wait("vector", mT, rT)
        s.op("vector", lambda e, bank=bank: e.tensor_copy(rt[:], bank[:, 0:8]), mV)
        s.wait("vector", mV, mV.n)
        s.op("vector", lambda e: e.reduce_max(rs1[:, 0:1], rt[:], axis=AX), mV)
        s.wait("vector", mV, mV.n)
        s.op("vector", lambda e: e.tensor_scalar(rs1[:, 1:2], rs1[:, 0:1], -1.0, None, ALU.mult), mV)
        s.wait("scalar", mV, mV.n)
        s.op("scalar", lambda e: e.activation(rp[:], rt[:], AF.Exp, bias=rs1[:, 1:2],
                                              accum_out=rs1[:, 2:3]), mA)
        s.wait("vector", mA, mA.n)
        s.op("vector", lambda e: e.reciprocal(rs1[:, 3:4], rs1[:, 2:3]), mV)
        s.wait("vector", mV, mV.n)
        s.op("vector", lambda e: e.tensor_scalar(rp[:], rp[:], rs1[:, 3:4], None, ALU.mult), mV)
        s.wait("vector", mV, mV.n)
        s.op("vector", lambda e: e.reduce_max(rs1[:, 4:5], rp[:], axis=AX), mV)
        s.wait("vector", mV, mV.n)
        s.op("vector", lambda e: e.tensor_scalar(rs2[:, 0:8], rp[:], rs1[:, 4:5], None, ALU.is_ge), mV)
        s.wait("vector", mV, mV.n)
        s.op("vector", lambda e: e.tensor_scalar(rs2[:, 8:16], rs2[:, 0:8], -1.0, 1.0, ALU.mult, ALU.add), mV)
        s.wait("vector", mV, mV.n)
        s.op("vector", lambda e: e.tensor_mul(rt[:], rp[:], rs2[:, 8:16]), mV)
        s.wait("vector", mV, mV.n)
        s.op("vector", lambda e: e.reduce_max(rs1[:, 5:6], rt[:], axis=AX), mV)
        s.wait("vector", mV, mV.n)
        s.op("vector", lambda e: e.tensor_scalar(rs2[:, 8:16], rt[:], rs1[:, 5:6], None, ALU.is_ge), mV)
        s.wait("vector", mV, mV.n)
        s.op("vector", lambda e: e.tensor_sub(rs1[:, 6:7], rs1[:, 5:6], rs1[:, 4:5]), mV)
        s.wait("scalar", mV, mV.n)
        s.op("scalar", lambda e: e.activation(rs1[:, 7:8], rs1[:, 6:7], AF.Exp), mA)
        s.wait("vector", mA, mA.n)
        s.op("vector", lambda e: e.tensor_scalar(rs1[:, 6:7], rs1[:, 7:8], 1.0, None, ALU.add), mV)
        s.wait("vector", mV, mV.n)
        s.op("vector", lambda e: e.reciprocal(rs1[:, 6:7], rs1[:, 6:7]), mV)
        s.wait("vector", mV, mV.n)
        s.op("vector", lambda e: e.tensor_mul(rs1[:, 7:8], rs1[:, 7:8], rs1[:, 6:7]), mV)
        s.wait("vector", mV, mV.n)
        s.op("vector", lambda e: e.tensor_scalar(w9b[:, 0:8], rs2[:, 0:8], rs1[:, 6:7], None, ALU.mult), mV)
        s.op("vector", lambda e: e.memset(w9b[:, 9:16], 0.0), mV)
        s.wait("vector", mV, mV.n)
        s.op("vector", lambda e: e.tensor_scalar(rs2[:, 8:16], rs2[:, 8:16], rs1[:, 7:8], None, ALU.mult), mV)
        s.wait("vector", mV, mV.n)
        s.op("vector", lambda e: e.tensor_add(w9b[:, 0:8], w9b[:, 0:8], rs2[:, 8:16]), mV)
        s.wait("vector", mV, mV.n)
        s.op("vector", lambda e: e.memset(w9b[:, 8:9], 1.0), mV)
        s.wait("gpsimd", mV, mV.n)
        s.op("gpsimd", lambda e, i4=i4: e.dma_start(
            out=H["w9_bnc"][128 * i4:128 * (i4 + 1), :], in_=w9b[:]), mGD, 16)
        s.wait("vector", mGD, mGD.n)
        w9V = mV.n
    s.wait("gpsimd", mGD, mGD.n)
    s.op("gpsimd", lambda e: e.collective_compute(
        "AllGather", ALU.bypass, replica_groups=RG8,
        ins=[H["w9_bnc"][:, :].opt()], outs=[H["w9_ag"][:, :].opt()]), mCC, 1)
    s.barrier("B4B", pre_dma_waits=[("gpsimd", mGD, mGD.n)])
    p.pop_scope()
    p.pop_scope()

    p.push_scope()
    ew1b = p.sbuf("p4_ew1b", [128, 8 * 2560], BF16)
    ew2b = p.sbuf("p4_ew2b", [128, 20 * 1024], BF16)
    sel8b = p.sbuf("p4_sel8b", [8, 2560], BF16)
    eb1c = p.sbuf("p4_eb1c", [128, 20], F32)
    geb1 = p.sbuf("p4_geb1", [128, 20], F32)
    wle = p.sbuf("p4_wle", [128, 2560], F32)
    ew2s = p.sbuf("p4_ew2s", [128, 1024], BF16)
    v8f = p.sbuf("p4_v8f", [8, 1024], F32)
    v8nb = p.sbuf("p4_v8nb", [8, 1024], BF16)
    sumr = p.sbuf("p4_sumr", [1, 1024], F32)
    sumrb = p.sbuf("p4_sumrb", [1, 1024], BF16)
    ones8 = p.sbuf("p4_ones8", [8, 1], BF16)
    b9 = p.sbuf("p4_b9", [9, 1024], BF16)
    ghblk = p.sbuf("p4_ghblk", [128, 8 * 512], BF16)
    w9l = p.sbuf("p4_w9l", [128, 16], F32)
    w9lb = p.sbuf("p4_w9lb", [128, 16], BF16)
    w9T = p.sbuf("p4_w9T", [16, 4 * 128], BF16)
    mtile = p.sbuf("p4_mtile", [128, 512], BF16)
    mt_blk = p.sbuf("p4_mtblk", [128, 20 * 512], BF16)
    rsrow = p.sbuf("p4_rsrow", [128, 1024], F32)
    pm4 = [p.psum(f"p4_pm{i}", [128, 512], F32) for i in range(2)]
    pw4 = p.psum("p4_pw4", [128, 512], F32)
    pp4 = [p.psum(f"p4_pp{i}", [128, 512], F32) for i in range(4)]
    ptT4 = p.psum("p4_ptT4", [16, 128], BF16)

    s.op("vector", lambda e: e.memset(ones8[:], 1.0), mV)
    s.op("sync", lambda e: e.dma_start(out=si4c[:], in_=H["identb"]), mLDW, 16)
    for k4 in range(8):
        s.wait("sync", mV, mV.n)
        s.op("sync", lambda e, k4=k4: e.dma_start(
            out=wle[:], in_=H["ew1_s"][128 * k4:128 * (k4 + 1), :]), mLDW, 16)
        s.wait("vector", mLDW, mLDW.n)
        s.op("vector", lambda e, k4=k4: e.tensor_copy(
            ew1b[:, 2560 * k4:2560 * (k4 + 1)], wle[:]), mV)
    for g4 in range(20):
        s.wait("sync", mV, mV.n)
        s.op("sync", lambda e, g4=g4: e.dma_start(
            out=wle[:, 0:1024], in_=H["ew2_s"][128 * g4:128 * (g4 + 1), :]), mLDW, 16)
        s.wait("vector", mLDW, mLDW.n)
        s.op("vector", lambda e, g4=g4: e.tensor_copy(
            ew2b[:, 1024 * g4:1024 * (g4 + 1)], wle[:, 0:1024]), mV)
    s.wait("sync", mV, mV.n)
    s.op("sync", lambda e: e.dma_start(out=wle[0:8, :], in_=H["sel8_s"]), mLDW, 16)
    s.wait("vector", mLDW, mLDW.n)
    s.op("vector", lambda e: e.tensor_copy(sel8b[:], wle[0:8, :]), mV)
    s.wait("sync", mV, mV.n)
    s.op("sync", lambda e: e.dma_start(
        out=btmpC[:], in_=H["eb1_s"].rearrange("(c p) -> c p", p=128)), mLDW, 16)
    s.op("sync", lambda e: e.dma_start(out=sifC[:], in_=H["identf"]), mLDW, 16)
    s.wait("tensor", mLDW, mLDW.n)
    s.op("tensor", lambda e: e.transpose(pw4[:, 0:20], btmpC[:], sifC[0:20, 0:20]), mT)
    s.wait("vector", mT, mT.n)
    s.op("vector", lambda e: e.tensor_copy(eb1c[:], pw4[:, 0:20]), mV)
    s.wait("scalar", mV, mV.n)
    s.op("scalar", lambda e: e.activation(geb1[:], eb1c[:], AF.Gelu), mA)
    ecastV = mV.n
    selTb = p.sbuf("p4_selTb", [128, 20 * 8], BF16)
    si4c = p.sbuf("p4_si4c", [128, 128], BF16)
    btmpC = p.sbuf("p4_btmpC", [20, 128], F32)
    sifC = p.sbuf("p4_sifC", [128, 128], F32)
    v8pb = p.sbuf("p4_v8pb", [8, 1024], BF16)
    for g4 in range(20):
        s.wait("sync", mV, mV.n)
        s.op("sync", lambda e, g4=g4: e.dma_start(
            out=wle[:, 0:8], in_=H["selT_s"][128 * g4:128 * (g4 + 1), :]), mLDW, 16)
        s.wait("vector", mLDW, mLDW.n)
        s.op("vector", lambda e, g4=g4: e.tensor_copy(
            selTb[:, 8 * g4:8 * (g4 + 1)], wle[:, 0:8]), mV)
    s.wait("vector", mA, mA.n)
    for g4 in range(20):
        s.wait("vector", mT, mT.n)
        s.op("vector", lambda e, g4=g4: e.tensor_scalar(
            ew2s[:], ew2b[:, 1024 * g4:1024 * (g4 + 1)], geb1[:, g4:g4 + 1],
            None, ALU.mult), mV)
        s.wait("tensor", mV, mV.n)
        for dh4 in range(2):
            s.op("tensor", lambda e, g4=g4, dh4=dh4: e.matmul(
                pp4[2 + dh4][0:8, :], selTb[:, 8 * g4:8 * (g4 + 1)],
                ew2s[:, 512 * dh4:512 * (dh4 + 1)],
                start=(g4 == 0), stop=(g4 == 19)), mT)
    v8T = mT.n
    s.wait("vector", mT, v8T)
    for dh4 in range(2):
        s.op("vector", lambda e, dh4=dh4: e.tensor_copy(
            v8f[:, 512 * dh4:512 * (dh4 + 1)], pp4[2 + dh4][0:8, :]), mV)
    s.wait("vector", mV, mV.n)
    s.op("vector", lambda e: e.tensor_scalar(v8nb[:], v8f[:], -1.0, None, ALU.mult), mV)
    s.wait("vector", mV, mV.n)
    s.op("vector", lambda e: e.tensor_copy(v8pb[:], v8f[:]), mV)
    posV = mV.n
    s.wait("tensor", mV, posV)
    for dh4 in range(2):
        s.wait("tensor", mV, mV.n)
        s.op("tensor", lambda e, dh4=dh4: e.matmul(
            pw4[0:1, :], ones8[:], v8pb[:, 512 * dh4:512 * (dh4 + 1)],
            start=True, stop=True), mT)
        s.wait("vector", mT, mT.n)
        s.op("vector", lambda e, dh4=dh4: e.tensor_copy(
            sumr[:, 512 * dh4:512 * (dh4 + 1)], pw4[0:1, :]), mV)
    s.wait("vector", mV, mV.n)
    s.op("vector", lambda e: e.tensor_copy(sumrb[:], sumr[:]), mV)
    s.wait("gpsimd", mV, mV.n)
    s.op("gpsimd", lambda e: e.dma_start(out=b9[0:8, :], in_=v8nb[:]), mGD, 16)
    s.op("gpsimd", lambda e: e.dma_start(out=b9[8:9, :], in_=sumrb[:]), mGD, 16)
    s.wait("tensor", mGD, mGD.n)
    s.wait("vector", mGD, mGD.n)

    s.wait("sync", mCC, 2)
    s.wait("gpsimd", mCC, 2)
    w9v4 = H["w9_ag"].rearrange("(r t) c -> r t c", r=8)
    ghv4 = H["gh_ag"].rearrange("(r h) t -> r h t", r=8)
    pDone = 0
    mtDone = mV.n
    for tb4 in range(32):
        rr4, tt04 = tb4 // 4, (tb4 % 4) * 512
        s.wait("sync", mT, pDone)
        for k4 in range(8):
            s.op("sync", lambda e, k4=k4, rr4=rr4, tt04=tt04: e.dma_start(
                out=ghblk[:, 512 * k4:512 * (k4 + 1)],
                in_=ghv4[rr4, 128 * k4:128 * (k4 + 1), tt04:tt04 + 512]), mLD, 16)
        for ts4 in range(4):
            s.op("sync", lambda e, ts4=ts4, rr4=rr4, tt04=tt04: e.dma_start(
                out=w9l[:],
                in_=w9v4[rr4, tt04 + 128 * ts4: tt04 + 128 * (ts4 + 1), :]), mLD, 16)
            s.wait("vector", mLD, mLD.n)
            s.op("vector", lambda e, ts4=ts4: e.tensor_copy(w9lb[:], w9l[:]), mV)
            s.wait("tensor", mV, mV.n)
            s.op("tensor", lambda e, ts4=ts4: e.transpose(ptT4[:], w9lb[:], si4c[:]), mT)
            s.wait("vector", mT, mT.n)
            s.op("vector", lambda e, ts4=ts4: e.tensor_copy(
                w9T[:, 128 * ts4:128 * (ts4 + 1)], ptT4[:]), mV)
            s.wait("sync", mV, mV.n)
        ghdone = mLD.n
        w9ready = mV.n
        for g4 in range(20):
            bank = pm4[g4 % 2]
            s.wait("tensor", mLD, ghdone)
            s.wait("tensor", mV, mtDone)
            s.wait("tensor", mA, mA.n)
            for k4 in range(8):
                s.op("tensor", lambda e, g4=g4, k4=k4, bank=bank: e.matmul(
                    bank[:], ew1b[:, 2560 * k4 + 128 * g4: 2560 * k4 + 128 * (g4 + 1)],
                    ghblk[:, 512 * k4:512 * (k4 + 1)],
                    start=(k4 == 0), stop=(k4 == 7)), mT)
            s.wait("scalar", mT, mT.n)
            s.wait("scalar", mV, mV.n)
            s.op("scalar", lambda e, g4=g4, bank=bank: e.activation(
                mtile[:], bank[:], AF.Gelu, bias=eb1c[:, g4:g4 + 1]), mA)
            s.wait("tensor", mV, mV.n)
            s.op("tensor", lambda e, g4=g4: e.matmul(
                pw4[:], sel8b[:, 128 * g4:128 * (g4 + 1)], w9T[0:8, :],
                start=True, stop=True), mT)
            s.wait("vector", mT, mT.n)
            s.wait("vector", mA, mA.n)
            s.op("vector", lambda e, g4=g4: e.tensor_tensor(
                mt_blk[:, 512 * g4:512 * (g4 + 1)], mtile[:], pw4[:], ALU.mult), mV)
        mtDone = mV.n
        s.wait("tensor", mV, mtDone)
        for dh4 in range(2):
            s.wait("tensor", mV, mV.n)
            for ts4 in range(4):
                s.op("tensor", lambda e, ts4=ts4, dh4=dh4: e.matmul(
                    pp4[ts4][:], w9T[0:9, 128 * ts4:128 * (ts4 + 1)],
                    b9[:, 512 * dh4:512 * (dh4 + 1)], start=True, stop=False), mT)
                for g4 in range(20):
                    s.op("tensor", lambda e, ts4=ts4, dh4=dh4, g4=g4: e.matmul(
                        pp4[ts4][:],
                        mt_blk[:, 512 * g4 + 128 * ts4: 512 * g4 + 128 * (ts4 + 1)],
                        ew2b[:, 1024 * g4 + 512 * dh4: 1024 * g4 + 512 * (dh4 + 1)],
                        start=False, stop=(g4 == 19)), mT)
            pT4 = mT.n
            s.wait("vector", mT, pT4)
            s.wait("vector", mST, mST.n)
            for ts4 in range(4):
                s.op("vector", lambda e, ts4=ts4, dh4=dh4: e.tensor_copy(
                    rsrow[:, 512 * dh4:512 * (dh4 + 1)], pp4[ts4][:]), mV)
                s.wait("gpsimd", mV, mV.n)
                s.op("gpsimd", lambda e, ts4=ts4, dh4=dh4, tb4=tb4: e.dma_start(
                    out=H["rs_in"][512 * tb4 + 128 * ts4: 512 * tb4 + 128 * (ts4 + 1),
                                   512 * dh4:512 * (dh4 + 1)],
                    in_=rsrow[:, 512 * dh4:512 * (dh4 + 1)]), mST, 16)
                s.wait("vector", mST, mST.n)
        pDone = mT.n
    s.wait("gpsimd", mST, mST.n)
    s.op("gpsimd", lambda e: e.collective_compute(
        "ReduceScatter", ALU.add, replica_groups=RG8,
        ins=[H["rs_in"][:, :].opt()], outs=[H["rs_out"][:, :].opt()]), mCC, 1)
    s.barrier("B4C", pre_dma_waits=[("gpsimd", mST, mST.n), ("gpsimd", mGD, mGD.n)])
    p.pop_scope()

    p.push_scope()
    sh1b4 = p.sbuf("p4_sh1b", [128, 8 * 1024], BF16)
    sh2b4 = p.sbuf("p4_sh2b", [128, 8 * 1024], BF16)
    upb4 = p.sbuf("p4_upb", [128, 8 * 2048], BF16)
    corsl0 = p.sbuf("p4_corsl0", [128, 16 * 512], BF16)
    corsl = [corsl0, corsl0]
    glsl = p.sbuf("p4_glsl", [128, 16 * 128], BF16)
    rsb = p.sbuf("p4_rsb", [128, 1024], BF16)
    wl5 = p.sbuf("p4_wl5", [128, 2048], F32)
    ghlo = p.sbuf("p4_ghlo", [128, 8 * 512], BF16)
    msb = p.sbuf("p4_msb", [128, 8 * 512], BF16)
    psT = p.sbuf("p4_psT", [128, 8 * 512], BF16)
    rsT = p.sbuf("p4_rsT", [128, 8 * 512], BF16)
    accT = p.sbuf("p4_accT", [128, 8 * 512], BF16)
    s1c = p.sbuf("p4_s1c", [128, 8], F32)
    s2c = p.sbuf("p4_s2c", [128, 8], F32)
    ubrow = p.sbuf("p4_ubrow", [1, 2048], F32)
    cbrow = p.sbuf("p4_cbrow", [1, 2048], F32)
    eb2b = p.sbuf("p4_eb2b", [8, 1024], BF16)
    ones8e = p.sbuf("p4_ones8e", [8, 1], BF16)
    sb2c = p.sbuf("p4_sb2c", [128, 8], F32)
    bsum = p.sbuf("p4_bsum", [1, 2048], BF16)
    o5 = p.sbuf("p4_o5", [128, 2048], F32)
    l5 = p.sbuf("p4_l5", [128, 2048], F32)
    x5 = p.sbuf("p4_x5", [128, 2048], F32)
    st5 = p.sbuf("p4_st5", [128, 8], F32)
    sq5 = p.sbuf("p4_sq5", [128, 2048], F32)
    ep5 = p.sbuf("p4_ep5", [128, 1], F32)
    one5 = p.sbuf("p4_one5", [1, 128], BF16)
    si5 = p.sbuf("p4_si5", [128, 128], BF16)
    btmpE = p.sbuf("p4_btmpE", [16, 128], F32)
    sifE = p.sbuf("p4_sifE", [128, 128], F32)
    xr4e = p.sbuf("p4_xr4e", [128, 2048], F32)
    pe5 = [p.psum(f"p4_pe5{i}", [128, 512], F32) for i in range(6)]
    pt5 = [p.psum(f"p4_pt5{i}", [128, 128], BF16) for i in range(2)]

    rsl = wl5[:, 0:1024]
    s.op("vector", lambda e: e.memset(ep5[:], EPS_LN), mV)
    s.op("vector", lambda e: e.memset(one5[:], 1.0), mV)
    s.op("sync", lambda e: e.dma_start(out=si5[:], in_=H["identb"]), mLDW, 16)
    s.op("sync", lambda e: e.dma_start(out=ubrow[:], in_=H["up_b"]), mLDW, 16)
    s.op("sync", lambda e: e.dma_start(out=cbrow[:], in_=H["core_b"]), mLDW, 16)
    s.op("sync", lambda e: e.dma_start(
        out=btmpE[0:8, :], in_=H["sh1_b"].rearrange("(c p) -> c p", p=128)), mLDW, 16)
    s.op("sync", lambda e: e.dma_start(
        out=btmpE[8:16, :], in_=H["sh2_b"].rearrange("(c p) -> c p", p=128)), mLDW, 16)
    s.op("sync", lambda e: e.dma_start(out=sifE[:], in_=H["identf"]), mLDW, 16)
    s.wait("tensor", mLDW, mLDW.n)
    s.op("tensor", lambda e: e.transpose(pe5[4][:, 0:16], btmpE[:], sifE[0:16, 0:16]), mT)
    s.wait("vector", mT, mT.n)
    s.op("vector", lambda e: e.tensor_copy(s1c[:], pe5[4][:, 0:8]), mV)
    s.op("vector", lambda e: e.tensor_copy(s2c[:], pe5[4][:, 8:16]), mV)
    s.op("vector", lambda e: e.memset(ones8e[:], 1.0), mV)
    s.wait("sync", mV, mV.n)
    s.op("sync", lambda e: e.dma_start(out=wl5[0:8, 0:1024], in_=H["eb2_stack"]), mLDW, 16)
    s.wait("vector", mLDW, mLDW.n)
    s.op("vector", lambda e: e.tensor_copy(eb2b[:], wl5[0:8, 0:1024]), mV)
    s.wait("tensor", mV, mV.n)
    for mc in range(8):
        s.op("tensor", lambda e, mc=mc: e.matmul(
            pe5[5][:, mc:mc + 1], eb2b[:, 128 * mc:128 * (mc + 1)], ones8e[:],
            start=True, stop=True), mT)
    s.wait("vector", mT, mT.n)
    s.op("vector", lambda e: e.tensor_copy(sb2c[:], pe5[5][:, 0:8]), mV)
    for k4 in range(8):
        s.wait("sync", mV, mV.n)
        s.op("sync", lambda e, k4=k4: e.dma_start(
            out=wl5[:, 0:1024], in_=H["sh1_ag"][128 * k4:128 * (k4 + 1), :]), mLDW, 16)
        s.wait("vector", mLDW, mLDW.n)
        s.op("vector", lambda e, k4=k4: e.tensor_copy(
            sh1b4[:, 1024 * k4:1024 * (k4 + 1)], wl5[:, 0:1024]), mV)
        s.wait("sync", mV, mV.n)
        s.op("sync", lambda e, k4=k4: e.dma_start(
            out=wl5[:, 0:1024], in_=H["sh2_ag"][128 * k4:128 * (k4 + 1), :]), mLDW, 16)
        s.wait("vector", mLDW, mLDW.n)
        s.op("vector", lambda e, k4=k4: e.tensor_copy(
            sh2b4[:, 1024 * k4:1024 * (k4 + 1)], wl5[:, 0:1024]), mV)
        s.wait("sync", mV, mV.n)
        s.op("sync", lambda e, k4=k4: e.dma_start(
            out=wl5[:], in_=H["up_ag"][128 * k4:128 * (k4 + 1), :]), mLDW, 16)
        s.wait("vector", mLDW, mLDW.n)
        s.op("vector", lambda e, k4=k4: e.tensor_copy(
            upb4[:, 2048 * k4:2048 * (k4 + 1)], wl5[:]), mV)
    s.wait("vector", mLDW, mLDW.n)
    s.op("vector", lambda e: e.tensor_add(ubrow[:], ubrow[:], cbrow[:]), mV)
    s.wait("vector", mV, mV.n)
    s.op("vector", lambda e: e.tensor_copy(bsum[:], ubrow[:]), mV)
    s.op("vector", lambda e: e.tensor_scalar(s2c[:], s2c[:], 0.1, None, ALU.mult), mV)
    s.wait("vector", mV, mV.n)
    s.op("vector", lambda e: e.tensor_add(s2c[:], s2c[:], sb2c[:]), mV)
    wcastE = mV.n

    finT = 0
    accV = wcastE
    ghmsT = 0
    s.wait("sync", mCC, 3)
    if "rs_dbg" in H:
        s.wait("gpsimd", mCC, 3)
        s.op("gpsimd", lambda e: e.dma_start(out=H["rs_dbg"], in_=H["rs_out"]), mGD, 16)
    for tb4 in range(4):
        s.wait("sync", mT, ghmsT)
        for k4 in range(8):
            s.op("sync", lambda e, k4=k4, tb4=tb4: e.dma_start(
                out=ghlo[:, 512 * k4:512 * (k4 + 1)],
                in_=H["gh_bnc"][128 * k4:128 * (k4 + 1), 512 * tb4:512 * (tb4 + 1)]), mLD, 16)
        ghloD = mLD.n
        s.wait("tensor", mLD, ghloD)
        s.wait("tensor", mV, accV)
        for oc in range(8):
            bank = pe5[oc % 2]
            s.wait("tensor", mA, mA.n)
            for k4 in range(8):
                s.op("tensor", lambda e, oc=oc, k4=k4, bank=bank: e.matmul(
                    bank[:], sh1b4[:, 1024 * k4 + 128 * oc: 1024 * k4 + 128 * (oc + 1)],
                    ghlo[:, 512 * k4:512 * (k4 + 1)], start=(k4 == 0), stop=(k4 == 7)), mT)
            s.wait("scalar", mT, mT.n)
            s.op("scalar", lambda e, oc=oc, bank=bank: e.activation(
                msb[:, 512 * oc:512 * (oc + 1)], bank[:], AF.Gelu,
                bias=s1c[:, oc:oc + 1]), mA)
        msA = mA.n
        s.wait("tensor", mA, msA)
        for oc in range(8):
            bank = pe5[2 + oc % 2]
            s.wait("tensor", mV, mV.n)
            for k4 in range(8):
                s.op("tensor", lambda e, oc=oc, k4=k4, bank=bank: e.matmul(
                    bank[:], sh2b4[:, 1024 * k4 + 128 * oc: 1024 * k4 + 128 * (oc + 1)],
                    msb[:, 512 * k4:512 * (k4 + 1)], start=(k4 == 0), stop=(k4 == 7)), mT)
            s.wait("vector", mT, mT.n)
            s.op("vector", lambda e, oc=oc, bank=bank: e.tensor_copy(
                psT[:, 512 * oc:512 * (oc + 1)], bank[:]), mV)
        ghmsT = mT.n
        for ts4 in range(4):
            s.wait("sync", mV, mV.n)
            s.op("sync", lambda e, tb4=tb4, ts4=ts4: e.dma_start(
                out=rsl, in_=H["rs_out"][512 * tb4 + 128 * ts4: 512 * tb4 + 128 * (ts4 + 1), :]), mLDR, 16)
            s.wait("vector", mLDR, mLDR.n)
            s.op("vector", lambda e: e.tensor_copy(rsb[:], rsl), mV)
            s.wait("tensor", mV, mV.n)
            for mc in range(8):
                if mc >= 2:
                    s.wait("tensor", mV, mV.n)
                s.op("tensor", lambda e, mc=mc: e.transpose(
                    pt5[mc % 2][:], rsb[:, 128 * mc:128 * (mc + 1)], si5[:]), mT)
                s.wait("vector", mT, mT.n)
                s.op("vector", lambda e, mc=mc, ts4=ts4: e.tensor_copy(
                    rsT[:, 512 * mc + 128 * ts4: 512 * mc + 128 * (ts4 + 1)],
                    pt5[mc % 2][:]), mV)
        s.wait("vector", mV, mV.n)
        for mc in range(8):
            s.op("vector", lambda e, mc=mc: e.scalar_tensor_tensor(
                accT[:, 512 * mc:512 * (mc + 1)], psT[:, 512 * mc:512 * (mc + 1)],
                0.1, rsT[:, 512 * mc:512 * (mc + 1)], ALU.mult, ALU.add), mV)
        s.wait("vector", mV, mV.n)
        for mc in range(8):
            s.op("vector", lambda e, mc=mc: e.tensor_scalar(
                accT[:, 512 * mc:512 * (mc + 1)], accT[:, 512 * mc:512 * (mc + 1)],
                s2c[:, mc:mc + 1], None, ALU.add), mV)
        accV = mV.n
        for ts4 in range(4):
            i4g = 4 * tb4 + ts4
            s.wait("sync", mT, finT)
            s.op("sync", lambda e, i4g=i4g: e.dma_start(
                out=glsl[:].rearrange("p (k t) -> p k t", k=16),
                in_=H["gln_dram"].rearrange("(k p) t -> p k t", p=128)[
                    :, :, 128 * i4g:128 * (i4g + 1)]), mLD, 16)
            s.wait("sync", mV, mV.n)
            s.op("sync", lambda e, i4g=i4g: e.dma_start(
                out=xr4e[:], in_=H["x3_dram"][128 * i4g:128 * (i4g + 1), :]), mLDX, 16)
            glD = mLD.n
            s.wait("tensor", mLD, glD)
            s.wait("tensor", mV, accV)
            for db4 in range(4):
                pp = (tb4 * 4 + ts4) % 2
                s.wait("sync", mT, finT)
                s.op("sync", lambda e, db4=db4, pp=pp: e.dma_start(
                    out=corsl[pp][:].rearrange("p (k d) -> p k d", k=16),
                    in_=H["coreb_dram"].rearrange("(k p) d -> p k d", p=128)[
                        :, :, 512 * db4:512 * (db4 + 1)]), mLDW, 16)
                s.wait("tensor", mLDW, mLDW.n)
                s.op("tensor", lambda e, db4=db4: e.matmul(
                    pe5[db4][:], one5[:], bsum[0:1, 512 * db4:512 * (db4 + 1)],
                    start=True, stop=False), mT)
                for mc in range(8):
                    s.op("tensor", lambda e, mc=mc, ts4=ts4, db4=db4: e.matmul(
                        pe5[db4][:], accT[:, 512 * mc + 128 * ts4: 512 * mc + 128 * (ts4 + 1)],
                        upb4[:, 2048 * mc + 512 * db4: 2048 * mc + 512 * (db4 + 1)],
                        start=False, stop=False), mT)
                for dc in range(16):
                    s.op("tensor", lambda e, dc=dc, db4=db4, pp=pp: e.matmul(
                        pe5[db4][:], glsl[:, 128 * dc:128 * (dc + 1)],
                        corsl[pp][:, 512 * dc:512 * (dc + 1)],
                        start=False, stop=(dc == 15)), mT)
                finT = mT.n
            s.wait("vector", mT, finT)
            for db4 in range(4):
                s.op("vector", lambda e, db4=db4: e.tensor_copy(
                    o5[:, 512 * db4:512 * (db4 + 1)], pe5[db4][:]), mV)
            s.wait("vector", mV, mV.n)
            emit_ln_rows(s, mV, mA, o5[:], l5[:], st5, sq5, ep5[:])
            s.wait("vector", mV, mV.n)
            s.wait("vector", mLDX, mLDX.n)
            s.op("vector", lambda e: e.tensor_add(x5[:], l5[:], xr4e[:]), mV)
            s.wait("vector", mV, mV.n)
            s.wait("gpsimd", mV, mV.n)
            s.wait("gpsimd", mST, mST.n)
            s.op("gpsimd", lambda e, i4g=i4g: e.dma_start(
                out=H["out_rows"][128 * i4g:128 * (i4g + 1), :], in_=x5[:]), mST, 16)
            s.wait("vector", mST, mST.n)
    s.wait("gpsimd", mST, mST.n)
    p.pop_scope()


MOE_HID = [1024, 2048, 3072, 4096, 1024, 2048, 3072, 4096]
MOE_COST = [2 * 1024 * h for h in MOE_HID]


def declare_ph4_drams(nc, weights_external=False, dbg=False):
    H = {}
    dk = "ExternalOutput" if dbg else "Internal"
    H["y2_loc"] = nc.dram_tensor("y2_loc", [TPC, D], BF16,
                                 kind="ExternalInput" if weights_external else "Internal").ap()
    H["x3_dram"] = nc.dram_tensor("x3_dram", [TPC, D], F32, kind=dk).ap()
    H["out_rows"] = nc.dram_tensor("out_rows", [TPC, D], F32, kind="ExternalOutput").ap()
    H["gh_bnc"] = nc.dram_tensor("gh_bnc", [1024, TPC], BF16, kind="Internal").ap()
    H["gh_ag"] = nc.dram_tensor("gh_ag", [8 * 1024, TPC], BF16, kind="Internal",
                                addr_space="Shared").ap()
    H["w9_bnc"] = nc.dram_tensor("w9_bnc", [TPC, 16], F32, kind="Internal").ap()
    H["w9_ag"] = nc.dram_tensor("w9_ag", [8 * TPC, 16], F32, kind="Internal",
                                addr_space="Shared").ap()
    H["rs_in"] = nc.dram_tensor("rs_in", [L, 1024], F32, kind="Internal").ap()
    H["rs_out"] = nc.dram_tensor("rs_out", [TPC, 1024], F32, kind="Internal").ap()
    H["gln_dram"] = nc.dram_tensor("gln_dram", [D, TPC], BF16, kind=dk).ap()
    H["coreb_dram"] = nc.dram_tensor("coreb_dram", [D, D], BF16, kind=dk).ap()
    if dbg:
        H["rs_dbg"] = nc.dram_tensor("rs_dbg", [TPC, 1024], F32, kind="ExternalOutput").ap()
    H["costrow"] = nc.inline_tensor(
        (np.array(MOE_COST, np.float32) * 1e-7)[None, :], "c_cost").ap()
    H["router_w"] = nc.dram_tensor("router_w", [D, 8], F32, kind="ExternalInput").ap()
    H["router_b"] = nc.dram_tensor("router_b", [1, 8], F32, kind="ExternalInput").ap()
    H["down_b"] = nc.dram_tensor("down_b", [1024], F32, kind="ExternalInput").ap()
    H["up_b"] = nc.dram_tensor("up_b", [1, 2048], F32, kind="ExternalInput").ap()
    H["core_b"] = nc.dram_tensor("core_b", [1, 2048], F32, kind="ExternalInput").ap()
    H["sh1_b"] = nc.dram_tensor("sh1_b", [1024], F32, kind="ExternalInput").ap()
    H["sh2_b"] = nc.dram_tensor("sh2_b", [1024], F32, kind="ExternalInput").ap()
    H["ew1_s"] = nc.dram_tensor("ew1_s", [1024, 2560], F32, kind="ExternalInput").ap()
    H["ew2_s"] = nc.dram_tensor("ew2_s", [2560, 1024], F32, kind="ExternalInput").ap()
    H["eb1_s"] = nc.dram_tensor("eb1_s", [2560], F32, kind="ExternalInput").ap()
    H["sel8_s"] = nc.dram_tensor("sel8_s", [8, 2560], F32, kind="ExternalInput").ap()
    H["selT_s"] = nc.dram_tensor("selT_s", [2560, 8], F32, kind="ExternalInput").ap()
    H["eb2_stack"] = nc.dram_tensor("eb2_stack", [8, 1024], F32, kind="ExternalInput").ap()
    if weights_external:
        for nm, shp in (("wo_ag", [D, D]), ("core_ag", [D, D]), ("down_ag", [D, 1024]),
                        ("up_ag", [1024, D]), ("sh1_ag", [1024, 1024]), ("sh2_ag", [1024, 1024])):
            H[nm] = nc.dram_tensor(nm, shp, F32, kind="ExternalInput").ap()
        H["x2_dram"] = nc.dram_tensor("x2_in", [TPC, D], F32, kind="ExternalInput").ap()
    return H


def slice_ph4_inputs(inp, c):
    ew1 = np.concatenate([inp[f"L1_ffn_e{e}_w1"] for e in range(8)], 1)
    eb1 = np.concatenate([inp[f"L1_ffn_e{e}_b1"] for e in range(8)], 0)
    ew2 = np.concatenate([inp[f"L1_ffn_e{e}_w2"] for e in range(8)], 0)
    eid = np.concatenate([np.full(MOE_HID[e], e) for e in range(8)])
    sl = slice(2560 * c, 2560 * (c + 1))
    sel8 = (eid[None, sl] == np.arange(8)[:, None]).astype(np.float32)
    return {
        "ew1_s": np.ascontiguousarray(ew1[:, sl]).astype(np.float32),
        "eb1_s": np.ascontiguousarray(eb1[sl]).astype(np.float32),
        "ew2_s": np.ascontiguousarray(ew2[sl, :]).astype(np.float32),
        "sel8_s": np.ascontiguousarray(sel8),
        "selT_s": np.ascontiguousarray(sel8.T),
        "router_w": inp["L1_ffn_router_w"].astype(np.float32),
        "router_b": inp["L1_ffn_router_b"][None, :].astype(np.float32),
        "down_b": inp["L1_ffn_down_b"].astype(np.float32),
        "up_b": inp["L1_ffn_up_b"][None, :].astype(np.float32),
        "core_b": inp["L1_ffn_core_b"][None, :].astype(np.float32),
        "sh1_b": inp["L1_ffn_shared_b1"].astype(np.float32),
        "sh2_b": inp["L1_ffn_shared_b2"].astype(np.float32),
        "eb2_stack": np.stack([inp[f"L1_ffn_e{e}_b2"] for e in range(8)], 0).astype(np.float32),
    }


def build_ph4_test():
    nc = bass.Bass("TRN2", target_bir_lowering=False)
    es = ExitStack()
    p = Prog(nc, es)
    H = make_consts(nc)
    H.update(declare_ph4_drams(nc, weights_external=True,
                               dbg=_os.environ.get("PH4_DBG", "0") == "1"))
    with es:
        sched = Sched(p, "p4")
        block = es.enter_context(nc.Block())
        emit_ph4(p, sched, H)
        sched.emit(block)
    return nc



def slice_ph1_inputs(inp, c):
    Wqkv = inp["L0_attn_Wqkv"]
    Wa, Wb = inp["L0_attn_Wa"], inp["L0_attn_Wb"]
    qcols = list(range(128 * 2 * c, 128 * (2 * c + 2)))
    kcols = [2048 + i for i in qcols]
    vcols = list(range(4096 + 512 * c, 4096 + 512 * (c + 1)))
    cols = qcols + kcols + vcols
    return {
        "wqkv_s": np.ascontiguousarray(Wqkv[:, cols]),
        "wz_s": np.ascontiguousarray(inp["L0_attn_Wz"][:, 512 * c:512 * (c + 1)]),
        "wab_s": np.ascontiguousarray(np.concatenate(
            [Wa[:, 4 * c:4 * (c + 1)], Wb[:, 4 * c:4 * (c + 1)]], 1)),
        "conv_s": np.ascontiguousarray(inp["L0_attn_conv_w"][cols, :]),
        "alog_s": np.ascontiguousarray(inp["L0_attn_A_log"][4 * c:4 * (c + 1)][None, :]),
        "dtb_s": np.ascontiguousarray(inp["L0_attn_dt_bias"][4 * c:4 * (c + 1)][None, :]),
        "normw_s": np.ascontiguousarray(inp["L0_attn_norm_w"][:, None]),
    }


def slice_ph3_inputs(inp, c):
    return {
        "wq_s": np.ascontiguousarray(inp["L1_attn_Wq"][:, 256 * c:256 * (c + 1)]),
        "wk_s": np.ascontiguousarray(inp["L1_attn_Wk"][:, 256 * c:256 * (c + 1)]),
        "wv_s": np.ascontiguousarray(inp["L1_attn_Wv"][:, 256 * c:256 * (c + 1)]),
        "betas_s": np.ascontiguousarray(inp["L1_attn_betas"][0, 2 * c:2 * (c + 1), 0, :]),
    }


def kernel(**inputs):
    cores = list(range(NCORE))
    in_maps = [slice_fused_inputs(inputs, c) for c in cores]
    r = run_bass_kernel_spmd(build_fused(), in_maps, core_ids=cores).results
    out = np.concatenate([np.asarray(r[c]["out_rows"], np.float32) for c in cores], 0)
    return out[None]


def kernel_multilaunch(**inputs):
    inp = inputs
    x = np.asarray(inp["L0_attn_x"])[0]
    cores = list(range(NCORE))
    rows = lambda a, c: np.ascontiguousarray(a[TPC * c:TPC * (c + 1)])

    in1 = [{"x_rows": rows(x, c)} for c in cores]
    r1 = run_bass_kernel_spmd(build_ph0_test(), in1, core_ids=cores).results
    ln0T_ag = np.concatenate([np.asarray(r1[c]["ln0T_sh"]) for c in cores], 0)

    in2 = [dict(slice_ph1_inputs(inp, c), ln0T_ag=ln0T_ag) for c in cores]
    r2 = run_bass_kernel_spmd(build_ph1_test(dbg=False), in2, core_ids=cores).results
    yT = [np.asarray(r2[c]["yT_sh"]) for c in cores]

    in3 = []
    for c in cores:
        m3 = slice_prologue_inputs(inp, c)
        m3["yT_loc"] = np.concatenate([yT[r][512 * c:512 * (c + 1)] for r in cores], 0)
        m3["x_rows"] = rows(x, c)
        in3.append(m3)
    r3 = run_bass_kernel_spmd(build_ph2_test(lean=True), in3, core_ids=cores).results
    ln2T_ag = np.concatenate([np.asarray(r3[c]["ln2T_sh"]) for c in cores], 0)

    in4 = [dict(slice_ph3_inputs(inp, c), ln2T_ag=ln2T_ag) for c in cores]
    r4 = run_bass_kernel_spmd(build_ph3_test(), in4, core_ids=cores).results
    y2 = [np.asarray(r4[c]["y2_sh"]) for c in cores]

    in5 = []
    for c in cores:
        m5 = slice_ph4_inputs(inp, c)
        m5["y2_loc"] = np.concatenate([y2[r][256 * c:256 * (c + 1)] for r in cores], 0)
        m5["x2_in"] = np.asarray(r3[c]["x2_dram"], np.float32)
        m5["wo_ag"] = np.asarray(inp["L1_attn_Wo"], np.float32)
        m5["core_ag"] = np.asarray(inp["L1_ffn_core_w"], np.float32)
        m5["down_ag"] = np.asarray(inp["L1_ffn_down_w"], np.float32)
        m5["up_ag"] = np.asarray(inp["L1_ffn_up_w"], np.float32)
        m5["sh1_ag"] = np.asarray(inp["L1_ffn_shared_w1"], np.float32)
        m5["sh2_ag"] = np.asarray(inp["L1_ffn_shared_w2"], np.float32)
        in5.append(m5)
    r5 = run_bass_kernel_spmd(build_ph4_test(), in5, core_ids=cores).results

    out = np.concatenate([np.asarray(r5[c]["out_rows"], np.float32) for c in cores], 0)
    return out[None]



def build_fused():
    nc = bass.Bass("TRN2", target_bir_lowering=False)
    es = ExitStack()
    p = Prog(nc, es)
    H = make_consts(nc)
    x_rows = nc.dram_tensor("x_rows", [TPC, D], F32, kind="ExternalInput").ap()
    H["x_rows"] = x_rows
    for nm, shp in (("wqkv_s", [D, 1024]), ("wz_s", [D, 512]), ("wab_s", [D, 8]),
                    ("conv_s", [1024, 4]), ("alog_s", [1, 4]), ("dtb_s", [1, 4]),
                    ("normw_s", [128, 1]), ("wout_s", [512, 2048]),
                    ("wq_s", [D, 256]), ("wk_s", [D, 256]), ("wv_s", [D, 256]),
                    ("betas_s", [2, 128])):
        H[nm] = nc.dram_tensor(nm, shp, F32, kind="ExternalInput").ap()
    H["ysel_s"] = nc.dram_tensor("ysel_s", [TPC, 256], F32, kind="ExternalInput").ap()
    H.update(declare_prologue_drams(nc, need=("wo", "core", "down", "up", "sh1", "sh2",
                                              "w1", "w3", "w2")))
    H.update(declare_ph4_drams(nc, weights_external=False))
    ln0T_sh = nc.dram_tensor("ln0T_sh", [D, TPC], BF16, kind="Internal").ap()
    H["ln0T_ag"] = nc.dram_tensor("ln0T_ag", [NCORE * D, TPC], BF16, kind="Internal",
                                  addr_space="Shared").ap()
    H["yT_sh"] = nc.dram_tensor("yT_sh", [4096, TPC], BF16, kind="Internal").ap()
    H["rs_y"] = nc.dram_tensor("rs_y", [L, D], F32, kind="Internal").ap()
    H["x1_dram"] = nc.dram_tensor("x1_dram", [TPC, D], F32, kind="Internal").ap()
    H["att_dram"] = nc.dram_tensor("att_dram", [TPC, D], F32, kind="Internal").ap()
    H["ln1T_dram"] = nc.dram_tensor("ln1T_dram", [D, TPC], BF16, kind="Internal").ap()
    H["x2_dram"] = nc.dram_tensor("x2_dram", [TPC, D], F32, kind="Internal").ap()
    H["ln2T_sh"] = nc.dram_tensor("ln2T_sh", [D, TPC], BF16, kind="Internal").ap()
    H["ln2T_ag"] = nc.dram_tensor("ln2T_ag", [NCORE * D, TPC], BF16, kind="Internal",
                                  addr_space="Shared").ap()
    H["y2_sh"] = nc.dram_tensor("y2_sh", [TPC, D], BF16, kind="Internal").ap()
    H["y2_ag"] = nc.dram_tensor("y2_ag", [NCORE * TPC, D], BF16, kind="Internal",
                                addr_space="Shared").ap()
    H["dbg"] = None
    H["skip_pass3"] = False
    with es:
        s = Sched(p, "f")
        block = es.enter_context(nc.Block())
        DONE0 = s.ch("D0")
        CC0 = s.ch("CC0")
        CCL2 = s.ch("CCL2")
        CCY2 = s.ch("CCY2")
        p.push_scope()
        emit_ph0(p, block, x_rows, ln0T_sh, H["identb"],
                 ag=(H["ln0T_ag"], CC0.s), done=DONE0.s)
        p.pop_scope()
        CC0.n = 1
        DONE0.n = 5
        PCC = emit_prologue(p, s, H, f32_ags=("wo", "core", "down", "up", "sh1", "sh2"))
        s.barrier("Bpro")
        for eng in ENGS:
            s.wait(eng, DONE0, 5)
        H["pre_waits1"] = [(CC0, 1)]
        H["y_a2a"] = None
        p.push_scope()
        emit_ph1(p, s, H)
        p.pop_scope()
        s.barrier("Bp1")
        CCR1 = emit_ph1b(p, s, H)
        s.barrier("Bp1b")
        H["w2f_view"] = H["w2f_ag"].rearrange("(r h) d -> h r d", r=8)
        H["ln2_ag"] = (H["ln2T_ag"], CCL2)
        H["skip_pass1"] = True
        H["pre_waits"] = [(PCC, PCC.n), (CCR1, 1)]
        emit_ph2(p, s, H)
        s.barrier("Bp2")
        H["pre_waits3"] = [(CCL2, 1)]
        H["y2_a2a"] = None
        emit_ph3(p, s, H)
        s.op("gpsimd", lambda e: e.collective_compute(
            "AllGather", ALU.bypass, replica_groups=RG8,
            ins=[H["y2_sh"][:, :].opt()], outs=[H["y2_ag"][:, :].opt()]), CCY2, 1)
        s.barrier("Bp3")
        emit_y2sel(p, s, H, pre=[(CCY2, 1)])
        s.barrier("Bsel")
        H["pre_waits4"] = [(PCC, PCC.n)]
        emit_ph4(p, s, H)
        s.emit(block)
    return nc


def slice_fused_inputs(inp, c):
    x = np.asarray(inp["L0_attn_x"])[0]
    m = {"x_rows": np.ascontiguousarray(x[TPC * c:TPC * (c + 1)])}
    m.update(slice_ph1_inputs(inp, c))
    m.update(slice_prologue_inputs(inp, c))
    m.update(slice_ph3_inputs(inp, c))
    m.update(slice_ph4_inputs(inp, c))
    ysel = np.zeros((TPC, 256), np.float32)
    ysel[np.arange(256) + 256 * c, np.arange(256)] = 1.0
    m["ysel_s"] = ysel
    return m


def kernel_fused(**inputs):
    cores = list(range(NCORE))
    in_maps = [slice_fused_inputs(inputs, c) for c in cores]
    r = run_bass_kernel_spmd(build_fused(), in_maps, core_ids=cores).results
    out = np.concatenate([np.asarray(r[c]["out_rows"], np.float32) for c in cores], 0)
    return out[None]
